# version 28
# baseline (speedup 1.0000x reference)
import numpy as np
import ml_dtypes
import concourse.bass as bass
import concourse.mybir as mybir
from concourse.bass_utils import run_bass_kernel_spmd

F32 = mybir.dt.float32
BF16 = mybir.dt.bfloat16
AF = mybir.ActivationFunctionType
ALU = mybir.AluOpType
AX = mybir.AxisListType

NCORE = 8
NT = 2048
D = 1024
DI = 2816
NIT = DI // 128
NDG = D // 128
NTT = NT // 128
NQ = 4
QT = NT // NQ
QTT = QT // 128
LOG8 = float(np.log(8.0))
EPS = 1e-5

POOL_N = 30720
P_PW = [0, 4096, 8192, 12288]
P_PQ = [16384, 20480]
P_PS = [24576, 26624]
P_XQT = 0
P_WRG = [8192, 8704]
P_WRU = [9216, 9728]
P_SIL = 10240
P_HT = 12288
P_UT = [14336, 15360]
P_QTS = [16384, 17408]
P_QCS = [18432, 19456]
P_HRD = 0
P_HQ = 11264
P_WDT = 16896
P_DQ = [28160, 28672]
P_TD = 29184
P_UHI = [0, 20480]
P_ULO = [2048, 22528]
P_AWR = [4096, 6144]
P_AT = 8192
P_BT = 12288
P_Y = 16384
P_UT2 = [24576, 25600]
P_SC = [26624 + 512 * k for k in range(6)]
P_SCP = [[26624, 27136, 27648], [29184, 29696, 30208]]

C_TMP = 0
C_FLAG = 1
C_A = 2
C_EPS = 3
C_ONE = 4
C_HALF = 5
C_ZERO = 6
C_MLOG8H = 7
C_SIGXO = 8
C_SIGDO = 9
C_INVS = 10
C_SX = 12
C_HALFSH = 14
C_GMAX = 15
C_B2A = 16
C_B2X = 24
C_LNA = 32
C_LAM = 40
C_SG = 48
C_SU = 70
C_SGX = 92
C_SUX = 114
C_YL = 136
C_AL = 144
C_CC = 152
C_G8 = 160
C_SEL = 224
C_CLIPX = 232
C_CLIPD = 233
C_MONE = 234
C_TMP2 = 235
C_TMP3 = 236
C_MHALF = 237
C_NEGA = 238
CSCL = 184
CGM8 = 192
C_AM128 = 240
C_AP127 = 241
CONSTS_N = 244

S_MV = 0
S_RSTD = 32
S_RSTDI = 48
S_GM = 64
S_HM = 80
S_T1 = 81
S_BN = 82
STATS_N = 96


class Plan:
    ENGS = ("sync", "vector", "scalar", "tensor", "gpsimd")

    def __init__(self, nc):
        self.nc = nc
        self.ops = {e: [] for e in self.ENGS}
        self.sems = {}

    def sem(self, name):
        self.sems[name] = [self.nc.alloc_semaphore(name=name), 0]

    def count(self, name):
        return self.sems[name][1]

    def wait(self, eng, name, value=None):
        if value is None:
            value = self.sems[name][1]
        if value is None or value <= 0:
            return
        h = self.sems[name][0]
        self.ops[eng].append(lambda e, h=h, v=value: e.wait_ge(h, v))

    def op(self, eng, fn, inc=None, drain=False):
        if drain:
            self.ops[eng].append(lambda e: e.drain())
        if inc is None:
            self.ops[eng].append(lambda e, f=fn: f(e))
            return None
        self.sems[inc][1] += 1
        v = self.sems[inc][1]
        hh = self.sems[inc][0]

        def g(e, f=fn, hh=hh):
            f(e).then_inc(hh, 1)
        self.ops[eng].append(g)
        return v

    def drain_inc(self, eng, inc):
        self.sems[inc][1] += 1
        v = self.sems[inc][1]
        hh = self.sems[inc][0]
        self.ops[eng].append(lambda e, hh=hh: e.drain().then_inc(hh, 1))
        return v

    def dma(self, eng, out_ap, in_ap, inc):
        self.sems[inc][1] += 16
        v = self.sems[inc][1]
        hh = self.sems[inc][0]

        def g(e, o=out_ap, i=in_ap, hh=hh):
            e.dma_start(out=o, in_=i).then_inc(hh, 16)
        self.ops[eng].append(g)
        return v

    def barrier(self):
        self.drain_inc("vector", "VEC")
        self.drain_inc("scalar", "SCA")
        self.drain_inc("gpsimd", "GPS")
        snap = {n: s[1] for n, s in self.sems.items()}
        for e in self.ENGS:
            for n, v in snap.items():
                if v > 0:
                    self.wait(e, n, v)


def build_program():
    nc = bass.Bass("TRN2", target_bir_lowering=False, debug=False)

    x_h = nc.dram_tensor("x", [NT, D], F32, kind="ExternalInput")
    gus_h = nc.dram_tensor("gus", [6, 128, D], F32, kind="ExternalInput")
    wds_h = nc.dram_tensor("wds", [128, DI], F32, kind="ExternalInput")
    wa_h = nc.dram_tensor("wa", [D, D], F32, kind="ExternalInput")
    wx_h = nc.dram_tensor("wx", [D, D], F32, kind="ExternalInput")
    ba_h = nc.dram_tensor("ba", [D], F32, kind="ExternalInput")
    bx_h = nc.dram_tensor("bx", [D], F32, kind="ExternalInput")
    lam_h = nc.dram_tensor("lam", [D], F32, kind="ExternalInput")
    clipx_h = nc.dram_tensor("clipx", [1], F32, kind="ExternalInput")
    clipd_h = nc.dram_tensor("clipd", [1], F32, kind="ExternalInput")
    sel_h = nc.dram_tensor("sel8", [8], F32, kind="ExternalInput")
    y_h = nc.dram_tensor("y", [NT, D], F32, kind="ExternalOutput")

    ag1_in = nc.dram_tensor("ag1_in", [1, 6 * 131072], BF16, kind="Internal")
    ag1_out = nc.dram_tensor("ag1_out", [8, 6 * 131072], BF16, kind="Internal")
    ag2_in = nc.dram_tensor("ag2_in", [1, 3 * 131072], BF16, kind="Internal")
    ag2_out = nc.dram_tensor("ag2_out", [8, 3 * 131072], BF16, kind="Internal")
    wahl_img = nc.dram_tensor("wahl_img", [NDG, 128, 2048], BF16, kind="Internal")
    wxhl_img = nc.dram_tensor("wxhl_img", [NDG, 128, 2048], BF16, kind="Internal")
    scl_in = nc.dram_tensor("scl_in", [1, 1024], F32, kind="Internal")
    scl_out = nc.dram_tensor("scl_out", [8, 1024], F32, kind="Internal")
    h_spill = nc.dram_tensor("h_spill", [4, NIT, 128, 512], F32, kind="Internal")
    ac_spill = nc.dram_tensor("ac_spill", [NQ, 128, 4096], F32, kind="Internal")
    cc_in = nc.dram_tensor("cc_in", [1, 1], F32, kind="Internal")
    cc_out = nc.dram_tensor("cc_out", [1, 1], F32, kind="Internal")
    ag_in = nc.dram_tensor("ag_in", [1, D], F32, kind="Internal")
    ag_out = nc.dram_tensor("ag_out", [8, D], F32, kind="Internal")
    ident_h = nc.inline_tensor(np.eye(128, dtype=np.float32), name="ident_dat")
    RG = [[0, 1, 2, 3, 4, 5, 6, 7]]

    XR = nc.alloc_sbuf_tensor("XR", [128, NTT * 1024], F32)
    POOL = nc.alloc_sbuf_tensor("POOL", [128, POOL_N], F32)
    CO = nc.alloc_sbuf_tensor("CO", [128, CONSTS_N], F32)
    STT = nc.alloc_sbuf_tensor("STATS", [128, STATS_N], F32)
    SDB = nc.alloc_sbuf_tensor("SDB", [128, 1024], F32)
    SDBF = nc.alloc_sbuf_tensor("SDBF", [128, 1024], F32)
    IDT = nc.alloc_sbuf_tensor("IDT", [128, 128], F32)

    pAB = [nc.alloc_psum_tensor("p%d" % k, [128, 512], F32) for k in range(4)]
    pT = nc.alloc_psum_tensor("pT", [128, 1024], F32)
    pT2 = nc.alloc_psum_tensor("pT2", [128, 1024], F32)

    P = Plan(nc)
    for s in ("LD", "SCL", "XA0", "XA1", "XA2", "XA3",
              "GW0", "GW1", "GW2",
              "WS0", "WS1", "WS2", "WS3", "WDS",
              "HS0", "HS1", "HS2", "HS3", "ACS", "ACS2",
              "HR0", "HR1", "HR2", "HR3", "PS0", "PS1", "CCS", "STO",
              "TEN", "VEC", "SCA", "GPS", "CCD"):
        P.sem(s)

    def co(c):
        return CO[:, c:c + 1]

    def cob(c, n):
        return bass.AP(CO, c, [[CONSTS_N, 128], [0, n]])

    def stc(c):
        return STT[:, c:c + 1]

    def pool(a, n):
        return POOL[:, a:a + n]

    def poolb(off_f32, dims):
        return bass.AP(POOL, off_f32, [[POOL_N, 128]] + dims).bitcast(BF16)

    def pool3(off, dims):
        return bass.AP(POOL, off, [[POOL_N, 128]] + dims)

    def vop(fn, inc=None, drain=True):
        return P.op("vector", fn, inc=inc, drain=drain)

    def sop(fn, inc=None, drain=True):
        return P.op("scalar", fn, inc=inc, drain=drain)

    def gop(fn, inc=None, drain=True):
        return P.op("gpsimd", fn, inc=inc, drain=drain)

    def top(fn, inc=None):
        return P.op("tensor", fn, inc=inc)

    for src, c0 in ((ba_h, C_B2A), (bx_h, C_B2X), (lam_h, C_LAM)):
        P.dma("sync", CO[:, c0:c0 + 8], bass.AP(src, 0, [[1, 128], [128, 8]]),
              "LD")
    P.dma("sync", co(C_CLIPX), bass.AP(clipx_h, 0, [[0, 128], [1, 1]]), "LD")
    P.dma("sync", co(C_CLIPD), bass.AP(clipd_h, 0, [[0, 128], [1, 1]]), "LD")
    for r in range(8):
        P.dma("sync", co(C_SEL + r), bass.AP(sel_h, r, [[0, 128], [1, 1]]), "LD")
    P.dma("sync", IDT[:], ident_h.ap(), "LD")
    ld_in = P.count("LD")

    for c, val in ((C_EPS, EPS), (C_ONE, 1.0), (C_HALF, 0.5), (C_ZERO, 0.0),
                   (C_MONE, -1.0), (C_MHALF, -0.5), (C_MLOG8H, -LOG8 / 2),
                   (C_TMP, 1.0)):
        P.op("vector", lambda e, c=c, v=val: e.memset(co(c), v))
    vop(lambda e: e.tensor_scalar(co(C_TMP2), co(C_TMP), 2.0**-30, None, ALU.add))
    vop(lambda e: e.tensor_scalar(co(C_FLAG), co(C_TMP2), -1.0, 2.0**30,
                                  ALU.add, ALU.mult))
    vop(lambda e: e.tensor_scalar(co(C_A), co(C_FLAG), 1.5 * 2.0**52,
                                  1.5 * 2.0**23, ALU.mult, ALU.add))
    vop(lambda e: e.tensor_scalar(co(C_NEGA), co(C_A), -1.0, None, ALU.mult))
    vop(lambda e: e.memset(co(CSCL + 7), 0.0))
    vop(lambda e: e.tensor_scalar(co(C_AM128), co(C_A), -128.0, None,
                                  ALU.add))
    vop(lambda e: e.tensor_scalar(co(C_AP127), co(C_A), 127.0, None,
                                  ALU.add))
    P.wait("vector", "LD", ld_in)
    vop(lambda e: e.tensor_scalar(CO[:, C_B2A:C_B2A + 8], CO[:, C_B2A:C_B2A + 8],
                                  0.5, None, ALU.mult))
    vop(lambda e: e.tensor_scalar(CO[:, C_B2X:C_B2X + 8], CO[:, C_B2X:C_B2X + 8],
                                  0.5, None, ALU.mult))
    v_c0 = P.drain_inc("vector", "VEC")

    P.wait("scalar", "LD", ld_in)
    P.wait("scalar", "VEC", v_c0)
    sop(lambda e: e.activation(co(C_TMP), co(C_CLIPX), AF.Tanh,
                               bias=0.0, scale=co(C_HALF)))
    sop(lambda e: e.activation(co(C_TMP2), co(C_CLIPD), AF.Tanh,
                               bias=0.0, scale=co(C_HALF)))
    sop(lambda e: e.activation(CO[:, C_G8:C_G8 + 8], CO[:, C_LAM:C_LAM + 8],
                               AF.Tanh, bias=0.0, scale=co(C_HALF)))
    s_sig = P.drain_inc("scalar", "SCA")
    P.wait("vector", "SCA", s_sig)
    vop(lambda e: e.tensor_scalar(co(C_SIGXO), co(C_TMP), 0.5 / 127.0,
                                  0.5 / 127.0, ALU.mult, ALU.add))
    vop(lambda e: e.tensor_scalar(co(C_SIGDO), co(C_TMP2), 0.5 / 127.0,
                                  0.5 / 127.0, ALU.mult, ALU.add))
    vop(lambda e: e.tensor_scalar(CO[:, C_G8:C_G8 + 8], CO[:, C_G8:C_G8 + 8],
                                  0.5, 0.5, ALU.mult, ALU.add))
    v_sig = P.drain_inc("vector", "VEC")
    P.wait("scalar", "VEC", v_sig)
    sop(lambda e: e.activation(CO[:, C_LNA:C_LNA + 8], CO[:, C_G8:C_G8 + 8],
                               AF.Ln, bias=0.0, scale=1.0))
    s_lna = P.drain_inc("scalar", "SCA")
    P.wait("vector", "SCA", s_lna)
    vop(lambda e: e.tensor_scalar(CO[:, C_LNA:C_LNA + 8], CO[:, C_LNA:C_LNA + 8],
                                  -LOG8 / 2, None, ALU.add))
    P.drain_inc("vector", "VEC")

    state = {"tv": [0, 0], "sv_s": [0, 0], "sv_v": [0, 0], "ps": [0, 0],
             "sw": [0, 0, 0, 0], "first_load_gate": [None], "gbi": 0}
    PRM, PRR, PRI = 168, 172, 176

    def prep_quant(w_hh, img, c_scale, ntiles, width, kind, batch):
        nbat = (ntiles + batch - 1) // batch
        for bi in range(nbat):
            it0 = bi * batch
            nb = min(batch, ntiles - it0)
            w = nb * width
            wslot = bi % 4
            slot = bi % 2
            gw = "GW%d" % state["gbi"]
            state["gbi"] += 1
            P.wait("gpsimd", "SCA", state["sw"][wslot])
            if state["first_load_gate"][0] is not None:
                P.wait("gpsimd", "VEC", state["first_load_gate"][0])
                state["first_load_gate"][0] = None
            P.dma("gpsimd", pool3(P_PW[wslot], [[width, nb], [1, width]]),
                  bass.AP(w_hh, it0 * 128 * width,
                          [[width, 128], [128 * width, nb], [1, width]]),
                  gw)
            P.wait("vector", gw, 16)
            win3 = pool3(P_PW[wslot], [[width, nb], [1, width]])
            vop(lambda e, a=win3, nb=nb: e.tensor_reduce(
                CO[:, PRM:PRM + nb], a, AX.X, ALU.max,
                apply_absolute_value=True))
            vop(lambda e, nb=nb: e.reciprocal(CO[:, PRR:PRR + nb],
                                              CO[:, PRM:PRM + nb]))
            vop(lambda e, nb=nb: e.tensor_scalar(
                CO[:, PRI:PRI + nb], CO[:, PRR:PRR + nb], 127.0, None,
                ALU.mult))
            vop(lambda e, c=c_scale + it0, nb=nb: e.tensor_scalar(
                CO[:, c:c + nb], CO[:, PRM:PRM + nb], 1.0 / 127.0, None,
                ALU.mult))
            v_inv = P.drain_inc("vector", "VEC")
            P.wait("scalar", "VEC", v_inv)
            P.wait("scalar", "TEN", state["tv"][slot])
            for t in range(nb):
                state["sw"][wslot] = sop(
                    lambda e, t=t, ws=wslot, s=slot, width=width: e.activation(
                        pool(P_PQ[s] + t * width, width),
                        pool(P_PW[ws] + t * width, width),
                        AF.Identity, bias=co(C_A), scale=co(PRI + t)),
                    inc="SCA")
            P.wait("tensor", "SCA", state["sw"][wslot])
            nblk = w // 128
            nhalf = (nblk + 7) // 8
            tv = 0
            sv_s = sv_v = 0
            sv_prev = {}
            for half in range(nhalf):
                par2 = half % 2
                pp = pT if par2 == 0 else pT2
                nbk = min(8, nblk - half * 8)
                if half >= 2:
                    P.wait("tensor", sv_prev[half - 2][0], sv_prev[half - 2][1])
                elif half == 0:
                    P.wait("tensor", "SCA", state["sv_s"][1 - slot])
                elif state["sv_v"][1 - slot]:
                    P.wait("tensor", "VEC", state["sv_v"][1 - slot])
                for j in range(nbk):
                    b = half * 8 + j
                    tv = top(lambda e, pp=pp, j=j, b=b, s=slot: e.transpose(
                        pp[:, j * 128:(j + 1) * 128],
                        pool(P_PQ[s] + b * 128, 128), IDT[:]), inc="TEN")
                if par2 == 0:
                    P.wait("scalar", "TEN", tv)
                    P.wait("scalar", "PS%d" % slot, state["ps"][slot])
                    sv_s = sop(lambda e, pp=pp, half=half, nbk=nbk, s=slot:
                               e.activation(
                                   poolb(P_PS[s] + half * 512, [[1, nbk * 64]]),
                                   pp[:, 0:nbk * 128], AF.Identity,
                                   bias=co(C_NEGA), scale=1.0), inc="SCA")
                    sv_prev[half] = ("SCA", sv_s)
                else:
                    P.wait("vector", "TEN", tv)
                    P.wait("vector", "PS%d" % slot, state["ps"][slot])
                    sv_v = vop(lambda e, pp=pp, half=half, nbk=nbk, s=slot:
                               e.tensor_scalar(
                                   poolb(P_PS[s] + half * 512, [[1, nbk * 64]]),
                                   pp[:, 0:nbk * 128], co(C_NEGA), None,
                                   ALU.add), inc="VEC")
                    sv_prev[half] = ("VEC", sv_v)
            state["tv"][slot] = tv
            state["sv_s"][slot] = sv_s
            state["sv_v"][slot] = sv_v
            P.wait("scalar", "SCA", sv_s)
            if sv_v:
                P.wait("scalar", "VEC", sv_v)
            if kind == "gu":
                P.dma("scalar",
                      bass.AP(img, it0 * 131072,
                              [[1024, 128], [131072, nb], [1, 1024]]),
                      poolb(P_PS[slot], [[512, nb], [1, 512]]), "PS%d" % slot)
            else:
                P.wait("vector", "PS%d" % slot, state["ps"][slot])
                mz = vop(lambda e, s=slot: e.memset(
                    pool(P_PS[s] + DI // 2, 1536 - DI // 2), 0.0), inc="VEC")
                P.wait("scalar", "VEC", mz)
                P.dma("scalar",
                      bass.AP(img, 0, [[3072, 128], [1, 3072]]),
                      poolb(P_PS[slot], [[1, 1536]]), "PS%d" % slot)
            state["ps"][slot] = P.count("PS%d" % slot)

    V_SDB = [0]
    PREP_END = {"VEC": 0, "SCA": 0}
    CCD_AG1 = [0]
    CCD_AG2 = [0]

    PREP_N = {}

    XV = {}

    def prep_compute():
        for g in range(8):
            XV[g] = P.dma(
                "sync" if g < 4 else "scalar",
                bass.AP(XR, g * 2048,
                        [[NTT * 1024, 128], [1024, 2], [1, 1024]]),
                bass.AP(x_h, g * 262144,
                        [[1024, 128], [131072, 2], [1, 1024]]),
                "XA%d" % (g // 2))
        state["first_load_gate"][0] = P.count("VEC")
        prep_quant(gus_h, ag1_in, CSCL, 6, 1024, "gu", 4)
        PREP_N["ps0_gu"] = P.count("PS0")
        PREP_N["ps1_gu"] = P.count("PS1")
        prep_quant(wds_h, ag2_in, CSCL + 6, 1, DI, "wd1", 1)
        PREP_N["ps0_wd"] = P.count("PS0")
        PREP_N["ps1_wd"] = P.count("PS1")
        PREP_N["vec"] = P.count("VEC")
        PREP_N["sca"] = P.count("SCA")
        PREP_END["VEC"] = P.count("VEC")
        PREP_END["SCA"] = P.count("SCA")

    def prep_triggers():
        P.wait("sync", "VEC", P.count("VEC"))
        P.wait("sync", "SCA", PREP_N["sca"])
        sv = P.dma("sync", bass.AP(scl_in, 0, [[1, 128], [128, 8]]),
                   CO[:, CSCL:CSCL + 8], "CCS")
        P.wait("gpsimd", "CCS", sv)
        ccd_scl = P.op("gpsimd", lambda e: e.collective_compute(
            "AllGather", ALU.bypass, ins=[scl_in.ap()], outs=[scl_out.ap()],
            replica_groups=RG), inc="CCD")
        P.wait("gpsimd", "PS0", PREP_N["ps0_gu"])
        P.wait("gpsimd", "PS1", PREP_N["ps1_gu"])
        CCD_AG1[0] = P.op("gpsimd", lambda e: e.collective_compute(
            "AllGather", ALU.bypass, ins=[ag1_in.ap()], outs=[ag1_out.ap()],
            replica_groups=RG), inc="CCD")
        P.wait("gpsimd", "PS0", PREP_N["ps0_wd"])
        P.wait("gpsimd", "PS1", PREP_N["ps1_wd"])
        CCD_AG2[0] = P.op("gpsimd", lambda e: e.collective_compute(
            "AllGather", ALU.bypass, ins=[ag2_in.ap()], outs=[ag2_out.ap()],
            replica_groups=RG), inc="CCD")
        P.wait("sync", "CCD", ccd_scl)
        for dst, off, n in ((C_SG + 0, 0, 6), (C_SG + 6, 1024, 6),
                            (C_SG + 12, 2048, 6), (C_SG + 18, 3072, 4),
                            (C_SU + 0, 3584, 2), (C_SU + 2, 4096, 6),
                            (C_SU + 8, 5120, 6), (C_SU + 14, 6144, 6),
                            (C_SU + 20, 7168, 2)):
            P.dma("sync", CO[:, dst:dst + n],
                  bass.AP(scl_out, off, [[1, 128], [128, n]]), "SCL")
        V_SDB[0] = P.dma("sync", SDB[:],
                         bass.AP(scl_out, 768,
                                 [[0, 128], [1024, 8], [1, 128]]), "SCL")
        P.dma("sync", CO[:, CGM8:CGM8 + 8],
              bass.AP(scl_out, 896, [[0, 128], [1024, 8]]), "SCL")
        V_SDB[0] = P.count("SCL")

    PREP_AX_END = {"PS0": 0, "PS1": 0}
    AXW = [20480, 21504]

    AXS = {"vv": [0, 0], "ps": [0, 0]}

    def prep_ax_unit(u):
        w_hh, img = (wa_h, wahl_img) if u < NDG else (wx_h, wxhl_img)
        dt = u % NDG
        slot = u % 2
        if u == 0:
            P.wait("scalar", "TEN", P.count("TEN"))
            AXS["ps"] = [P.count("PS0"), P.count("PS1")]
        P.wait("scalar", "VEC", AXS["vv"][slot])
        P.dma("scalar", pool(AXW[slot], 1024),
              bass.AP(w_hh, dt * 128 * 1024, [[1024, 128], [1, 1024]]),
              "WS%d" % (2 + slot))
        wv = P.count("WS%d" % (2 + slot))
        P.wait("vector", "WS%d" % (2 + slot), wv)
        win = pool(AXW[slot], 1024)
        hic = poolb(P_PS[slot], [[1, 512]])
        loc = poolb(P_PS[slot] + 512, [[1, 512]])
        P.wait("vector", "PS%d" % slot, AXS["ps"][slot])
        vop(lambda e, a=win, h=hic: e.tensor_copy(h, a))
        vop(lambda e, a=win, h=hic, l=loc: e.tensor_tensor(
            l, a, h, ALU.subtract))
        vv = P.drain_inc("vector", "VEC")
        AXS["vv"][slot] = vv
        P.wait("scalar", "VEC", vv)
        for off, src0 in ((0, P_PS[slot]), (1024, P_PS[slot] + 512)):
            P.dma("scalar",
                  bass.AP(img, off + dt * 128,
                          [[2048, 128], [128 * 2048, NDG], [1, 128]]),
                  poolb(src0, [[64, NDG], [1, 64]]), "PS%d" % slot)
        AXS["ps"][slot] = P.count("PS%d" % slot)
        if u == 2 * NDG - 1:
            PREP_AX_END["PS0"] = P.count("PS0")
            PREP_AX_END["PS1"] = P.count("PS1")

    def ln_stats(xwaits=None):
        for tt in range(NTT):
            if xwaits is not None and tt % 2 == 0:
                P.wait("vector", xwaits[tt // 2][0], xwaits[tt // 2][1])
            xs = XR[:, tt * 1024:tt * 1024 + 512]
            xs2 = XR[:, tt * 1024 + 512:tt * 1024 + 1024]
            vop(lambda e, a=xs: e.bn_stats(STT[:, S_BN:S_BN + 6], a))
            P.op("vector", lambda e, a=xs2: e.bn_stats(
                STT[:, S_BN + 6:S_BN + 12], a))
            vop(lambda e, tt=tt: e.bn_aggr(
                STT[:, S_MV + 2 * tt:S_MV + 2 * tt + 2],
                STT[:, S_BN:S_BN + 12]))
        v_st = P.drain_inc("vector", "VEC")
        P.wait("scalar", "VEC", v_st)
        var_ap = bass.AP(STT, S_MV + 1, [[STATS_N, 128], [2, NTT]])
        sop(lambda e: e.activation(STT[:, S_RSTD:S_RSTD + NTT], var_ap, AF.Ln,
                                   bias=co(C_EPS), scale=1.0))
        sop(lambda e: e.activation(STT[:, S_RSTD:S_RSTD + NTT],
                                   STT[:, S_RSTD:S_RSTD + NTT], AF.Exp,
                                   bias=0.0, scale=co(C_MHALF)))
        return P.drain_inc("scalar", "SCA")

    def ln_stats_quarter(q4, absmax=True):
        for tl in range(4):
            tt = q4 * 4 + tl
            xs = XR[:, tt * 1024:tt * 1024 + 512]
            xs2 = XR[:, tt * 1024 + 512:tt * 1024 + 1024]
            vop(lambda e, a=xs: e.bn_stats(STT[:, S_BN:S_BN + 6], a))
            P.op("vector", lambda e, a=xs2: e.bn_stats(
                STT[:, S_BN + 6:S_BN + 12], a))
            vop(lambda e, tt=tt: e.bn_aggr(
                STT[:, S_MV + 2 * tt:S_MV + 2 * tt + 2],
                STT[:, S_BN:S_BN + 12]))
        v_st = P.drain_inc("vector", "VEC")
        P.wait("scalar", "VEC", v_st)
        var_ap = bass.AP(STT, S_MV + 8 * q4 + 1, [[STATS_N, 128], [2, 4]])
        sop(lambda e, q4=q4, va=var_ap: e.activation(
            STT[:, S_RSTD + 4 * q4:S_RSTD + 4 * q4 + 4], va, AF.Ln,
            bias=co(C_EPS), scale=1.0))
        s_r = sop(lambda e, q4=q4: e.activation(
            STT[:, S_RSTD + 4 * q4:S_RSTD + 4 * q4 + 4],
            STT[:, S_RSTD + 4 * q4:S_RSTD + 4 * q4 + 4], AF.Exp,
            bias=0.0, scale=co(C_MHALF)), inc="SCA")
        if not absmax:
            return s_r
        P.wait("vector", "SCA", s_r)
        for tl in range(4):
            tt = q4 * 4 + tl
            vop(lambda e, tt=tt, s=tl % 2: e.tensor_scalar(
                pool(P_UT2[s], 1024), XR[:, tt * 1024:(tt + 1) * 1024],
                stc(S_MV + 2 * tt), stc(S_RSTD + tt),
                ALU.subtract, ALU.mult))
            vop(lambda e, tt=tt, s=tl % 2: e.tensor_reduce(
                stc(S_GM + tt), pool(P_UT2[s], 1024), AX.X, ALU.max,
                apply_absolute_value=True))
        return s_r

    def allreduce_max(src_col, dst_col):
        tv = top(lambda e, s=src_col: e.transpose(pT[0:1, 0:128], s, IDT[:]),
                 inc="TEN")
        P.wait("vector", "TEN", tv)
        vop(lambda e: e.tensor_reduce(CO[0:1, C_TMP:C_TMP + 1],
                                      pT[0:1, 0:128], AX.X, ALU.max))
        v = P.drain_inc("vector", "VEC")
        P.wait("sync", "VEC", v)
        P.wait("sync", "CCD", P.count("CCD"))
        cv = P.dma("sync", cc_in.ap(), CO[0:1, C_TMP:C_TMP + 1], "CCS")
        P.wait("gpsimd", "CCS", cv)
        cc = P.op("gpsimd", lambda e: e.collective_compute(
            "AllReduce", ALU.max, ins=[cc_in.ap()], outs=[cc_out.ap()],
            replica_groups=RG), inc="CCD")
        P.wait("sync", "CCD", cc)
        gv = P.dma("sync", co(dst_col), bass.AP(cc_out, 0, [[0, 128], [1, 1]]),
                   "LD")
        return gv, tv

    def ffn_pass(fi, overlap_fn=None, pre_fn=None, pre=None):
        import os as _os
        _stop = _os.environ.get("KSTOP", "")
        if pre is not None:
            s_rstd = pre["s_rstd"]
            gv = pre["gv"]
            gv_dist = None
            P.wait("vector", "LD", gv)
            P.wait("vector", "SCA", s_rstd)
            vop(lambda e: e.tensor_scalar(co(C_SX), co(C_GMAX), co(C_SIGXO),
                                          None, ALU.mult))
            return ffn_tail(fi, _stop, s_rstd, gv_dist, overlap_fn)
        xw = None
        if pre_fn is not None:
            pre_fn()
            xw = [("XA%d" % (g // 2), XV[g | 1]) for g in range(8)]
        s_rstd = ln_stats(xw)
        P.wait("vector", "SCA", s_rstd)
        for tt in range(NTT):
            slot = tt % 2
            vop(lambda e, tt=tt, s=slot: e.tensor_scalar(
                pool(P_UT[s], 1024), XR[:, tt * 1024:(tt + 1) * 1024],
                stc(S_MV + 2 * tt), stc(S_RSTD + tt),
                ALU.subtract, ALU.mult))
            vop(lambda e, tt=tt, s=slot: e.tensor_reduce(
                stc(S_GM + tt), pool(P_UT[s], 1024), AX.X, ALU.max,
                apply_absolute_value=True))
        vop(lambda e: e.tensor_reduce(co(C_GMAX), STT[:, S_GM:S_GM + NTT],
                                      AX.X, ALU.max))
        v_gm = P.drain_inc("vector", "VEC")
        P.wait("tensor", "VEC", v_gm)
        gv_dist = None
        if overlap_fn is not None:
            tv = top(lambda e: e.transpose(pT[0:1, 0:128], co(C_GMAX),
                                           IDT[:]), inc="TEN")
            P.wait("vector", "TEN", tv)
            vop(lambda e: e.tensor_reduce(CO[0:1, CSCL + 7:CSCL + 8],
                                          pT[0:1, 0:128], AX.X, ALU.max))
            P.drain_inc("vector", "VEC")
            overlap_fn()
            gv_dist = P.count("SCL")
            P.wait("vector", "SCL", gv_dist)
            vop(lambda e: e.tensor_reduce(co(C_GMAX), CO[:, CGM8:CGM8 + 8],
                                          AX.X, ALU.max))
        else:
            gv, _ = allreduce_max(co(C_GMAX), C_GMAX)
            P.wait("vector", "LD", gv)
        vop(lambda e: e.tensor_scalar(co(C_SX), co(C_GMAX), co(C_SIGXO), None,
                                      ALU.mult))
        return ffn_tail(fi, _stop, s_rstd, gv_dist, overlap_fn)

    def ffn_tail(fi, _stop, s_rstd, gv_dist, overlap_fn):
        vop(lambda e: e.reciprocal(co(C_INVS), co(C_SX)))
        vop(lambda e: e.tensor_scalar(STT[:, S_RSTDI:S_RSTDI + NTT],
                                      STT[:, S_RSTD:S_RSTD + NTT],
                                      co(C_INVS), None, ALU.mult))
        P.drain_inc("vector", "VEC")

        if _stop == "ffnAmax":
            for it in range(NIT):
                P.op("vector", lambda e, it=it: e.tensor_scalar(
                    co(C_SGX + it), co(C_SG + it), co(C_SX), None, ALU.mult))
                P.op("vector", lambda e, it=it: e.tensor_scalar(
                    co(C_SUX + it), co(C_SU + it), co(C_SX), None, ALU.mult))
            P.drain_inc("vector", "VEC")
            return
        t_xp = [0] * NTT
        s_ev = [0] * NTT
        g_cl = [0] * NTT
        P.wait("vector", "TEN", P.count("TEN"))
        P.wait("scalar", "WS0", P.count("WS0"))
        P.wait("scalar", "WS1", P.count("WS1"))
        s_q = [0] * NTT
        for tt in range(NTT):
            slot = tt % 2
            if tt >= 2:
                P.wait("vector", "SCA", s_q[tt - 2])
            v_u = vop(lambda e, tt=tt, s=slot: e.tensor_scalar(
                pool(P_UT[s], 1024), XR[:, tt * 1024:(tt + 1) * 1024],
                stc(S_MV + 2 * tt), stc(S_RSTDI + tt),
                ALU.subtract, ALU.mult), inc="VEC")
            P.wait("scalar", "VEC", v_u)
            if tt >= 2:
                P.wait("scalar", "VEC", g_cl[tt - 2])
            s_q[tt] = sop(lambda e, s=slot: e.activation(
                pool(P_QTS[s], 1024), pool(P_UT[s], 1024),
                AF.Identity, bias=co(C_A), scale=1.0), inc="SCA")
            P.wait("vector", "SCA", s_q[tt])
            if tt >= 2:
                P.wait("vector", "TEN", t_xp[tt - 2])
            vop(lambda e, s=slot: e.tensor_scalar(
                pool(P_QCS[s], 1024), pool(P_QTS[s], 1024),
                co(C_AM128), co(C_AP127), ALU.max, ALU.min))
            g_cl[tt] = P.drain_inc("vector", "VEC")
            pp = pT if slot == 0 else pT2
            P.wait("tensor", "VEC", g_cl[tt])
            if tt >= 2:
                P.wait("tensor", "SCA", s_ev[tt - 2])
            for j in range(8):
                t_xp[tt] = top(lambda e, pp=pp, j=j, s=slot: e.transpose(
                    pp[:, j * 128:(j + 1) * 128],
                    pool(P_QCS[s] + j * 128, 128), IDT[:]),
                    inc="TEN" if j == 7 else None)
            P.wait("scalar", "TEN", t_xp[tt])
            s_ev[tt] = sop(lambda e, pp=pp, tt=tt: e.activation(
                poolb(P_XQT + tt * 64, [[1024, 8], [1, 64]]),
                bass.AP(pp, 0, [[1024, 128], [128, 8], [1, 128]]),
                AF.Identity, bias=co(C_NEGA), scale=1.0), inc="SCA")

        if gv_dist is not None:
            P.wait("vector", "SCL", gv_dist)
        for it in range(NIT):
            P.op("vector", lambda e, it=it: e.tensor_scalar(
                co(C_SGX + it), co(C_SG + it), co(C_SX), None, ALU.mult))
            P.op("vector", lambda e, it=it: e.tensor_scalar(
                co(C_SUX + it), co(C_SU + it), co(C_SX), None, ALU.mult))
        v_scales = P.drain_inc("vector", "VEC")
        if _stop == "ffnB":
            return
        vop(lambda e: e.memset(stc(S_HM), 0.0))
        P.wait("sync", "VEC", PREP_END["VEC"])
        P.wait("sync", "SCA", PREP_END["SCA"])
        P.wait("sync", "CCD", CCD_AG1[0])
        t_xp_last = t_xp[NTT - 1]
        v_bq_done = P.count("VEC")
        w_v = [0] * NIT
        for it in (0, 1):
            slot = it % 2
            P.dma("sync", poolb(P_WRG[slot], [[1, 512]]),
                  bass.AP(ag1_out, it * 131072, [[1024, 128], [1, 1024]]),
                  "WS%d" % slot)
            w_v[it] = P.dma("sync", poolb(P_WRU[slot], [[1, 512]]),
                            bass.AP(ag1_out, (NIT + it) * 131072,
                                    [[1024, 128], [1, 1024]]), "WS%d" % slot)
        NK = NIT * 4
        t_g = [0] * NK
        t_u = [0] * NK
        s_si = [0] * NK
        v_h = [0] * NK
        g_h = [0] * NK
        hsp = [0] * NK
        t_last_it = [0] * NIT
        P.wait("scalar", "VEC", v_scales)
        for it in range(NIT):
            slot = it % 2
            if overlap_fn is not None and 2 <= it < 2 + 2 * NDG:
                prep_ax_unit(it - 2)
            if it >= 2:
                P.wait("sync", "TEN", t_last_it[it - 2])
                P.dma("sync", poolb(P_WRG[slot], [[1, 512]]),
                      bass.AP(ag1_out, it * 131072, [[1024, 128], [1, 1024]]),
                      "WS%d" % slot)
                w_v[it] = P.dma("sync", poolb(P_WRU[slot], [[1, 512]]),
                                bass.AP(ag1_out, (NIT + it) * 131072,
                                        [[1024, 128], [1, 1024]]),
                                "WS%d" % slot)
            for tc in range(4):
                k = it * 4 + tc
                pg = pAB[2 * (k % 2)]
                pu = pAB[2 * (k % 2) + 1]
                P.wait("tensor", "WS%d" % slot, w_v[it])
                if it == 0:
                    P.wait("tensor", "SCA", s_ev[4 * tc + 3])
                if k >= 2:
                    P.wait("tensor", "SCA", s_si[k - 2])
                    P.wait("tensor", "VEC", v_h[k - 2])
                for g in range(NDG):
                    t_g[k] = top(lambda e, g=g, s=slot, tc=tc, pg=pg: e.matmul(
                        pg[:], poolb(P_WRG[s] + g * 64, [[1, 64]]),
                        poolb(P_XQT + g * 1024 + tc * 256, [[1, 256]]),
                        start=(g == 0), stop=(g == NDG - 1)),
                        inc="TEN" if g == NDG - 1 else None)
                for g in range(NDG):
                    t_u[k] = top(lambda e, g=g, s=slot, tc=tc, pu=pu: e.matmul(
                        pu[:], poolb(P_WRU[s] + g * 64, [[1, 64]]),
                        poolb(P_XQT + g * 1024 + tc * 256, [[1, 256]]),
                        start=(g == 0), stop=(g == NDG - 1)),
                        inc="TEN" if g == NDG - 1 else None)
                if tc == 3:
                    t_last_it[it] = t_u[k]
                P.wait("scalar", "TEN", t_g[k])
                if k >= 4:
                    P.wait("scalar", "VEC", v_h[k - 4])
                s_si[k] = sop(lambda e, k=k, it=it, pg=pg: e.activation(
                    pool(P_SIL + (k % 4) * 512, 512), pg[:], AF.Silu,
                    bias=0.0, scale=co(C_SGX + it)), inc="SCA")
                P.wait("vector", "TEN", t_u[k])
                P.wait("vector", "SCA", s_si[k])
                if k >= 4:
                    P.wait("vector", "HS%d" % (k % 4), hsp[k - 4])
                v_h[k] = vop(lambda e, k=k, it=it, pu=pu: e.scalar_tensor_tensor(
                    pool(P_HT + (k % 4) * 512, 512), pu[:], co(C_SUX + it),
                    pool(P_SIL + (k % 4) * 512, 512), ALU.mult, ALU.mult),
                    inc="VEC")
                vop(lambda e, k=k: e.tensor_reduce(
                    stc(S_T1), pool(P_HT + (k % 4) * 512, 512), AX.X, ALU.max,
                    apply_absolute_value=True))
                vop(lambda e: e.tensor_tensor(stc(S_HM), stc(S_HM), stc(S_T1),
                                              ALU.max))
                g_h[k] = P.drain_inc("vector", "VEC")
                P.wait("sync", "VEC", v_h[k])
                hsp[k] = P.dma("sync",
                               bass.AP(h_spill, (tc * NIT + it) * 65536,
                                       [[512, 128], [1, 512]]),
                               pool(P_HT + (k % 4) * 512, 512), "HS%d" % (k % 4))

        if _stop == "ffnC":
            return
        for q in ("sync", "scalar"):
            P.wait(q, "TEN", t_xp_last)
            P.wait(q, "VEC", v_bq_done)
            P.wait(q, "PS0", P.count("PS0"))
            P.wait(q, "PS1", P.count("PS1"))
            P.wait(q, "CCD", CCD_AG2[0])
        wdt_v = 0
        for t8 in range(8):
            wdt_v = P.dma("sync" if t8 % 2 == 0 else "scalar",
                          poolb(P_WDT + t8 * 1408, [[1, 1408]]),
                          bass.AP(ag2_out, 3 * t8 * 131072,
                                  [[3072, 128], [1, 2816]]), "WDS")
        P.wait("tensor", "VEC", g_h[NK - 1])
        gv2, _ = allreduce_max(stc(S_HM), C_GMAX)
        P.wait("vector", "LD", gv2)
        P.wait("vector", "SCL", V_SDB[0])
        vop(lambda e: e.tensor_scalar(co(C_SX), co(C_GMAX), co(C_SIGDO), None,
                                      ALU.mult))
        vop(lambda e: e.reciprocal(co(C_INVS), co(C_SX)))
        vop(lambda e: e.tensor_scalar(co(C_HALFSH), co(C_SX), 0.5, None,
                                      ALU.mult))
        vop(lambda e: e.tensor_scalar(SDBF[:], SDB[:], co(C_HALFSH), None,
                                      ALU.mult))
        v_sdbf = P.drain_inc("vector", "VEC")
        P.wait("sync", "TEN", t_last_it[NIT - 1])
        P.wait("sync", "VEC", g_h[NK - 1])
        P.wait("sync", "SCA", s_si[NK - 1])
        hrv = [[0, 0] for _ in range(4)]
        g_hq = [[0] * NIT for _ in range(4)]
        t_dm = [0] * NTT
        v_ep = [0] * NTT
        first_q = True
        prev_vdq = None
        for s4 in range(4):
            P.wait("sync", "HS%d" % s4, P.count("HS%d" % s4))
        dps = [pT[:, 0:512], pT[:, 512:1024], pT2[:, 0:512], pT2[:, 512:1024],
               pAB[0][:], pAB[1][:], pAB[2][:], pAB[3][:]]
        t_mm = [[0] * NIT for _ in range(4)]
        v_ep2 = [[0] * 4 for _ in range(4)]

        def d_readback(tc):
            for hf, q in ((0, "sync"), (1, "scalar")):
                if tc >= 1:
                    P.wait(q, "SCA", prev_sdq[hf * 11 + 10])
                elif hf == 1:
                    P.wait(q, "TEN", t_last_it[NIT - 1])
                    P.wait(q, "VEC", g_h[NK - 1])
                    for s4 in range(4):
                        P.wait(q, "HS%d" % s4, P.count("HS%d" % s4))
                hrv[tc][hf] = P.dma(
                    q, pool3(P_HRD + hf * 11 * 512, [[512, 11], [1, 512]]),
                    bass.AP(h_spill, (tc * NIT + hf * 11) * 65536,
                            [[512, 128], [65536, 11], [1, 512]]), "HR%d" % hf)

        d_readback(0)
        prev_sdq = None
        for tc in range(4):
            v_dq = [0] * NIT
            s_dq = [0] * NIT
            for ig in range(NIT):
                P.wait("scalar", "HR%d" % (ig // 11), hrv[tc][ig // 11])
                if first_q:
                    P.wait("scalar", "VEC", v_sdbf)
                    for s4 in range(4):
                        P.wait("vector", "HS%d" % s4, P.count("HS%d" % s4))
                    first_q = False
                if ig >= 2:
                    P.wait("scalar", "VEC", v_dq[ig - 2])
                elif prev_sdq is not None:
                    P.wait("scalar", "VEC", prev_vdq[NIT - 2 + ig])
                s_dq[ig] = sop(lambda e, ig=ig: e.activation(
                    pool(P_DQ[ig % 2], 512), pool(P_HRD + ig * 512, 512),
                    AF.Identity, bias=co(C_A), scale=co(C_INVS)), inc="SCA")
                P.wait("vector", "SCA", s_dq[ig])
                if tc >= 1:
                    P.wait("vector", "TEN", t_mm[tc - 1][ig])
                vop(lambda e, ig=ig: e.tensor_scalar(
                    pool(P_DQ[ig % 2], 512), pool(P_DQ[ig % 2], 512),
                    co(C_A), -128.0, ALU.subtract, ALU.max), drain=True)
                vop(lambda e, ig=ig: e.tensor_single_scalar(
                    poolb(P_HQ + ig * 256, [[1, 256]]),
                    pool(P_DQ[ig % 2], 512), 127.0, ALU.min))
                v_dq[ig] = P.drain_inc("vector", "VEC")
                g_hq[tc][ig] = v_dq[ig]
                P.wait("tensor", "VEC", v_dq[ig])
                if ig == 0:
                    if tc == 0:
                        P.wait("tensor", "WDS", wdt_v)
                    else:
                        for tq2 in range(4):
                            P.wait("tensor", "VEC", v_ep2[tc - 1][tq2])
                for tq in range(4):
                    for half in range(2):
                        ps = dps[(2 * tq + half) % 8]
                        t_mm[tc][ig] = top(
                            lambda e, ig=ig, tq=tq, half=half, ps=ps: e.matmul(
                                ps, poolb(P_HQ + ig * 256 + tq * 64, [[1, 64]]),
                                poolb(P_WDT + half * 4 * 1408 + ig * 64,
                                      [[1408, 4], [1, 64]]),
                                start=(ig == 0), stop=(ig == NIT - 1)),
                            inc="TEN" if half == 1 and tq == 3 else None)
            prev_sdq = s_dq
            prev_vdq = v_dq
            if tc < 3:
                d_readback(tc + 1)
            for tq in range(4):
                tt = tc * 4 + tq
                P.wait("vector", "TEN", t_mm[tc][NIT - 1])
                for half in range(2):
                    ps = dps[(2 * tq + half) % 8]
                    vop(lambda e, ps=ps, half=half: e.tensor_tensor(
                        pool(P_TD + half * 512, 512), ps,
                        SDBF[:, half * 512:(half + 1) * 512], ALU.mult))
                vop(lambda e, tt=tt: e.tensor_tensor(
                    XR[:, tt * 1024:(tt + 1) * 1024],
                    XR[:, tt * 1024:(tt + 1) * 1024],
                    pool(P_TD, 1024), ALU.add))
                v_ep2[tc][tq] = P.drain_inc("vector", "VEC")
            if fi == 0:
                ATTN_RSTD[0] = ln_stats_quarter(tc, absmax=False)
        return v_ep2

    ATTN_RSTD = [0]

    def attn_pass():
        s_rstd = ATTN_RSTD[0]
        if not s_rstd:
            s_rstd = ln_stats()
        P.wait("vector", "SCA", s_rstd)
        for g in range(8):
            P.op("vector", lambda e, g=g: e.memset(co(C_YL + g), 0.0))
            P.op("vector", lambda e, g=g: e.memset(co(C_AL + g), 1.0))
        v_init = P.drain_inc("vector", "VEC")
        aw_v = [0] * 8
        t_ri = [0] * 8
        ac_v = [0] * NQ
        st = {
            "v_scan": [0] * 8,
            "t_y": [0] * QTT,
            "v_xr": [0] * QTT,
            "s_at": [0] * 8,
            "s_tb": [0] * 8,
        }

        def emit_A(qq, prev_t_ut):
            par = qq % 2
            s_uhi = [0] * QTT
            v_ulo = [0] * QTT
            t_ut = [0] * QTT
            for tl in range(QTT):
                ti = qq * QTT + tl
                slot = tl % 2
                pp = pT if slot == 0 else pT2
                if tl >= 2:
                    P.wait("vector", "TEN", t_ut[tl - 2])
                elif prev_t_ut is not None:
                    P.wait("vector", "TEN", prev_t_ut[tl + 2])
                vop(lambda e, ti=ti, s=slot: e.tensor_scalar(
                    pool(P_UT2[s], 1024), XR[:, ti * 1024:(ti + 1) * 1024],
                    stc(S_MV + 2 * ti), stc(S_RSTD + ti),
                    ALU.subtract, ALU.mult))
                vu = P.drain_inc("vector", "VEC")
                P.wait("tensor", "VEC", vu)
                if tl >= 2:
                    P.wait("tensor", "SCA", s_uhi[tl - 2])
                    P.wait("tensor", "VEC", v_ulo[tl - 2])
                else:
                    P.wait("tensor", "VEC", st["v_xr"][2 + tl])
                for j in range(8):
                    t_ut[tl] = top(lambda e, pp=pp, j=j, s=slot: e.transpose(
                        pp[:, j * 128:(j + 1) * 128],
                        pool(P_UT2[s] + j * 128, 128), IDT[:]),
                        inc="TEN" if j == 7 else None)
                P.wait("scalar", "TEN", t_ut[tl])
                s_uhi[tl] = sop(lambda e, pp=pp, tl=tl, par=par: e.activation(
                    poolb(P_UHI[par] + tl * 64, [[256, 8], [1, 64]]),
                    bass.AP(pp, 0, [[1024, 128], [128, 8], [1, 128]]),
                    AF.Identity, bias=0.0, scale=1.0), inc="SCA")
                P.wait("vector", "SCA", s_uhi[tl])
                v_ulo[tl] = vop(lambda e, pp=pp, tl=tl, par=par: e.tensor_tensor(
                    poolb(P_ULO[par] + tl * 64, [[256, 8], [1, 64]]),
                    bass.AP(pp, 0, [[1024, 128], [128, 8], [1, 128]]),
                    poolb(P_UHI[par] + tl * 64, [[256, 8], [1, 64]]),
                    ALU.subtract), inc="VEC")
            return s_uhi, v_ulo, t_ut

        def aw_load(ht, q="sync", sem=None):
            slot = ht % 2
            sem = sem or "WS%d" % slot
            P.dma(q, poolb(P_AWR[slot], [[1, 1024]]),
                  bass.AP(wahl_img, ht * 262144, [[2048, 128], [1, 2048]]),
                  sem)
            v = P.dma(q, poolb(P_AWR[slot] + 1024, [[1, 1024]]),
                      bass.AP(wxhl_img, ht * 262144,
                              [[2048, 128], [1, 2048]]),
                      sem)
            aw_v[ht] = (sem, v)

        def emit_MM(qq, v_ulo, ht0, ht1, s_at, s_tb, binter=None):
            par = qq % 2
            if qq == 0 and ht0 == 0:
                P.wait("sync", "PS0", PREP_AX_END["PS0"])
                P.wait("sync", "PS1", PREP_AX_END["PS1"])
                aw_load(0)
                aw_load(1)
            for ht in range(ht0, ht1):
                slot = ht % 2
                if ht >= 2:
                    P.wait("sync", "TEN", t_ri[ht - 2])
                    aw_load(ht)
                pr = pAB[2 * (ht % 2)]
                pi = pAB[2 * (ht % 2) + 1]
                P.wait("tensor", aw_v[ht][0], aw_v[ht][1])
                if ht == 0:
                    P.wait("tensor", "VEC", v_ulo[QTT - 1])
                if ht >= 2:
                    P.wait("tensor", "SCA", s_at[ht - 2])
                    P.wait("tensor", "SCA", s_tb[ht - 2])
                elif qq > 0:
                    P.wait("tensor", "SCA", st["s_at"][ht + 6])
                    P.wait("tensor", "SCA", st["s_tb"][ht + 6])
                mmn = 0
                for woff, uoff in ((0, P_UHI[par]), (0, P_ULO[par]),
                                   (512, P_UHI[par])):
                    for g in range(NDG):
                        mmn += 1
                        t_ri[ht] = top(
                            lambda e, g=g, s=slot, wo=woff, uo=uoff, pr=pr,
                            mm=mmn: e.matmul(
                                pr[:], poolb(P_AWR[s] + wo + g * 64, [[1, 64]]),
                                poolb(uo + g * 256, [[1, 256]]),
                                start=(mm == 1), stop=(mm == 24)),
                            inc="TEN" if mmn == 24 else None)
                P.wait("scalar", "TEN", t_ri[ht])
                if qq > 0:
                    P.wait("scalar", "VEC", st["v_scan"][ht])
                    if ht == 0:
                        P.wait("scalar", "ACS", ac_v[qq - 1][0])
                    elif ht == 4:
                        P.wait("scalar", "ACS2", ac_v[qq - 1][1])
                sop(lambda e, pr=pr, ht=ht: e.activation(
                    pool(P_SC[0], 512), pr[:], AF.Tanh,
                    bias=co(C_B2A + ht), scale=co(C_HALF)))
                s_at[ht] = sop(lambda e, ht=ht: e.activation(
                    pool(P_AT + ht * 512, 512), pool(P_SC[0], 512), AF.Exp,
                    bias=co(C_LNA + ht), scale=co(C_MLOG8H)), inc="SCA")
                mmn = 0
                for woff, uoff in ((1024, P_UHI[par]), (1024, P_ULO[par]),
                                   (1536, P_UHI[par])):
                    for g in range(NDG):
                        mmn += 1
                        t_ri[ht] = top(
                            lambda e, g=g, s=slot, wo=woff, uo=uoff, pi=pi,
                            mm=mmn: e.matmul(
                                pi[:], poolb(P_AWR[s] + wo + g * 64, [[1, 64]]),
                                poolb(uo + g * 256, [[1, 256]]),
                                start=(mm == 1), stop=(mm == 24)),
                            inc="TEN" if mmn == 24 else None)
                P.wait("scalar", "TEN", t_ri[ht])
                s_tb[ht] = sop(lambda e, pi=pi, ht=ht: e.activation(
                    pool(P_BT + ht * 512, 512), pi[:], AF.Tanh,
                    bias=co(C_B2X + ht), scale=co(C_HALF)), inc="SCA")
                if binter is not None and ht >= 4:
                    emit_B_ht(qq, ht - 4, s_at, s_tb, binter[0], binter[1])
            if ht1 == 8 and binter is not None:
                for bh in range(ht1 - 4, 8):
                    emit_B_ht(qq, bh, s_at, s_tb, binter[0], binter[1])
            if ht1 == 8 and qq + 1 < NQ:
                gw = "GW%d" % qq
                P.wait("gpsimd", gw, P.count(gw))
                P.wait("gpsimd", "TEN", t_ri[6])
                aw_load(0, q="gpsimd", sem=gw)
                P.wait("gpsimd", gw, P.count(gw))
                P.wait("gpsimd", "TEN", t_ri[7])
                aw_load(1, q="gpsimd", sem=gw)
            return s_at, s_tb

        def emit_B_ht(qq, ht, s_at, s_tb, v_scan, v_b1):
            par = qq % 2
            if True:
                sc = P_SCP[ht % 2]
                P.wait("scalar", "VEC",
                       v_b1[ht - 2] if ht >= 2 else st["v_scan"][6 + (ht % 2)])
                P.wait("scalar", "SCA", s_tb[ht])
                sop(lambda e, ht=ht, sc=sc: e.activation(
                    pool(sc[0], 512), pool(P_AT + ht * 512, 512), AF.Square,
                    bias=0.0, scale=1.0))
                sop(lambda e, sc=sc: e.activation(
                    pool(sc[1], 512), pool(sc[0], 512), AF.Ln,
                    bias=co(C_ONE), scale=co(C_MONE)))
                s_s = sop(lambda e, sc=sc: e.activation(
                    pool(sc[2], 512), pool(sc[1], 512), AF.Exp,
                    bias=0.0, scale=co(C_HALF)), inc="SCA")
                P.wait("vector", "SCA", s_s)
                vop(lambda e, ht=ht, par=par: e.tensor_tensor(
                    pool(P_SC[3], 512),
                    poolb(P_UHI[par] + ht * 256, [[1, 256]]),
                    poolb(P_ULO[par] + ht * 256, [[1, 256]]), ALU.add))
                vop(lambda e, ht=ht: e.tensor_scalar(
                    pool(P_SC[4], 512), pool(P_BT + ht * 512, 512),
                    0.5, 0.5, ALU.mult, ALU.add))
                vop(lambda e, sc=sc: e.tensor_tensor(
                    pool(P_SC[4], 512), pool(P_SC[4], 512), pool(sc[2], 512),
                    ALU.mult))
                v_b1[ht] = P.drain_inc("vector", "VEC")
                vop(lambda e, ht=ht: e.tensor_tensor(
                    pool(P_BT + ht * 512, 512), pool(P_SC[4], 512),
                    pool(P_SC[3], 512), ALU.mult))
                if ht == 0:
                    P.wait("vector", "TEN", st["t_y"][QTT - 1])
                vop(lambda e, ht=ht: e.tensor_tensor_scan(
                    pool(P_Y + ht * 512, 512), pool(P_AT + ht * 512, 512),
                    pool(P_BT + ht * 512, 512), co(C_YL + ht),
                    ALU.mult, ALU.add))
                vop(lambda e, ht=ht: e.tensor_copy(
                    co(C_YL + ht), pool(P_Y + ht * 512 + 511, 1)))
                vop(lambda e, ht=ht: e.tensor_tensor_scan(
                    pool(P_BT + ht * 512, 512), pool(P_AT + ht * 512, 512),
                    cob(C_ZERO, 512), co(C_AL + ht), ALU.mult, ALU.add))
                vop(lambda e, ht=ht: e.tensor_copy(
                    co(C_AL + ht), pool(P_BT + ht * 512 + 511, 1)))
                v_scan[ht] = P.drain_inc("vector", "VEC")
            if ht == 3:
                P.wait("sync", "VEC", v_scan[3])
                a0 = P.dma("sync",
                           bass.AP(ac_spill, qq * 524288,
                                   [[4096, 128], [1, 2048]]),
                           pool(P_BT, 2048), "ACS")
                ac_v[qq] = (a0, 0)
            if ht == 7:
                P.wait("sync", "VEC", v_scan[7])
                a1 = P.dma("sync",
                           bass.AP(ac_spill, qq * 524288 + 2048,
                                   [[4096, 128], [1, 2048]]),
                           pool(P_BT + 2048, 2048), "ACS2")
                ac_v[qq] = (ac_v[qq][0], a1)

        def emit_EPI(qq, uhi_gate, v_scan_q, src_off=P_Y, v_gate=None):
            t_y = [0] * QTT
            v_xr = [0] * QTT
            for tl in range(QTT):
                ti = qq * QTT + tl
                pp = pT if tl % 2 == 0 else pT2
                if tl == 0:
                    if v_gate is not None:
                        P.wait("tensor", "VEC", v_gate)
                    else:
                        P.wait("tensor", "VEC", v_scan_q[7])
                    if uhi_gate is not None:
                        P.wait("tensor", "SCA", uhi_gate[0][QTT - 1])
                        P.wait("tensor", "VEC", uhi_gate[1][QTT - 1])
                if tl >= 2:
                    P.wait("tensor", "VEC", v_xr[tl - 2])
                for j in range(8):
                    t_y[tl] = top(lambda e, pp=pp, j=j, tl=tl, so=src_off:
                                  e.transpose(
                                      pp[:, j * 128:(j + 1) * 128],
                                      pool(so + j * 512 + tl * 128, 128),
                                      IDT[:]),
                                  inc="TEN" if j == 7 else None)
                P.wait("vector", "TEN", t_y[tl])
                v_xr[tl] = vop(lambda e, ti=ti, pp=pp: e.tensor_tensor(
                    XR[:, ti * 1024:(ti + 1) * 1024],
                    XR[:, ti * 1024:(ti + 1) * 1024], pp[:], ALU.add),
                    inc="VEC")
            return t_y, v_xr

        cur = emit_A(0, None)
        pend = None
        for qq in range(NQ):
            s_at, s_tb = emit_MM(qq, cur[1], 0, 2, [0] * 8, [0] * 8)
            if pend is not None:
                st["t_y"], st["v_xr"] = emit_EPI(
                    pend[0], (cur[0], cur[1]), pend[1])
            if qq + 1 < NQ:
                nxt = emit_A(qq + 1, cur[2])
            else:
                nxt = cur
            v_scan = [0] * 8
            v_b1 = [0] * 8
            s_at, s_tb = emit_MM(qq, cur[1], 2, 8, s_at, s_tb,
                                 binter=(v_scan, v_b1))
            st["v_scan"] = v_scan
            st["s_at"] = s_at
            st["s_tb"] = s_tb
            pend = (qq, v_scan)
            cur = nxt
        st["t_y"], st["v_xr"] = emit_EPI(pend[0], None, pend[1])
        P.wait("sync", "VEC", st["v_scan"][7])
        cv = P.dma("sync", bass.AP(ag_in, 0, [[1, 128], [128, 8]]),
                   CO[:, C_YL:C_YL + 8], "CCS")
        P.wait("gpsimd", "CCS", cv)
        agv = P.op("gpsimd", lambda e: e.collective_compute(
            "AllGather", ALU.bypass, ins=[ag_in.ap()], outs=[ag_out.ap()],
            replica_groups=RG), inc="CCD")
        rbuf = [P_BT, 0, P_AT, P_SC[0]]
        ybuf = [P_Y, P_AWR[0]]
        rb = [0] * NQ
        v_fq = [0] * NQ
        tys = [None] * NQ
        P.wait("sync", "ACS", ac_v[NQ - 1][0])
        P.wait("sync", "ACS2", ac_v[NQ - 1][1])
        P.wait("scalar", "ACS", ac_v[NQ - 1][0])
        P.wait("scalar", "ACS2", ac_v[NQ - 1][1])
        for qq in range(NQ):
            rb[qq] = P.dma("sync" if qq % 2 == 0 else "scalar",
                           pool(rbuf[qq], 4096),
                           bass.AP(ac_spill, qq * 524288,
                                   [[4096, 128], [1, 4096]]),
                           "HR%d" % qq)
        P.wait("sync", "CCD", agv)
        for r in range(8):
            g8v = P.dma("sync", CO[:, C_G8 + r * 8:C_G8 + r * 8 + 8],
                        bass.AP(ag_out, r * 1024, [[1, 128], [128, 8]]), "LD")
        P.wait("vector", "LD", g8v)
        vop(lambda e: e.memset(CO[:, C_CC:C_CC + 8], 0.0))
        for r in range(8):
            vop(lambda e, r=r: e.scalar_tensor_tensor(
                CO[:, C_CC:C_CC + 8],
                CO[:, C_G8 + r * 8:C_G8 + r * 8 + 8],
                co(C_SEL + r), CO[:, C_CC:C_CC + 8], ALU.mult, ALU.add))
        v_cc = P.drain_inc("vector", "VEC")
        for qq in range(NQ):
            par = qq % 2
            P.wait("vector", "HR%d" % qq, rb[qq])
            P.wait("vector", "VEC", v_cc)
            if qq == 0:
                P.wait("vector", "TEN", st["t_y"][QTT - 1])
            elif qq >= 2:
                P.wait("vector", "TEN", tys[qq - 2][QTT - 1])
            for ht in range(8):
                P.op("vector", lambda e, ht=ht, qq=qq, par=par:
                     e.tensor_scalar(
                         pool(ybuf[par] + ht * 512, 512),
                         pool(rbuf[qq] + ht * 512, 512),
                         co(C_CC + ht), None, ALU.mult))
            v_fq[qq] = P.drain_inc("vector", "VEC")
            tys[qq], st["v_xr"] = emit_EPI(qq, None, None,
                                           src_off=ybuf[par],
                                           v_gate=v_fq[qq])
            s_r2 = ln_stats_quarter(qq)
        st["t_y"] = tys[NQ - 1]
        vop(lambda e: e.tensor_reduce(co(C_GMAX), STT[:, S_GM:S_GM + NTT],
                                      AX.X, ALU.max))
        v_gm2 = P.drain_inc("vector", "VEC")
        P.wait("tensor", "VEC", v_gm2)
        gv2p, _ = allreduce_max(co(C_GMAX), C_GMAX)
        return {"s_rstd": s_r2, "gv": gv2p}

    import os
    stop = os.environ.get("KSTOP", "")
    P.barrier()
    if stop != "ln":
        if stop == "ffnA":
            ln_stats()
        else:
            ffn_pass(0, overlap_fn=prep_triggers, pre_fn=prep_compute)
            P.barrier()
            if stop not in ("ffn0", "ffnAmax", "ffnB", "ffnC", "ffnD"):
                attn_pre = attn_pass()
                P.barrier()
                if stop != "attn":
                    v_ep2f = ffn_pass(1, pre=attn_pre)
                    for tc in range(4):
                        q = "sync" if tc % 2 == 0 else "scalar"
                        P.wait(q, "VEC", v_ep2f[tc][3])
                        P.dma(q,
                              bass.AP(y_h, tc * 524288,
                                      [[1024, 128], [131072, 4], [1, 1024]]),
                              bass.AP(XR, tc * 4096,
                                      [[NTT * 1024, 128], [1024, 4],
                                       [1, 1024]]), "STO")
                    P.barrier()
    if stop in ("ln", "ffnA", "ffn0", "ffnAmax", "ffnB", "ffnC", "ffnD",
                "attn"):
        P.dma("sync",
              bass.AP(y_h, 0, [[1024, 128], [131072, NTT], [1, 1024]]),
              bass.AP(XR, 0, [[NTT * 1024, 128], [1024, NTT], [1, 1024]]),
              "STO")
        P.wait("sync", "STO", 16)
    else:
        P.wait("sync", "STO", 64)

    with nc.allow_non_contiguous_dma("small vector loads"), nc.Block() as block:
        @block.sync
        def _(e):
            for f in P.ops["sync"]:
                f(e)

        @block.vector
        def _(e):
            for f in P.ops["vector"]:
                f(e)

        @block.scalar
        def _(e):
            for f in P.ops["scalar"]:
                f(e)

        @block.tensor
        def _(e):
            for f in P.ops["tensor"]:
                f(e)

        @block.gpsimd
        def _(e):
            for f in P.ops["gpsimd"]:
                f(e)

    return nc


_CACHED = {}


def _get_nc():
    if "nc" not in _CACHED:
        _CACHED["nc"] = build_program()
    return _CACHED["nc"]


def kernel(L0_attn_x, L0_attn_Wa, L0_attn_Wx, L0_attn_ba, L0_attn_bx,
           L0_attn_Lam, L0_ffn_clip_x, L0_ffn_clip_down, L0_ffn_w_gate,
           L0_ffn_w_up, L0_ffn_w_down):
    x = np.ascontiguousarray(np.asarray(L0_attn_x, dtype=np.float32))
    B, T, _ = x.shape
    wg = np.ascontiguousarray(L0_ffn_w_gate, dtype=np.float32)
    wu = np.ascontiguousarray(L0_ffn_w_up, dtype=np.float32)
    wd = np.ascontiguousarray(L0_ffn_w_down, dtype=np.float32)
    wa = np.ascontiguousarray(L0_attn_Wa, dtype=np.float32)
    wx = np.ascontiguousarray(L0_attn_Wx, dtype=np.float32)
    gu_all = np.concatenate(
        [wg.reshape(NIT, 128, D), wu.reshape(NIT, 128, D),
         np.ones((4, 128, D), dtype=np.float32)], axis=0)
    shared = {
        "wa": wa,
        "wx": wx,
        "ba": np.ascontiguousarray(L0_attn_ba, dtype=np.float32),
        "bx": np.ascontiguousarray(L0_attn_bx, dtype=np.float32),
        "lam": np.ascontiguousarray(L0_attn_Lam, dtype=np.float32),
        "clipx": np.ascontiguousarray(L0_ffn_clip_x, dtype=np.float32),
        "clipd": np.ascontiguousarray(L0_ffn_clip_down, dtype=np.float32),
    }
    in_maps = []
    for c in range(NCORE):
        b, half = c // 2, c % 2
        sel = np.zeros(8, dtype=np.float32)
        if half == 1:
            sel[c - 1] = 1.0
        m = dict(shared)
        m["x"] = np.ascontiguousarray(x[b, half * NT:(half + 1) * NT, :])
        m["sel8"] = sel
        m["gus"] = np.ascontiguousarray(gu_all[6 * c:6 * c + 6])
        m["wds"] = np.ascontiguousarray(wd[128 * c:128 * c + 128, :])
        in_maps.append(m)
    nc = _get_nc()
    res = run_bass_kernel_spmd(nc, in_maps, core_ids=list(range(NCORE)))
    out = np.empty((B, T, D), dtype=np.float32)
    for c in range(NCORE):
        b, half = c // 2, c % 2
        out[b, half * NT:(half + 1) * NT, :] = np.asarray(res.results[c]["y"])
    return out


# version 29
# speedup vs baseline: 1.0242x; 1.0242x over previous
import numpy as np
import ml_dtypes
import concourse.bass as bass
import concourse.mybir as mybir
from concourse.bass_utils import run_bass_kernel_spmd

F32 = mybir.dt.float32
BF16 = mybir.dt.bfloat16
AF = mybir.ActivationFunctionType
ALU = mybir.AluOpType
AX = mybir.AxisListType

NCORE = 8
NT = 2048
D = 1024
DI = 2816
NIT = DI // 128
NDG = D // 128
NTT = NT // 128
NQ = 4
QT = NT // NQ
QTT = QT // 128
LOG8 = float(np.log(8.0))
EPS = 1e-5

POOL_N = 30720
P_PW = [0, 4096, 8192, 12288]
P_PQ = [16384, 20480]
P_PS = [24576, 26624]
P_XQT = 0
P_WRG = [8192, 8704]
P_WRU = [9216, 9728]
P_SIL = 10240
P_HT = 12288
P_UT = [14336, 15360]
P_QTS = [16384, 17408]
P_QCS = [18432, 19456]
P_HRD = 0
P_HQ = 11264
P_WDT = 16896
P_DQ = [28160, 28672]
P_TD = 29184
P_UHI = [0, 20480]
P_ULO = [2048, 22528]
P_AWR = [4096, 6144]
P_AT = 8192
P_BT = 12288
P_Y = 16384
P_UT2 = [24576, 25600]
P_SC = [26624 + 512 * k for k in range(6)]
P_SCP = [[26624, 27136, 27648], [29184, 29696, 30208]]

C_TMP = 0
C_FLAG = 1
C_A = 2
C_EPS = 3
C_ONE = 4
C_HALF = 5
C_ZERO = 6
C_MLOG8H = 7
C_SIGXO = 8
C_SIGDO = 9
C_INVS = 10
C_SX = 12
C_HALFSH = 14
C_GMAX = 15
C_B2A = 16
C_B2X = 24
C_LNA = 32
C_LAM = 40
C_SG = 48
C_SU = 70
C_SGX = 92
C_SUX = 114
C_YL = 136
C_AL = 144
C_CC = 152
C_G8 = 160
C_SEL = 224
C_CLIPX = 232
C_CLIPD = 233
C_MONE = 234
C_TMP2 = 235
C_TMP3 = 236
C_MHALF = 237
C_NEGA = 238
CSCL = 184
CGM8 = 192
C_SD = 200
CONSTS_N = 240

S_MV = 0
S_RSTD = 32
S_RSTDI = 48
S_GM = 64
S_HM = 80
S_T1 = 81
S_BN = 82
STATS_N = 96


class Plan:
    ENGS = ("sync", "vector", "scalar", "tensor", "gpsimd")

    def __init__(self, nc):
        self.nc = nc
        self.ops = {e: [] for e in self.ENGS}
        self.sems = {}

    def sem(self, name):
        self.sems[name] = [self.nc.alloc_semaphore(name=name), 0]

    def count(self, name):
        return self.sems[name][1]

    def wait(self, eng, name, value=None):
        if value is None:
            value = self.sems[name][1]
        if value is None or value <= 0:
            return
        h = self.sems[name][0]
        self.ops[eng].append(lambda e, h=h, v=value: e.wait_ge(h, v))

    def op(self, eng, fn, inc=None, drain=False):
        if drain:
            self.ops[eng].append(lambda e: e.drain())
        if inc is None:
            self.ops[eng].append(lambda e, f=fn: f(e))
            return None
        self.sems[inc][1] += 1
        v = self.sems[inc][1]
        hh = self.sems[inc][0]

        def g(e, f=fn, hh=hh):
            f(e).then_inc(hh, 1)
        self.ops[eng].append(g)
        return v

    def drain_inc(self, eng, inc):
        self.sems[inc][1] += 1
        v = self.sems[inc][1]
        hh = self.sems[inc][0]
        self.ops[eng].append(lambda e, hh=hh: e.drain().then_inc(hh, 1))
        return v

    def dma(self, eng, out_ap, in_ap, inc):
        self.sems[inc][1] += 16
        v = self.sems[inc][1]
        hh = self.sems[inc][0]

        def g(e, o=out_ap, i=in_ap, hh=hh):
            e.dma_start(out=o, in_=i).then_inc(hh, 16)
        self.ops[eng].append(g)
        return v

    def barrier(self):
        self.drain_inc("vector", "VEC")
        self.drain_inc("scalar", "SCA")
        self.drain_inc("gpsimd", "GPS")
        snap = {n: s[1] for n, s in self.sems.items()}
        for e in self.ENGS:
            for n, v in snap.items():
                if v > 0:
                    self.wait(e, n, v)


def build_program():
    nc = bass.Bass("TRN2", target_bir_lowering=False, debug=False)

    x_h = nc.dram_tensor("x", [NT, D], F32, kind="ExternalInput")
    gus_h = nc.dram_tensor("gus", [6, 128, D], F32, kind="ExternalInput")
    wds_h = nc.dram_tensor("wds", [128, DI], F32, kind="ExternalInput")
    wa_h = nc.dram_tensor("wa", [D, D], F32, kind="ExternalInput")
    wx_h = nc.dram_tensor("wx", [D, D], F32, kind="ExternalInput")
    ba_h = nc.dram_tensor("ba", [D], F32, kind="ExternalInput")
    bx_h = nc.dram_tensor("bx", [D], F32, kind="ExternalInput")
    lam_h = nc.dram_tensor("lam", [D], F32, kind="ExternalInput")
    clipx_h = nc.dram_tensor("clipx", [1], F32, kind="ExternalInput")
    clipd_h = nc.dram_tensor("clipd", [1], F32, kind="ExternalInput")
    sel_h = nc.dram_tensor("sel8", [8], F32, kind="ExternalInput")
    y_h = nc.dram_tensor("y", [NT, D], F32, kind="ExternalOutput")

    ag1_in = nc.dram_tensor("ag1_in", [1, 6 * 131072], BF16, kind="Internal")
    ag1_out = nc.dram_tensor("ag1_out", [8, 6 * 131072], BF16, kind="Internal")
    ag2_in = nc.dram_tensor("ag2_in", [1, 3 * 131072], BF16, kind="Internal")
    ag2_out = nc.dram_tensor("ag2_out", [8, 3 * 131072], BF16, kind="Internal")
    wahl_img = nc.dram_tensor("wahl_img", [NDG, 128, 2048], BF16, kind="Internal")
    wxhl_img = nc.dram_tensor("wxhl_img", [NDG, 128, 2048], BF16, kind="Internal")
    scl_in = nc.dram_tensor("scl_in", [1, 1024], F32, kind="Internal")
    scl_out = nc.dram_tensor("scl_out", [8, 1024], F32, kind="Internal")
    h_spill = nc.dram_tensor("h_spill", [4, NIT, 128, 512], F32, kind="Internal")
    ac_spill = nc.dram_tensor("ac_spill", [NQ, 128, 4096], F32, kind="Internal")
    cc_in = nc.dram_tensor("cc_in", [1, 1], F32, kind="Internal")
    cc_out = nc.dram_tensor("cc_out", [1, 1], F32, kind="Internal")
    ag_in = nc.dram_tensor("ag_in", [1, D], F32, kind="Internal")
    ag_out = nc.dram_tensor("ag_out", [8, D], F32, kind="Internal")
    ident_h = nc.inline_tensor(np.eye(128, dtype=np.float32), name="ident_dat")
    RG = [[0, 1, 2, 3, 4, 5, 6, 7]]

    XR = nc.alloc_sbuf_tensor("XR", [128, NTT * 1024], F32)
    POOL = nc.alloc_sbuf_tensor("POOL", [128, POOL_N], F32)
    CO = nc.alloc_sbuf_tensor("CO", [128, CONSTS_N], F32)
    STT = nc.alloc_sbuf_tensor("STATS", [128, STATS_N], F32)
    SDB = nc.alloc_sbuf_tensor("SDB", [128, 1024], F32)
    SDBF = nc.alloc_sbuf_tensor("SDBF", [128, 1024], F32)
    IDT = nc.alloc_sbuf_tensor("IDT", [128, 128], F32)

    pAB = [nc.alloc_psum_tensor("p%d" % k, [128, 512], F32) for k in range(4)]
    pT = nc.alloc_psum_tensor("pT", [128, 1024], F32)
    pT2 = nc.alloc_psum_tensor("pT2", [128, 1024], F32)

    P = Plan(nc)
    for s in ("LD", "SCL", "XA0", "XA1", "XA2", "XA3",
              "GW0", "GW1", "GW2",
              "WS0", "WS1", "WS2", "WS3", "WDS",
              "HS0", "HS1", "HS2", "HS3", "ACS", "ACS2",
              "HR0", "HR1", "HR2", "HR3", "PS0", "PS1", "CCS", "STO",
              "TEN", "VEC", "SCA", "GPS", "CCD"):
        P.sem(s)

    def co(c):
        return CO[:, c:c + 1]

    def cob(c, n):
        return bass.AP(CO, c, [[CONSTS_N, 128], [0, n]])

    def stc(c):
        return STT[:, c:c + 1]

    def pool(a, n):
        return POOL[:, a:a + n]

    def poolb(off_f32, dims):
        return bass.AP(POOL, off_f32, [[POOL_N, 128]] + dims).bitcast(BF16)

    def pool3(off, dims):
        return bass.AP(POOL, off, [[POOL_N, 128]] + dims)

    def vop(fn, inc=None, drain=True):
        return P.op("vector", fn, inc=inc, drain=drain)

    def sop(fn, inc=None, drain=True):
        return P.op("scalar", fn, inc=inc, drain=drain)

    def gop(fn, inc=None, drain=True):
        return P.op("gpsimd", fn, inc=inc, drain=drain)

    def top(fn, inc=None):
        return P.op("tensor", fn, inc=inc)

    for src, c0 in ((ba_h, C_B2A), (bx_h, C_B2X), (lam_h, C_LAM)):
        P.dma("sync", CO[:, c0:c0 + 8], bass.AP(src, 0, [[1, 128], [128, 8]]),
              "LD")
    P.dma("sync", co(C_CLIPX), bass.AP(clipx_h, 0, [[0, 128], [1, 1]]), "LD")
    P.dma("sync", co(C_CLIPD), bass.AP(clipd_h, 0, [[0, 128], [1, 1]]), "LD")
    for r in range(8):
        P.dma("sync", co(C_SEL + r), bass.AP(sel_h, r, [[0, 128], [1, 1]]), "LD")
    P.dma("sync", IDT[:], ident_h.ap(), "LD")
    ld_in = P.count("LD")

    for c, val in ((C_EPS, EPS), (C_ONE, 1.0), (C_HALF, 0.5), (C_ZERO, 0.0),
                   (C_MONE, -1.0), (C_MHALF, -0.5), (C_MLOG8H, -LOG8 / 2),
                   (C_TMP, 1.0)):
        P.op("vector", lambda e, c=c, v=val: e.memset(co(c), v))
    vop(lambda e: e.tensor_scalar(co(C_TMP2), co(C_TMP), 2.0**-30, None, ALU.add))
    vop(lambda e: e.tensor_scalar(co(C_FLAG), co(C_TMP2), -1.0, 2.0**30,
                                  ALU.add, ALU.mult))
    vop(lambda e: e.tensor_scalar(co(C_A), co(C_FLAG), 1.5 * 2.0**52,
                                  1.5 * 2.0**23, ALU.mult, ALU.add))
    vop(lambda e: e.tensor_scalar(co(C_NEGA), co(C_A), -1.0, None, ALU.mult))
    vop(lambda e: e.memset(co(CSCL + 7), 0.0))
    P.wait("vector", "LD", ld_in)
    vop(lambda e: e.tensor_scalar(CO[:, C_B2A:C_B2A + 8], CO[:, C_B2A:C_B2A + 8],
                                  0.5, None, ALU.mult))
    vop(lambda e: e.tensor_scalar(CO[:, C_B2X:C_B2X + 8], CO[:, C_B2X:C_B2X + 8],
                                  0.5, None, ALU.mult))
    v_c0 = P.drain_inc("vector", "VEC")

    P.wait("scalar", "LD", ld_in)
    P.wait("scalar", "VEC", v_c0)
    sop(lambda e: e.activation(co(C_TMP), co(C_CLIPX), AF.Tanh,
                               bias=0.0, scale=co(C_HALF)))
    sop(lambda e: e.activation(co(C_TMP2), co(C_CLIPD), AF.Tanh,
                               bias=0.0, scale=co(C_HALF)))
    sop(lambda e: e.activation(CO[:, C_G8:C_G8 + 8], CO[:, C_LAM:C_LAM + 8],
                               AF.Tanh, bias=0.0, scale=co(C_HALF)))
    s_sig = P.drain_inc("scalar", "SCA")
    P.wait("vector", "SCA", s_sig)
    vop(lambda e: e.tensor_scalar(co(C_SIGXO), co(C_TMP), 0.5 / 127.0,
                                  0.5 / 127.0, ALU.mult, ALU.add))
    vop(lambda e: e.tensor_scalar(co(C_SIGDO), co(C_TMP2), 0.5 / 127.0,
                                  0.5 / 127.0, ALU.mult, ALU.add))
    vop(lambda e: e.tensor_scalar(CO[:, C_G8:C_G8 + 8], CO[:, C_G8:C_G8 + 8],
                                  0.5, 0.5, ALU.mult, ALU.add))
    v_sig = P.drain_inc("vector", "VEC")
    P.wait("scalar", "VEC", v_sig)
    sop(lambda e: e.activation(CO[:, C_LNA:C_LNA + 8], CO[:, C_G8:C_G8 + 8],
                               AF.Ln, bias=0.0, scale=1.0))
    s_lna = P.drain_inc("scalar", "SCA")
    P.wait("vector", "SCA", s_lna)
    vop(lambda e: e.tensor_scalar(CO[:, C_LNA:C_LNA + 8], CO[:, C_LNA:C_LNA + 8],
                                  -LOG8 / 2, None, ALU.add))
    P.drain_inc("vector", "VEC")

    state = {"tv": [0, 0], "sv_s": [0, 0], "sv_v": [0, 0], "ps": [0, 0],
             "sw": [0, 0, 0, 0], "first_load_gate": [None], "gbi": 0}
    PRM, PRR, PRI = 168, 172, 176

    def prep_quant(w_hh, img, c_scale, ntiles, width, kind, batch):
        nbat = (ntiles + batch - 1) // batch
        for bi in range(nbat):
            it0 = bi * batch
            nb = min(batch, ntiles - it0)
            w = nb * width
            wslot = bi % 4
            slot = bi % 2
            gw = "GW%d" % state["gbi"]
            state["gbi"] += 1
            P.wait("gpsimd", "SCA", state["sw"][wslot])
            if state["first_load_gate"][0] is not None:
                P.wait("gpsimd", "VEC", state["first_load_gate"][0])
                state["first_load_gate"][0] = None
            P.dma("gpsimd", pool3(P_PW[wslot], [[width, nb], [1, width]]),
                  bass.AP(w_hh, it0 * 128 * width,
                          [[width, 128], [128 * width, nb], [1, width]]),
                  gw)
            P.wait("vector", gw, 16)
            win3 = pool3(P_PW[wslot], [[width, nb], [1, width]])
            vop(lambda e, a=win3, nb=nb: e.tensor_reduce(
                CO[:, PRM:PRM + nb], a, AX.X, ALU.max,
                apply_absolute_value=True))
            vop(lambda e, nb=nb: e.reciprocal(CO[:, PRR:PRR + nb],
                                              CO[:, PRM:PRM + nb]))
            vop(lambda e, nb=nb: e.tensor_scalar(
                CO[:, PRI:PRI + nb], CO[:, PRR:PRR + nb], 127.0, None,
                ALU.mult))
            vop(lambda e, c=c_scale + it0, nb=nb: e.tensor_scalar(
                CO[:, c:c + nb], CO[:, PRM:PRM + nb], 1.0 / 127.0, None,
                ALU.mult))
            v_inv = P.drain_inc("vector", "VEC")
            P.wait("scalar", "VEC", v_inv)
            P.wait("scalar", "TEN", state["tv"][slot])
            for t in range(nb):
                state["sw"][wslot] = sop(
                    lambda e, t=t, ws=wslot, s=slot, width=width: e.activation(
                        pool(P_PQ[s] + t * width, width),
                        pool(P_PW[ws] + t * width, width),
                        AF.Identity, bias=co(C_A), scale=co(PRI + t)),
                    inc="SCA")
            P.wait("tensor", "SCA", state["sw"][wslot])
            nblk = w // 128
            nhalf = (nblk + 7) // 8
            tv = 0
            sv_s = sv_v = 0
            sv_prev = {}
            for half in range(nhalf):
                par2 = half % 2
                pp = pT if par2 == 0 else pT2
                nbk = min(8, nblk - half * 8)
                if half >= 2:
                    P.wait("tensor", sv_prev[half - 2][0], sv_prev[half - 2][1])
                elif half == 0:
                    P.wait("tensor", "SCA", state["sv_s"][1 - slot])
                elif state["sv_v"][1 - slot]:
                    P.wait("tensor", "VEC", state["sv_v"][1 - slot])
                for j in range(nbk):
                    b = half * 8 + j
                    tv = top(lambda e, pp=pp, j=j, b=b, s=slot: e.transpose(
                        pp[:, j * 128:(j + 1) * 128],
                        pool(P_PQ[s] + b * 128, 128), IDT[:]), inc="TEN")
                if par2 == 0:
                    P.wait("scalar", "TEN", tv)
                    P.wait("scalar", "PS%d" % slot, state["ps"][slot])
                    sv_s = sop(lambda e, pp=pp, half=half, nbk=nbk, s=slot:
                               e.activation(
                                   poolb(P_PS[s] + half * 512, [[1, nbk * 64]]),
                                   pp[:, 0:nbk * 128], AF.Identity,
                                   bias=co(C_NEGA), scale=1.0), inc="SCA")
                    sv_prev[half] = ("SCA", sv_s)
                else:
                    P.wait("vector", "TEN", tv)
                    P.wait("vector", "PS%d" % slot, state["ps"][slot])
                    sv_v = vop(lambda e, pp=pp, half=half, nbk=nbk, s=slot:
                               e.tensor_scalar(
                                   poolb(P_PS[s] + half * 512, [[1, nbk * 64]]),
                                   pp[:, 0:nbk * 128], co(C_NEGA), None,
                                   ALU.add), inc="VEC")
                    sv_prev[half] = ("VEC", sv_v)
            state["tv"][slot] = tv
            state["sv_s"][slot] = sv_s
            state["sv_v"][slot] = sv_v
            P.wait("scalar", "SCA", sv_s)
            if sv_v:
                P.wait("scalar", "VEC", sv_v)
            if kind == "gu":
                P.dma("scalar",
                      bass.AP(img, it0 * 131072,
                              [[1024, 128], [131072, nb], [1, 1024]]),
                      poolb(P_PS[slot], [[512, nb], [1, 512]]), "PS%d" % slot)
            else:
                P.wait("vector", "PS%d" % slot, state["ps"][slot])
                mz = vop(lambda e, s=slot: e.memset(
                    pool(P_PS[s] + DI // 2, 1536 - DI // 2), 0.0), inc="VEC")
                P.wait("scalar", "VEC", mz)
                P.dma("scalar",
                      bass.AP(img, 0, [[3072, 128], [1, 3072]]),
                      poolb(P_PS[slot], [[1, 1536]]), "PS%d" % slot)
            state["ps"][slot] = P.count("PS%d" % slot)

    V_SDB = [0]
    PREP_END = {"VEC": 0, "SCA": 0}
    CCD_AG1 = [0]
    CCD_AG2 = [0]

    PREP_N = {}

    XV = {}

    def prep_compute():
        for g in range(8):
            XV[g] = P.dma(
                "sync" if g < 4 else "scalar",
                bass.AP(XR, g * 2048,
                        [[NTT * 1024, 128], [1024, 2], [1, 1024]]),
                bass.AP(x_h, g * 262144,
                        [[1024, 128], [131072, 2], [1, 1024]]),
                "XA%d" % (g // 2))
        state["first_load_gate"][0] = P.count("VEC")
        prep_quant(gus_h, ag1_in, CSCL, 6, 1024, "gu", 4)
        PREP_N["ps0_gu"] = P.count("PS0")
        PREP_N["ps1_gu"] = P.count("PS1")
        prep_quant(wds_h, ag2_in, CSCL + 6, 1, DI, "wd1", 1)
        PREP_N["ps0_wd"] = P.count("PS0")
        PREP_N["ps1_wd"] = P.count("PS1")
        PREP_N["vec"] = P.count("VEC")
        PREP_N["sca"] = P.count("SCA")
        PREP_END["VEC"] = P.count("VEC")
        PREP_END["SCA"] = P.count("SCA")

    def prep_triggers():
        P.wait("sync", "VEC", P.count("VEC"))
        P.wait("sync", "SCA", PREP_N["sca"])
        sv = P.dma("sync", bass.AP(scl_in, 0, [[1, 128], [128, 8]]),
                   CO[:, CSCL:CSCL + 8], "CCS")
        P.wait("gpsimd", "CCS", sv)
        ccd_scl = P.op("gpsimd", lambda e: e.collective_compute(
            "AllGather", ALU.bypass, ins=[scl_in.ap()], outs=[scl_out.ap()],
            replica_groups=RG), inc="CCD")
        P.wait("gpsimd", "PS0", PREP_N["ps0_gu"])
        P.wait("gpsimd", "PS1", PREP_N["ps1_gu"])
        CCD_AG1[0] = P.op("gpsimd", lambda e: e.collective_compute(
            "AllGather", ALU.bypass, ins=[ag1_in.ap()], outs=[ag1_out.ap()],
            replica_groups=RG), inc="CCD")
        P.wait("gpsimd", "PS0", PREP_N["ps0_wd"])
        P.wait("gpsimd", "PS1", PREP_N["ps1_wd"])
        CCD_AG2[0] = P.op("gpsimd", lambda e: e.collective_compute(
            "AllGather", ALU.bypass, ins=[ag2_in.ap()], outs=[ag2_out.ap()],
            replica_groups=RG), inc="CCD")
        P.wait("sync", "CCD", ccd_scl)
        for dst, off, n in ((C_SG + 0, 0, 6), (C_SG + 6, 1024, 6),
                            (C_SG + 12, 2048, 6), (C_SG + 18, 3072, 4),
                            (C_SU + 0, 3584, 2), (C_SU + 2, 4096, 6),
                            (C_SU + 8, 5120, 6), (C_SU + 14, 6144, 6),
                            (C_SU + 20, 7168, 2)):
            P.dma("sync", CO[:, dst:dst + n],
                  bass.AP(scl_out, off, [[1, 128], [128, n]]), "SCL")
        V_SDB[0] = P.dma("sync", SDB[:],
                         bass.AP(scl_out, 768,
                                 [[0, 128], [1024, 8], [1, 128]]), "SCL")
        P.dma("sync", CO[:, CGM8:CGM8 + 8],
              bass.AP(scl_out, 896, [[0, 128], [1024, 8]]), "SCL")
        V_SDB[0] = P.count("SCL")

    PREP_AX_END = {"PS0": 0, "PS1": 0}
    AXW = [20480, 21504]

    AXS = {"vv": [0, 0], "ps": [0, 0]}

    def prep_ax_unit(u):
        w_hh, img = (wa_h, wahl_img) if u < NDG else (wx_h, wxhl_img)
        dt = u % NDG
        slot = u % 2
        if u == 0:
            P.wait("scalar", "TEN", P.count("TEN"))
            AXS["ps"] = [P.count("PS0"), P.count("PS1")]
        P.wait("scalar", "VEC", AXS["vv"][slot])
        P.dma("scalar", pool(AXW[slot], 1024),
              bass.AP(w_hh, dt * 128 * 1024, [[1024, 128], [1, 1024]]),
              "WS%d" % (2 + slot))
        wv = P.count("WS%d" % (2 + slot))
        P.wait("vector", "WS%d" % (2 + slot), wv)
        win = pool(AXW[slot], 1024)
        hic = poolb(P_PS[slot], [[1, 512]])
        loc = poolb(P_PS[slot] + 512, [[1, 512]])
        P.wait("vector", "PS%d" % slot, AXS["ps"][slot])
        vop(lambda e, a=win, h=hic: e.tensor_copy(h, a))
        vop(lambda e, a=win, h=hic, l=loc: e.tensor_tensor(
            l, a, h, ALU.subtract))
        vv = P.drain_inc("vector", "VEC")
        AXS["vv"][slot] = vv
        P.wait("scalar", "VEC", vv)
        for off, src0 in ((0, P_PS[slot]), (1024, P_PS[slot] + 512)):
            P.dma("scalar",
                  bass.AP(img, off + dt * 128,
                          [[2048, 128], [128 * 2048, NDG], [1, 128]]),
                  poolb(src0, [[64, NDG], [1, 64]]), "PS%d" % slot)
        AXS["ps"][slot] = P.count("PS%d" % slot)
        if u == 2 * NDG - 1:
            PREP_AX_END["PS0"] = P.count("PS0")
            PREP_AX_END["PS1"] = P.count("PS1")

    def ln_stats(xwaits=None):
        for tt in range(NTT):
            if xwaits is not None and tt % 2 == 0:
                P.wait("vector", xwaits[tt // 2][0], xwaits[tt // 2][1])
            xs = XR[:, tt * 1024:tt * 1024 + 512]
            xs2 = XR[:, tt * 1024 + 512:tt * 1024 + 1024]
            vop(lambda e, a=xs: e.bn_stats(STT[:, S_BN:S_BN + 6], a))
            P.op("vector", lambda e, a=xs2: e.bn_stats(
                STT[:, S_BN + 6:S_BN + 12], a))
            vop(lambda e, tt=tt: e.bn_aggr(
                STT[:, S_MV + 2 * tt:S_MV + 2 * tt + 2],
                STT[:, S_BN:S_BN + 12]))
        v_st = P.drain_inc("vector", "VEC")
        P.wait("scalar", "VEC", v_st)
        var_ap = bass.AP(STT, S_MV + 1, [[STATS_N, 128], [2, NTT]])
        sop(lambda e: e.activation(STT[:, S_RSTD:S_RSTD + NTT], var_ap, AF.Ln,
                                   bias=co(C_EPS), scale=1.0))
        sop(lambda e: e.activation(STT[:, S_RSTD:S_RSTD + NTT],
                                   STT[:, S_RSTD:S_RSTD + NTT], AF.Exp,
                                   bias=0.0, scale=co(C_MHALF)))
        return P.drain_inc("scalar", "SCA")

    def ln_stats_quarter(q4, absmax=True):
        for tl in range(4):
            tt = q4 * 4 + tl
            xs = XR[:, tt * 1024:tt * 1024 + 512]
            xs2 = XR[:, tt * 1024 + 512:tt * 1024 + 1024]
            vop(lambda e, a=xs: e.bn_stats(STT[:, S_BN:S_BN + 6], a))
            P.op("vector", lambda e, a=xs2: e.bn_stats(
                STT[:, S_BN + 6:S_BN + 12], a))
            vop(lambda e, tt=tt: e.bn_aggr(
                STT[:, S_MV + 2 * tt:S_MV + 2 * tt + 2],
                STT[:, S_BN:S_BN + 12]))
        v_st = P.drain_inc("vector", "VEC")
        P.wait("scalar", "VEC", v_st)
        var_ap = bass.AP(STT, S_MV + 8 * q4 + 1, [[STATS_N, 128], [2, 4]])
        sop(lambda e, q4=q4, va=var_ap: e.activation(
            STT[:, S_RSTD + 4 * q4:S_RSTD + 4 * q4 + 4], va, AF.Ln,
            bias=co(C_EPS), scale=1.0))
        s_r = sop(lambda e, q4=q4: e.activation(
            STT[:, S_RSTD + 4 * q4:S_RSTD + 4 * q4 + 4],
            STT[:, S_RSTD + 4 * q4:S_RSTD + 4 * q4 + 4], AF.Exp,
            bias=0.0, scale=co(C_MHALF)), inc="SCA")
        if not absmax:
            return s_r
        P.wait("vector", "SCA", s_r)
        for tl in range(4):
            tt = q4 * 4 + tl
            vop(lambda e, tt=tt, s=tl % 2: e.tensor_scalar(
                pool(P_UT2[s], 1024), XR[:, tt * 1024:(tt + 1) * 1024],
                stc(S_MV + 2 * tt), stc(S_RSTD + tt),
                ALU.subtract, ALU.mult))
            vop(lambda e, tt=tt, s=tl % 2: e.tensor_reduce(
                stc(S_GM + tt), pool(P_UT2[s], 1024), AX.X, ALU.max,
                apply_absolute_value=True))
        return s_r

    def allreduce_max(src_col, dst_col):
        tv = top(lambda e, s=src_col: e.transpose(pT[0:1, 0:128], s, IDT[:]),
                 inc="TEN")
        P.wait("vector", "TEN", tv)
        vop(lambda e: e.tensor_reduce(CO[0:1, C_TMP:C_TMP + 1],
                                      pT[0:1, 0:128], AX.X, ALU.max))
        v = P.drain_inc("vector", "VEC")
        P.wait("sync", "VEC", v)
        P.wait("sync", "CCD", P.count("CCD"))
        cv = P.dma("sync", cc_in.ap(), CO[0:1, C_TMP:C_TMP + 1], "CCS")
        P.wait("gpsimd", "CCS", cv)
        cc = P.op("gpsimd", lambda e: e.collective_compute(
            "AllReduce", ALU.max, ins=[cc_in.ap()], outs=[cc_out.ap()],
            replica_groups=RG), inc="CCD")
        P.wait("sync", "CCD", cc)
        gv = P.dma("sync", co(dst_col), bass.AP(cc_out, 0, [[0, 128], [1, 1]]),
                   "LD")
        return gv, tv

    def ffn_pass(fi, overlap_fn=None, pre_fn=None, pre=None):
        import os as _os
        _stop = _os.environ.get("KSTOP", "")
        if pre is not None:
            s_rstd = pre["s_rstd"]
            gv = pre["gv"]
            gv_dist = None
            P.wait("vector", "LD", gv)
            P.wait("vector", "SCA", s_rstd)
            vop(lambda e: e.tensor_scalar(co(C_SX), co(C_GMAX), co(C_SIGXO),
                                          None, ALU.mult))
            return ffn_tail(fi, _stop, s_rstd, gv_dist, overlap_fn)
        xw = None
        if pre_fn is not None:
            pre_fn()
            xw = [("XA%d" % (g // 2), XV[g | 1]) for g in range(8)]
        s_rstd = ln_stats(xw)
        P.wait("vector", "SCA", s_rstd)
        for tt in range(NTT):
            slot = tt % 2
            vop(lambda e, tt=tt, s=slot: e.tensor_scalar(
                pool(P_UT[s], 1024), XR[:, tt * 1024:(tt + 1) * 1024],
                stc(S_MV + 2 * tt), stc(S_RSTD + tt),
                ALU.subtract, ALU.mult))
            vop(lambda e, tt=tt, s=slot: e.tensor_reduce(
                stc(S_GM + tt), pool(P_UT[s], 1024), AX.X, ALU.max,
                apply_absolute_value=True))
        vop(lambda e: e.tensor_reduce(co(C_GMAX), STT[:, S_GM:S_GM + NTT],
                                      AX.X, ALU.max))
        v_gm = P.drain_inc("vector", "VEC")
        P.wait("tensor", "VEC", v_gm)
        gv_dist = None
        if overlap_fn is not None:
            tv = top(lambda e: e.transpose(pT[0:1, 0:128], co(C_GMAX),
                                           IDT[:]), inc="TEN")
            P.wait("vector", "TEN", tv)
            vop(lambda e: e.tensor_reduce(CO[0:1, CSCL + 7:CSCL + 8],
                                          pT[0:1, 0:128], AX.X, ALU.max))
            P.drain_inc("vector", "VEC")
            overlap_fn()
            gv_dist = P.count("SCL")
            P.wait("vector", "SCL", gv_dist)
            vop(lambda e: e.tensor_reduce(co(C_GMAX), CO[:, CGM8:CGM8 + 8],
                                          AX.X, ALU.max))
        else:
            gv, _ = allreduce_max(co(C_GMAX), C_GMAX)
            P.wait("vector", "LD", gv)
        vop(lambda e: e.tensor_scalar(co(C_SX), co(C_GMAX), co(C_SIGXO), None,
                                      ALU.mult))
        return ffn_tail(fi, _stop, s_rstd, gv_dist, overlap_fn)

    def ffn_tail(fi, _stop, s_rstd, gv_dist, overlap_fn):
        vop(lambda e: e.reciprocal(co(C_INVS), co(C_SX)))
        vop(lambda e: e.tensor_scalar(STT[:, S_RSTDI:S_RSTDI + NTT],
                                      STT[:, S_RSTD:S_RSTD + NTT],
                                      co(C_INVS), None, ALU.mult))
        P.drain_inc("vector", "VEC")

        if _stop == "ffnAmax":
            for it in range(NIT):
                P.op("vector", lambda e, it=it: e.tensor_scalar(
                    co(C_SGX + it), co(C_SG + it), co(C_SX), None, ALU.mult))
                P.op("vector", lambda e, it=it: e.tensor_scalar(
                    co(C_SUX + it), co(C_SU + it), co(C_SX), None, ALU.mult))
            P.drain_inc("vector", "VEC")
            return
        t_xp = [0] * NTT
        s_ev = [0] * NTT
        g_cl = [0] * NTT
        P.wait("vector", "TEN", P.count("TEN"))
        P.wait("scalar", "WS0", P.count("WS0"))
        P.wait("scalar", "WS1", P.count("WS1"))
        for tt in range(NTT):
            slot = tt % 2
            vop(lambda e, tt=tt, s=slot: e.tensor_scalar(
                pool(P_UT[s], 1024), XR[:, tt * 1024:(tt + 1) * 1024],
                stc(S_MV + 2 * tt), stc(S_RSTDI + tt),
                ALU.subtract, ALU.mult))
            vop(lambda e, s=slot: e.tensor_scalar(
                pool(P_QTS[s], 1024), pool(P_UT[s], 1024),
                co(C_A), co(C_A), ALU.add, ALU.subtract))
            if tt >= 2:
                P.wait("vector", "TEN", t_xp[tt - 2])
            vop(lambda e, s=slot: e.tensor_scalar(
                pool(P_QCS[s], 1024), pool(P_QTS[s], 1024),
                -128.0, 127.0, ALU.max, ALU.min))
            g_cl[tt] = P.drain_inc("vector", "VEC")
            pp = pT if slot == 0 else pT2
            P.wait("tensor", "VEC", g_cl[tt])
            if tt >= 2:
                P.wait("tensor", "SCA", s_ev[tt - 2])
            for j in range(8):
                t_xp[tt] = top(lambda e, pp=pp, j=j, s=slot: e.transpose(
                    pp[:, j * 128:(j + 1) * 128],
                    pool(P_QCS[s] + j * 128, 128), IDT[:]),
                    inc="TEN" if j == 7 else None)
            P.wait("scalar", "TEN", t_xp[tt])
            s_ev[tt] = sop(lambda e, pp=pp, tt=tt: e.activation(
                poolb(P_XQT + tt * 64, [[1024, 8], [1, 64]]),
                bass.AP(pp, 0, [[1024, 128], [128, 8], [1, 128]]),
                AF.Identity, bias=0.0, scale=1.0), inc="SCA")

        if gv_dist is not None:
            P.wait("vector", "SCL", gv_dist)
        for it in range(NIT):
            P.op("vector", lambda e, it=it: e.tensor_scalar(
                co(C_SGX + it), co(C_SG + it), co(C_SX), None, ALU.mult))
            P.op("vector", lambda e, it=it: e.tensor_scalar(
                co(C_SUX + it), co(C_SU + it), co(C_SX), None, ALU.mult))
        v_scales = P.drain_inc("vector", "VEC")
        if _stop == "ffnB":
            return
        vop(lambda e: e.memset(stc(S_HM), 0.0))
        P.wait("sync", "VEC", PREP_END["VEC"])
        P.wait("sync", "SCA", PREP_END["SCA"])
        P.wait("sync", "CCD", CCD_AG1[0])
        t_xp_last = t_xp[NTT - 1]
        v_bq_done = P.count("VEC")
        w_v = [0] * NIT
        for it in (0, 1):
            slot = it % 2
            P.dma("sync", poolb(P_WRG[slot], [[1, 512]]),
                  bass.AP(ag1_out, it * 131072, [[1024, 128], [1, 1024]]),
                  "WS%d" % slot)
            w_v[it] = P.dma("sync", poolb(P_WRU[slot], [[1, 512]]),
                            bass.AP(ag1_out, (NIT + it) * 131072,
                                    [[1024, 128], [1, 1024]]), "WS%d" % slot)
        NK = NIT * 4
        t_g = [0] * NK
        t_u = [0] * NK
        s_si = [0] * NK
        v_h = [0] * NK
        g_h = [0] * NK
        hsp = [0] * NK
        t_last_it = [0] * NIT
        P.wait("scalar", "VEC", v_scales)
        for it in range(NIT):
            slot = it % 2
            if overlap_fn is not None and 2 <= it < 2 + 2 * NDG:
                prep_ax_unit(it - 2)
            if it >= 2:
                P.wait("sync", "TEN", t_last_it[it - 2])
                P.dma("sync", poolb(P_WRG[slot], [[1, 512]]),
                      bass.AP(ag1_out, it * 131072, [[1024, 128], [1, 1024]]),
                      "WS%d" % slot)
                w_v[it] = P.dma("sync", poolb(P_WRU[slot], [[1, 512]]),
                                bass.AP(ag1_out, (NIT + it) * 131072,
                                        [[1024, 128], [1, 1024]]),
                                "WS%d" % slot)
            for tc in range(4):
                k = it * 4 + tc
                pg = pAB[2 * (k % 2)]
                pu = pAB[2 * (k % 2) + 1]
                P.wait("tensor", "WS%d" % slot, w_v[it])
                if it == 0:
                    P.wait("tensor", "SCA", s_ev[4 * tc + 3])
                if k >= 2:
                    P.wait("tensor", "SCA", s_si[k - 2])
                    P.wait("tensor", "VEC", v_h[k - 2])
                for g in range(NDG):
                    t_g[k] = top(lambda e, g=g, s=slot, tc=tc, pg=pg: e.matmul(
                        pg[:], poolb(P_WRG[s] + g * 64, [[1, 64]]),
                        poolb(P_XQT + g * 1024 + tc * 256, [[1, 256]]),
                        start=(g == 0), stop=(g == NDG - 1)),
                        inc="TEN" if g == NDG - 1 else None)
                for g in range(NDG):
                    t_u[k] = top(lambda e, g=g, s=slot, tc=tc, pu=pu: e.matmul(
                        pu[:], poolb(P_WRU[s] + g * 64, [[1, 64]]),
                        poolb(P_XQT + g * 1024 + tc * 256, [[1, 256]]),
                        start=(g == 0), stop=(g == NDG - 1)),
                        inc="TEN" if g == NDG - 1 else None)
                if tc == 3:
                    t_last_it[it] = t_u[k]
                P.wait("scalar", "TEN", t_g[k])
                if k >= 4:
                    P.wait("scalar", "VEC", v_h[k - 4])
                s_si[k] = sop(lambda e, k=k, it=it, pg=pg: e.activation(
                    pool(P_SIL + (k % 4) * 512, 512), pg[:], AF.Silu,
                    bias=0.0, scale=co(C_SGX + it)), inc="SCA")
                P.wait("vector", "TEN", t_u[k])
                P.wait("vector", "SCA", s_si[k])
                if k >= 4:
                    P.wait("vector", "HS%d" % (k % 4), hsp[k - 4])
                v_h[k] = vop(lambda e, k=k, it=it, pu=pu: e.scalar_tensor_tensor(
                    pool(P_HT + (k % 4) * 512, 512), pu[:], co(C_SUX + it),
                    pool(P_SIL + (k % 4) * 512, 512), ALU.mult, ALU.mult),
                    inc="VEC")
                vop(lambda e, k=k: e.tensor_reduce(
                    stc(S_T1), pool(P_HT + (k % 4) * 512, 512), AX.X, ALU.max,
                    apply_absolute_value=True))
                vop(lambda e: e.tensor_tensor(stc(S_HM), stc(S_HM), stc(S_T1),
                                              ALU.max))
                g_h[k] = P.drain_inc("vector", "VEC")
                P.wait("sync", "VEC", v_h[k])
                hsp[k] = P.dma("sync",
                               bass.AP(h_spill, (tc * NIT + it) * 65536,
                                       [[512, 128], [1, 512]]),
                               pool(P_HT + (k % 4) * 512, 512), "HS%d" % (k % 4))

        if _stop == "ffnC":
            return
        for q in ("sync", "scalar"):
            P.wait(q, "TEN", t_xp_last)
            P.wait(q, "VEC", v_bq_done)
            P.wait(q, "PS0", P.count("PS0"))
            P.wait(q, "PS1", P.count("PS1"))
            P.wait(q, "CCD", CCD_AG2[0])
        wdt_v = 0
        for t8 in range(8):
            wdt_v = P.dma("sync" if t8 % 2 == 0 else "scalar",
                          poolb(P_WDT + t8 * 1408, [[1, 1408]]),
                          bass.AP(ag2_out, 3 * t8 * 131072,
                                  [[3072, 128], [1, 2816]]), "WDS")
        P.wait("tensor", "VEC", g_h[NK - 1])
        gv2, _ = allreduce_max(stc(S_HM), C_GMAX)
        P.wait("vector", "LD", gv2)
        P.wait("vector", "SCL", V_SDB[0])
        vop(lambda e: e.tensor_scalar(co(C_SX), co(C_GMAX), co(C_SIGDO), None,
                                      ALU.mult))
        vop(lambda e: e.reciprocal(co(C_INVS), co(C_SX)))
        vop(lambda e: e.tensor_scalar(co(C_HALFSH), co(C_SX), 0.5, None,
                                      ALU.mult))
        vop(lambda e: e.tensor_scalar(SDBF[:], SDB[:], co(C_HALFSH), None,
                                      ALU.mult))
        v_sdbf = P.drain_inc("vector", "VEC")
        P.wait("sync", "TEN", t_last_it[NIT - 1])
        P.wait("sync", "VEC", g_h[NK - 1])
        P.wait("sync", "SCA", s_si[NK - 1])
        hrv = [[0, 0] for _ in range(4)]
        g_hq = [[0] * NIT for _ in range(4)]
        t_dm = [0] * NTT
        v_ep = [0] * NTT
        first_q = True
        prev_vdq = None
        for s4 in range(4):
            P.wait("sync", "HS%d" % s4, P.count("HS%d" % s4))
        dps = [pT[:, 0:512], pT[:, 512:1024], pT2[:, 0:512], pT2[:, 512:1024],
               pAB[0][:], pAB[1][:], pAB[2][:], pAB[3][:]]
        t_mm = [[0] * NIT for _ in range(4)]
        v_ep2 = [[0] * 4 for _ in range(4)]

        def d_readback(tc):
            for hf, q in ((0, "sync"), (1, "scalar")):
                if tc >= 1:
                    P.wait(q, "SCA", prev_sdq[hf * 11 + 10])
                elif hf == 1:
                    P.wait(q, "TEN", t_last_it[NIT - 1])
                    P.wait(q, "VEC", g_h[NK - 1])
                    for s4 in range(4):
                        P.wait(q, "HS%d" % s4, P.count("HS%d" % s4))
                hrv[tc][hf] = P.dma(
                    q, pool3(P_HRD + hf * 11 * 512, [[512, 11], [1, 512]]),
                    bass.AP(h_spill, (tc * NIT + hf * 11) * 65536,
                            [[512, 128], [65536, 11], [1, 512]]), "HR%d" % hf)

        d_readback(0)
        prev_sdq = None
        for tc in range(4):
            v_dq = [0] * NIT
            s_dq = [0] * NIT
            for ig in range(NIT):
                P.wait("scalar", "HR%d" % (ig // 11), hrv[tc][ig // 11])
                if first_q:
                    P.wait("scalar", "VEC", v_sdbf)
                    for s4 in range(4):
                        P.wait("vector", "HS%d" % s4, P.count("HS%d" % s4))
                    first_q = False
                if ig >= 2:
                    P.wait("scalar", "VEC", v_dq[ig - 2])
                elif prev_sdq is not None:
                    P.wait("scalar", "VEC", prev_vdq[NIT - 2 + ig])
                s_dq[ig] = sop(lambda e, ig=ig: e.activation(
                    pool(P_DQ[ig % 2], 512), pool(P_HRD + ig * 512, 512),
                    AF.Identity, bias=co(C_A), scale=co(C_INVS)), inc="SCA")
                P.wait("vector", "SCA", s_dq[ig])
                if tc >= 1:
                    P.wait("vector", "TEN", t_mm[tc - 1][ig])
                vop(lambda e, ig=ig: e.tensor_scalar(
                    pool(P_DQ[ig % 2], 512), pool(P_DQ[ig % 2], 512),
                    co(C_A), -128.0, ALU.subtract, ALU.max), drain=True)
                vop(lambda e, ig=ig: e.tensor_single_scalar(
                    poolb(P_HQ + ig * 256, [[1, 256]]),
                    pool(P_DQ[ig % 2], 512), 127.0, ALU.min))
                v_dq[ig] = P.drain_inc("vector", "VEC")
                g_hq[tc][ig] = v_dq[ig]
                P.wait("tensor", "VEC", v_dq[ig])
                if ig == 0:
                    if tc == 0:
                        P.wait("tensor", "WDS", wdt_v)
                    else:
                        for tq2 in range(4):
                            P.wait("tensor", "VEC", v_ep2[tc - 1][tq2])
                for tq in range(4):
                    for half in range(2):
                        ps = dps[(2 * tq + half) % 8]
                        t_mm[tc][ig] = top(
                            lambda e, ig=ig, tq=tq, half=half, ps=ps: e.matmul(
                                ps, poolb(P_HQ + ig * 256 + tq * 64, [[1, 64]]),
                                poolb(P_WDT + half * 4 * 1408 + ig * 64,
                                      [[1408, 4], [1, 64]]),
                                start=(ig == 0), stop=(ig == NIT - 1)),
                            inc="TEN" if half == 1 and tq == 3 else None)
            prev_sdq = s_dq
            prev_vdq = v_dq
            if tc < 3:
                d_readback(tc + 1)
            for tq in range(4):
                tt = tc * 4 + tq
                P.wait("vector", "TEN", t_mm[tc][NIT - 1])
                for half in range(2):
                    ps = dps[(2 * tq + half) % 8]
                    vop(lambda e, ps=ps, half=half: e.tensor_tensor(
                        pool(P_TD + half * 512, 512), ps,
                        SDBF[:, half * 512:(half + 1) * 512], ALU.mult))
                vop(lambda e, tt=tt: e.tensor_tensor(
                    XR[:, tt * 1024:(tt + 1) * 1024],
                    XR[:, tt * 1024:(tt + 1) * 1024],
                    pool(P_TD, 1024), ALU.add))
                v_ep2[tc][tq] = P.drain_inc("vector", "VEC")
            if fi == 0:
                ATTN_RSTD[0] = ln_stats_quarter(tc, absmax=False)
        return v_ep2

    ATTN_RSTD = [0]

    def attn_pass():
        s_rstd = ATTN_RSTD[0]
        if not s_rstd:
            s_rstd = ln_stats()
        P.wait("vector", "SCA", s_rstd)
        for g in range(8):
            P.op("vector", lambda e, g=g: e.memset(co(C_YL + g), 0.0))
            P.op("vector", lambda e, g=g: e.memset(co(C_AL + g), 1.0))
        v_init = P.drain_inc("vector", "VEC")
        aw_v = [0] * 8
        t_ri = [0] * 8
        ac_v = [0] * NQ
        st = {
            "v_scan": [0] * 8,
            "t_y": [0] * QTT,
            "v_xr": [0] * QTT,
            "s_at": [0] * 8,
            "s_tb": [0] * 8,
        }

        def emit_A(qq, prev_t_ut):
            par = qq % 2
            s_uhi = [0] * QTT
            v_ulo = [0] * QTT
            t_ut = [0] * QTT
            for tl in range(QTT):
                ti = qq * QTT + tl
                slot = tl % 2
                pp = pT if slot == 0 else pT2
                if tl >= 2:
                    P.wait("vector", "TEN", t_ut[tl - 2])
                elif prev_t_ut is not None:
                    P.wait("vector", "TEN", prev_t_ut[tl + 2])
                vop(lambda e, ti=ti, s=slot: e.tensor_scalar(
                    pool(P_UT2[s], 1024), XR[:, ti * 1024:(ti + 1) * 1024],
                    stc(S_MV + 2 * ti), stc(S_RSTD + ti),
                    ALU.subtract, ALU.mult))
                vu = P.drain_inc("vector", "VEC")
                P.wait("tensor", "VEC", vu)
                if tl >= 2:
                    P.wait("tensor", "SCA", s_uhi[tl - 2])
                    P.wait("tensor", "VEC", v_ulo[tl - 2])
                else:
                    P.wait("tensor", "VEC", st["v_xr"][2 + tl])
                for j in range(8):
                    t_ut[tl] = top(lambda e, pp=pp, j=j, s=slot: e.transpose(
                        pp[:, j * 128:(j + 1) * 128],
                        pool(P_UT2[s] + j * 128, 128), IDT[:]),
                        inc="TEN" if j == 7 else None)
                P.wait("scalar", "TEN", t_ut[tl])
                s_uhi[tl] = sop(lambda e, pp=pp, tl=tl, par=par: e.activation(
                    poolb(P_UHI[par] + tl * 64, [[256, 8], [1, 64]]),
                    bass.AP(pp, 0, [[1024, 128], [128, 8], [1, 128]]),
                    AF.Identity, bias=0.0, scale=1.0), inc="SCA")
                P.wait("vector", "SCA", s_uhi[tl])
                v_ulo[tl] = vop(lambda e, pp=pp, tl=tl, par=par: e.tensor_tensor(
                    poolb(P_ULO[par] + tl * 64, [[256, 8], [1, 64]]),
                    bass.AP(pp, 0, [[1024, 128], [128, 8], [1, 128]]),
                    poolb(P_UHI[par] + tl * 64, [[256, 8], [1, 64]]),
                    ALU.subtract), inc="VEC")
            return s_uhi, v_ulo, t_ut

        def aw_load(ht, q="sync", sem=None):
            slot = ht % 2
            sem = sem or "WS%d" % slot
            P.dma(q, poolb(P_AWR[slot], [[1, 1024]]),
                  bass.AP(wahl_img, ht * 262144, [[2048, 128], [1, 2048]]),
                  sem)
            v = P.dma(q, poolb(P_AWR[slot] + 1024, [[1, 1024]]),
                      bass.AP(wxhl_img, ht * 262144,
                              [[2048, 128], [1, 2048]]),
                      sem)
            aw_v[ht] = (sem, v)

        def emit_MM(qq, v_ulo, ht0, ht1, s_at, s_tb, binter=None):
            par = qq % 2
            if qq == 0 and ht0 == 0:
                P.wait("sync", "PS0", PREP_AX_END["PS0"])
                P.wait("sync", "PS1", PREP_AX_END["PS1"])
                aw_load(0)
                aw_load(1)
            for ht in range(ht0, ht1):
                slot = ht % 2
                if ht >= 2:
                    P.wait("sync", "TEN", t_ri[ht - 2])
                    aw_load(ht)
                pr = pAB[2 * (ht % 2)]
                pi = pAB[2 * (ht % 2) + 1]
                P.wait("tensor", aw_v[ht][0], aw_v[ht][1])
                if ht == 0:
                    P.wait("tensor", "VEC", v_ulo[QTT - 1])
                if ht >= 2:
                    P.wait("tensor", "SCA", s_at[ht - 2])
                    P.wait("tensor", "SCA", s_tb[ht - 2])
                elif qq > 0:
                    P.wait("tensor", "SCA", st["s_at"][ht + 6])
                    P.wait("tensor", "SCA", st["s_tb"][ht + 6])
                mmn = 0
                for woff, uoff in ((0, P_UHI[par]), (0, P_ULO[par]),
                                   (512, P_UHI[par])):
                    for g in range(NDG):
                        mmn += 1
                        t_ri[ht] = top(
                            lambda e, g=g, s=slot, wo=woff, uo=uoff, pr=pr,
                            mm=mmn: e.matmul(
                                pr[:], poolb(P_AWR[s] + wo + g * 64, [[1, 64]]),
                                poolb(uo + g * 256, [[1, 256]]),
                                start=(mm == 1), stop=(mm == 24)),
                            inc="TEN" if mmn == 24 else None)
                P.wait("scalar", "TEN", t_ri[ht])
                if qq > 0:
                    P.wait("scalar", "VEC", st["v_scan"][ht])
                    if ht == 0:
                        P.wait("scalar", "ACS", ac_v[qq - 1][0])
                    elif ht == 4:
                        P.wait("scalar", "ACS2", ac_v[qq - 1][1])
                sop(lambda e, pr=pr, ht=ht: e.activation(
                    pool(P_SC[0], 512), pr[:], AF.Tanh,
                    bias=co(C_B2A + ht), scale=co(C_HALF)))
                s_at[ht] = sop(lambda e, ht=ht: e.activation(
                    pool(P_AT + ht * 512, 512), pool(P_SC[0], 512), AF.Exp,
                    bias=co(C_LNA + ht), scale=co(C_MLOG8H)), inc="SCA")
                mmn = 0
                for woff, uoff in ((1024, P_UHI[par]), (1024, P_ULO[par]),
                                   (1536, P_UHI[par])):
                    for g in range(NDG):
                        mmn += 1
                        t_ri[ht] = top(
                            lambda e, g=g, s=slot, wo=woff, uo=uoff, pi=pi,
                            mm=mmn: e.matmul(
                                pi[:], poolb(P_AWR[s] + wo + g * 64, [[1, 64]]),
                                poolb(uo + g * 256, [[1, 256]]),
                                start=(mm == 1), stop=(mm == 24)),
                            inc="TEN" if mmn == 24 else None)
                P.wait("scalar", "TEN", t_ri[ht])
                s_tb[ht] = sop(lambda e, pi=pi, ht=ht: e.activation(
                    pool(P_BT + ht * 512, 512), pi[:], AF.Tanh,
                    bias=co(C_B2X + ht), scale=co(C_HALF)), inc="SCA")
                if binter is not None and ht >= 4:
                    emit_B_ht(qq, ht - 4, s_at, s_tb, binter[0], binter[1])
            if ht1 == 8 and binter is not None:
                for bh in range(ht1 - 4, 8):
                    emit_B_ht(qq, bh, s_at, s_tb, binter[0], binter[1])
            if ht1 == 8 and qq + 1 < NQ:
                gw = "GW%d" % qq
                P.wait("gpsimd", gw, P.count(gw))
                P.wait("gpsimd", "TEN", t_ri[6])
                aw_load(0, q="gpsimd", sem=gw)
                P.wait("gpsimd", gw, P.count(gw))
                P.wait("gpsimd", "TEN", t_ri[7])
                aw_load(1, q="gpsimd", sem=gw)
            return s_at, s_tb

        def emit_B_ht(qq, ht, s_at, s_tb, v_scan, v_b1):
            par = qq % 2
            if True:
                sc = P_SCP[ht % 2]
                P.wait("scalar", "VEC",
                       v_b1[ht - 2] if ht >= 2 else st["v_scan"][6 + (ht % 2)])
                P.wait("scalar", "SCA", s_tb[ht])
                sop(lambda e, ht=ht, sc=sc: e.activation(
                    pool(sc[0], 512), pool(P_AT + ht * 512, 512), AF.Square,
                    bias=0.0, scale=1.0))
                sop(lambda e, sc=sc: e.activation(
                    pool(sc[1], 512), pool(sc[0], 512), AF.Ln,
                    bias=co(C_ONE), scale=co(C_MONE)))
                s_s = sop(lambda e, sc=sc: e.activation(
                    pool(sc[2], 512), pool(sc[1], 512), AF.Exp,
                    bias=0.0, scale=co(C_HALF)), inc="SCA")
                P.wait("vector", "SCA", s_s)
                vop(lambda e, ht=ht, par=par: e.tensor_tensor(
                    pool(P_SC[3], 512),
                    poolb(P_UHI[par] + ht * 256, [[1, 256]]),
                    poolb(P_ULO[par] + ht * 256, [[1, 256]]), ALU.add))
                vop(lambda e, ht=ht: e.tensor_scalar(
                    pool(P_SC[4], 512), pool(P_BT + ht * 512, 512),
                    0.5, 0.5, ALU.mult, ALU.add))
                vop(lambda e, sc=sc: e.tensor_tensor(
                    pool(P_SC[4], 512), pool(P_SC[4], 512), pool(sc[2], 512),
                    ALU.mult))
                v_b1[ht] = P.drain_inc("vector", "VEC")
                vop(lambda e, ht=ht: e.tensor_tensor(
                    pool(P_BT + ht * 512, 512), pool(P_SC[4], 512),
                    pool(P_SC[3], 512), ALU.mult))
                if ht == 0:
                    P.wait("vector", "TEN", st["t_y"][QTT - 1])
                vop(lambda e, ht=ht: e.tensor_tensor_scan(
                    pool(P_Y + ht * 512, 512), pool(P_AT + ht * 512, 512),
                    pool(P_BT + ht * 512, 512), co(C_YL + ht),
                    ALU.mult, ALU.add))
                vop(lambda e, ht=ht: e.tensor_copy(
                    co(C_YL + ht), pool(P_Y + ht * 512 + 511, 1)))
                vop(lambda e, ht=ht: e.tensor_tensor_scan(
                    pool(P_BT + ht * 512, 512), pool(P_AT + ht * 512, 512),
                    cob(C_ZERO, 512), co(C_AL + ht), ALU.mult, ALU.add))
                vop(lambda e, ht=ht: e.tensor_copy(
                    co(C_AL + ht), pool(P_BT + ht * 512 + 511, 1)))
                v_scan[ht] = P.drain_inc("vector", "VEC")
            if ht == 3:
                P.wait("sync", "VEC", v_scan[3])
                a0 = P.dma("sync",
                           bass.AP(ac_spill, qq * 524288,
                                   [[4096, 128], [1, 2048]]),
                           pool(P_BT, 2048), "ACS")
                ac_v[qq] = (a0, 0)
            if ht == 7:
                P.wait("sync", "VEC", v_scan[7])
                a1 = P.dma("sync",
                           bass.AP(ac_spill, qq * 524288 + 2048,
                                   [[4096, 128], [1, 2048]]),
                           pool(P_BT + 2048, 2048), "ACS2")
                ac_v[qq] = (ac_v[qq][0], a1)

        def emit_EPI(qq, uhi_gate, v_scan_q, src_off=P_Y, v_gate=None):
            t_y = [0] * QTT
            v_xr = [0] * QTT
            for tl in range(QTT):
                ti = qq * QTT + tl
                pp = pT if tl % 2 == 0 else pT2
                if tl == 0:
                    if v_gate is not None:
                        P.wait("tensor", v_gate[0], v_gate[1])
                    else:
                        P.wait("tensor", "VEC", v_scan_q[7])
                    if uhi_gate is not None:
                        P.wait("tensor", "SCA", uhi_gate[0][QTT - 1])
                        P.wait("tensor", "VEC", uhi_gate[1][QTT - 1])
                if tl >= 2:
                    P.wait("tensor", "VEC", v_xr[tl - 2])
                for j in range(8):
                    t_y[tl] = top(lambda e, pp=pp, j=j, tl=tl, so=src_off:
                                  e.transpose(
                                      pp[:, j * 128:(j + 1) * 128],
                                      pool(so + j * 512 + tl * 128, 128),
                                      IDT[:]),
                                  inc="TEN" if j == 7 else None)
                P.wait("vector", "TEN", t_y[tl])
                v_xr[tl] = vop(lambda e, ti=ti, pp=pp: e.tensor_tensor(
                    XR[:, ti * 1024:(ti + 1) * 1024],
                    XR[:, ti * 1024:(ti + 1) * 1024], pp[:], ALU.add),
                    inc="VEC")
            return t_y, v_xr

        cur = emit_A(0, None)
        pend = None
        for qq in range(NQ):
            s_at, s_tb = emit_MM(qq, cur[1], 0, 2, [0] * 8, [0] * 8)
            if pend is not None:
                st["t_y"], st["v_xr"] = emit_EPI(
                    pend[0], (cur[0], cur[1]), pend[1])
            if qq + 1 < NQ:
                nxt = emit_A(qq + 1, cur[2])
            else:
                nxt = cur
            v_scan = [0] * 8
            v_b1 = [0] * 8
            s_at, s_tb = emit_MM(qq, cur[1], 2, 8, s_at, s_tb,
                                 binter=(v_scan, v_b1))
            st["v_scan"] = v_scan
            st["s_at"] = s_at
            st["s_tb"] = s_tb
            pend = (qq, v_scan)
            cur = nxt
        st["t_y"], st["v_xr"] = emit_EPI(pend[0], None, pend[1])
        P.wait("sync", "VEC", st["v_scan"][7])
        cv = P.dma("sync", bass.AP(ag_in, 0, [[1, 128], [128, 8]]),
                   CO[:, C_YL:C_YL + 8], "CCS")
        P.wait("gpsimd", "CCS", cv)
        agv = P.op("gpsimd", lambda e: e.collective_compute(
            "AllGather", ALU.bypass, ins=[ag_in.ap()], outs=[ag_out.ap()],
            replica_groups=RG), inc="CCD")
        rbuf = [P_BT, 0, P_AT, P_SC[0]]
        ybuf = [P_Y, P_AWR[0]]
        rb = [0] * NQ
        v_fq = [0] * NQ
        tys = [None] * NQ
        P.wait("sync", "ACS", ac_v[NQ - 1][0])
        P.wait("sync", "ACS2", ac_v[NQ - 1][1])
        P.wait("scalar", "ACS", ac_v[NQ - 1][0])
        P.wait("scalar", "ACS2", ac_v[NQ - 1][1])
        for qq in range(NQ):
            rb[qq] = P.dma("sync" if qq % 2 == 0 else "scalar",
                           pool(rbuf[qq], 4096),
                           bass.AP(ac_spill, qq * 524288,
                                   [[4096, 128], [1, 4096]]),
                           "HR%d" % qq)
        P.wait("sync", "CCD", agv)
        for r in range(8):
            g8v = P.dma("sync", CO[:, C_G8 + r * 8:C_G8 + r * 8 + 8],
                        bass.AP(ag_out, r * 1024, [[1, 128], [128, 8]]), "LD")
        P.wait("vector", "LD", g8v)
        vop(lambda e: e.memset(CO[:, C_CC:C_CC + 8], 0.0))
        for r in range(8):
            vop(lambda e, r=r: e.scalar_tensor_tensor(
                CO[:, C_CC:C_CC + 8],
                CO[:, C_G8 + r * 8:C_G8 + r * 8 + 8],
                co(C_SEL + r), CO[:, C_CC:C_CC + 8], ALU.mult, ALU.add))
        v_cc = P.drain_inc("vector", "VEC")
        for qq in range(NQ):
            par = qq % 2
            P.wait("scalar", "HR%d" % qq, rb[qq])
            P.wait("scalar", "VEC", v_cc)
            if qq == 0:
                P.wait("scalar", "TEN", st["t_y"][QTT - 1])
            elif qq >= 2:
                P.wait("scalar", "TEN", tys[qq - 2][QTT - 1])
            for ht in range(8):
                P.op("scalar", lambda e, ht=ht, qq=qq, par=par:
                     e.activation(
                         pool(ybuf[par] + ht * 512, 512),
                         pool(rbuf[qq] + ht * 512, 512),
                         AF.Identity, bias=0.0, scale=co(C_CC + ht)))
            v_fq[qq] = P.drain_inc("scalar", "SCA")
            tys[qq], st["v_xr"] = emit_EPI(qq, None, None,
                                           src_off=ybuf[par],
                                           v_gate=("SCA", v_fq[qq]))
            s_r2 = ln_stats_quarter(qq)
        st["t_y"] = tys[NQ - 1]
        vop(lambda e: e.tensor_reduce(co(C_GMAX), STT[:, S_GM:S_GM + NTT],
                                      AX.X, ALU.max))
        v_gm2 = P.drain_inc("vector", "VEC")
        P.wait("tensor", "VEC", v_gm2)
        gv2p, _ = allreduce_max(co(C_GMAX), C_GMAX)
        return {"s_rstd": s_r2, "gv": gv2p}

    import os
    stop = os.environ.get("KSTOP", "")
    P.barrier()
    if stop != "ln":
        if stop == "ffnA":
            ln_stats()
        else:
            ffn_pass(0, overlap_fn=prep_triggers, pre_fn=prep_compute)
            P.barrier()
            if stop not in ("ffn0", "ffnAmax", "ffnB", "ffnC", "ffnD"):
                attn_pre = attn_pass()
                P.barrier()
                if stop != "attn":
                    v_ep2f = ffn_pass(1, pre=attn_pre)
                    for tc in range(4):
                        q = "sync" if tc % 2 == 0 else "scalar"
                        P.wait(q, "VEC", v_ep2f[tc][3])
                        P.dma(q,
                              bass.AP(y_h, tc * 524288,
                                      [[1024, 128], [131072, 4], [1, 1024]]),
                              bass.AP(XR, tc * 4096,
                                      [[NTT * 1024, 128], [1024, 4],
                                       [1, 1024]]), "STO")
                    P.barrier()
    if stop in ("ln", "ffnA", "ffn0", "ffnAmax", "ffnB", "ffnC", "ffnD",
                "attn"):
        P.dma("sync",
              bass.AP(y_h, 0, [[1024, 128], [131072, NTT], [1, 1024]]),
              bass.AP(XR, 0, [[NTT * 1024, 128], [1024, NTT], [1, 1024]]),
              "STO")
        P.wait("sync", "STO", 16)
    else:
        P.wait("sync", "STO", 64)

    with nc.allow_non_contiguous_dma("small vector loads"), nc.Block() as block:
        @block.sync
        def _(e):
            for f in P.ops["sync"]:
                f(e)

        @block.vector
        def _(e):
            for f in P.ops["vector"]:
                f(e)

        @block.scalar
        def _(e):
            for f in P.ops["scalar"]:
                f(e)

        @block.tensor
        def _(e):
            for f in P.ops["tensor"]:
                f(e)

        @block.gpsimd
        def _(e):
            for f in P.ops["gpsimd"]:
                f(e)

    return nc


_CACHED = {}


def _get_nc():
    if "nc" not in _CACHED:
        _CACHED["nc"] = build_program()
    return _CACHED["nc"]


def kernel(L0_attn_x, L0_attn_Wa, L0_attn_Wx, L0_attn_ba, L0_attn_bx,
           L0_attn_Lam, L0_ffn_clip_x, L0_ffn_clip_down, L0_ffn_w_gate,
           L0_ffn_w_up, L0_ffn_w_down):
    x = np.ascontiguousarray(np.asarray(L0_attn_x, dtype=np.float32))
    B, T, _ = x.shape
    wg = np.ascontiguousarray(L0_ffn_w_gate, dtype=np.float32)
    wu = np.ascontiguousarray(L0_ffn_w_up, dtype=np.float32)
    wd = np.ascontiguousarray(L0_ffn_w_down, dtype=np.float32)
    wa = np.ascontiguousarray(L0_attn_Wa, dtype=np.float32)
    wx = np.ascontiguousarray(L0_attn_Wx, dtype=np.float32)
    gu_all = np.concatenate(
        [wg.reshape(NIT, 128, D), wu.reshape(NIT, 128, D),
         np.ones((4, 128, D), dtype=np.float32)], axis=0)
    shared = {
        "wa": wa,
        "wx": wx,
        "ba": np.ascontiguousarray(L0_attn_ba, dtype=np.float32),
        "bx": np.ascontiguousarray(L0_attn_bx, dtype=np.float32),
        "lam": np.ascontiguousarray(L0_attn_Lam, dtype=np.float32),
        "clipx": np.ascontiguousarray(L0_ffn_clip_x, dtype=np.float32),
        "clipd": np.ascontiguousarray(L0_ffn_clip_down, dtype=np.float32),
    }
    in_maps = []
    for c in range(NCORE):
        b, half = c // 2, c % 2
        sel = np.zeros(8, dtype=np.float32)
        if half == 1:
            sel[c - 1] = 1.0
        m = dict(shared)
        m["x"] = np.ascontiguousarray(x[b, half * NT:(half + 1) * NT, :])
        m["sel8"] = sel
        m["gus"] = np.ascontiguousarray(gu_all[6 * c:6 * c + 6])
        m["wds"] = np.ascontiguousarray(wd[128 * c:128 * c + 128, :])
        in_maps.append(m)
    nc = _get_nc()
    res = run_bass_kernel_spmd(nc, in_maps, core_ids=list(range(NCORE)))
    out = np.empty((B, T, D), dtype=np.float32)
    for c in range(NCORE):
        b, half = c // 2, c % 2
        out[b, half * NT:(half + 1) * NT, :] = np.asarray(res.results[c]["y"])
    return out


# version 30
# speedup vs baseline: 1.0328x; 1.0083x over previous
import numpy as np
import ml_dtypes
import concourse.bass as bass
import concourse.mybir as mybir
from concourse.bass_utils import run_bass_kernel_spmd

F32 = mybir.dt.float32
BF16 = mybir.dt.bfloat16
AF = mybir.ActivationFunctionType
ALU = mybir.AluOpType
AX = mybir.AxisListType

NCORE = 8
NT = 2048
D = 1024
DI = 2816
NIT = DI // 128
NDG = D // 128
NTT = NT // 128
NQ = 4
QT = NT // NQ
QTT = QT // 128
LOG8 = float(np.log(8.0))
EPS = 1e-5

POOL_N = 30720
P_PW = [0, 4096, 8192, 12288]
P_PQ = [16384, 20480]
P_PS = [24576, 26624]
P_XQT = 0
P_WRG = [8192, 8704]
P_WRU = [9216, 9728]
P_SIL = 10240
P_HT = 12288
P_UT = [14336, 15360]
P_QTS = [16384, 17408]
P_QCS = [18432, 19456]
P_HRD = 0
P_HQ = 11264
P_WDT = 16896
P_DQ = [28160, 28672]
P_TD = 29184
P_UHI = [0, 20480]
P_ULO = [2048, 22528]
P_AWR = [4096, 6144]
P_AT = 8192
P_BT = 12288
P_Y = 16384
P_UT2 = [24576, 25600]
P_SC = [26624 + 512 * k for k in range(6)]
P_SCP = [[26624, 27136, 27648], [29184, 29696, 30208]]

C_TMP = 0
C_FLAG = 1
C_A = 2
C_EPS = 3
C_ONE = 4
C_HALF = 5
C_ZERO = 6
C_MLOG8H = 7
C_SIGXO = 8
C_SIGDO = 9
C_INVS = 10
C_SX = 12
C_HALFSH = 14
C_GMAX = 15
C_B2A = 16
C_B2X = 24
C_LNA = 32
C_LAM = 40
C_SG = 48
C_SU = 70
C_SGX = 92
C_SUX = 114
C_YL = 136
C_AL = 144
C_CC = 152
C_G8 = 160
C_SEL = 224
C_CLIPX = 232
C_CLIPD = 233
C_MONE = 234
C_TMP2 = 235
C_TMP3 = 236
C_MHALF = 237
C_NEGA = 238
CSCL = 184
CGM8 = 192
C_SD = 200
CONSTS_N = 240

S_MV = 0
S_RSTD = 32
S_RSTDI = 48
S_GM = 64
S_HM = 80
S_T1 = 81
S_BN = 82
STATS_N = 96


class Plan:
    ENGS = ("sync", "vector", "scalar", "tensor", "gpsimd")

    def __init__(self, nc):
        self.nc = nc
        self.ops = {e: [] for e in self.ENGS}
        self.sems = {}

    def sem(self, name):
        self.sems[name] = [self.nc.alloc_semaphore(name=name), 0]

    def count(self, name):
        return self.sems[name][1]

    def wait(self, eng, name, value=None):
        if value is None:
            value = self.sems[name][1]
        if value is None or value <= 0:
            return
        h = self.sems[name][0]
        self.ops[eng].append(lambda e, h=h, v=value: e.wait_ge(h, v))

    def op(self, eng, fn, inc=None, drain=False):
        if drain:
            self.ops[eng].append(lambda e: e.drain())
        if inc is None:
            self.ops[eng].append(lambda e, f=fn: f(e))
            return None
        self.sems[inc][1] += 1
        v = self.sems[inc][1]
        hh = self.sems[inc][0]

        def g(e, f=fn, hh=hh):
            f(e).then_inc(hh, 1)
        self.ops[eng].append(g)
        return v

    def drain_inc(self, eng, inc):
        self.sems[inc][1] += 1
        v = self.sems[inc][1]
        hh = self.sems[inc][0]
        self.ops[eng].append(lambda e, hh=hh: e.drain().then_inc(hh, 1))
        return v

    def dma(self, eng, out_ap, in_ap, inc):
        self.sems[inc][1] += 16
        v = self.sems[inc][1]
        hh = self.sems[inc][0]

        def g(e, o=out_ap, i=in_ap, hh=hh):
            e.dma_start(out=o, in_=i).then_inc(hh, 16)
        self.ops[eng].append(g)
        return v

    def barrier(self):
        self.drain_inc("vector", "VEC")
        self.drain_inc("scalar", "SCA")
        self.drain_inc("gpsimd", "GPS")
        snap = {n: s[1] for n, s in self.sems.items()}
        for e in self.ENGS:
            for n, v in snap.items():
                if v > 0:
                    self.wait(e, n, v)


def build_program():
    nc = bass.Bass("TRN2", target_bir_lowering=False, debug=False)

    x_h = nc.dram_tensor("x", [NT, D], F32, kind="ExternalInput")
    gus_h = nc.dram_tensor("gus", [6, 128, D], F32, kind="ExternalInput")
    wds_h = nc.dram_tensor("wds", [128, DI], F32, kind="ExternalInput")
    wa_h = nc.dram_tensor("wa", [D, D], F32, kind="ExternalInput")
    wx_h = nc.dram_tensor("wx", [D, D], F32, kind="ExternalInput")
    ba_h = nc.dram_tensor("ba", [D], F32, kind="ExternalInput")
    bx_h = nc.dram_tensor("bx", [D], F32, kind="ExternalInput")
    lam_h = nc.dram_tensor("lam", [D], F32, kind="ExternalInput")
    clipx_h = nc.dram_tensor("clipx", [1], F32, kind="ExternalInput")
    clipd_h = nc.dram_tensor("clipd", [1], F32, kind="ExternalInput")
    sel_h = nc.dram_tensor("sel8", [8], F32, kind="ExternalInput")
    y_h = nc.dram_tensor("y", [NT, D], F32, kind="ExternalOutput")

    ag1_in = nc.dram_tensor("ag1_in", [1, 6 * 131072], BF16, kind="Internal")
    ag1_out = nc.dram_tensor("ag1_out", [8, 6 * 131072], BF16, kind="Internal")
    ag2_in = nc.dram_tensor("ag2_in", [1, 3 * 131072], BF16, kind="Internal")
    ag2_out = nc.dram_tensor("ag2_out", [8, 3 * 131072], BF16, kind="Internal")
    wahl_img = nc.dram_tensor("wahl_img", [NDG, 128, 2048], BF16, kind="Internal")
    wxhl_img = nc.dram_tensor("wxhl_img", [NDG, 128, 2048], BF16, kind="Internal")
    scl_in = nc.dram_tensor("scl_in", [1, 1024], F32, kind="Internal")
    scl_out = nc.dram_tensor("scl_out", [8, 1024], F32, kind="Internal")
    h_spill = nc.dram_tensor("h_spill", [4, NIT, 128, 512], F32, kind="Internal")
    ac_spill = nc.dram_tensor("ac_spill", [NQ, 128, 4096], F32, kind="Internal")
    cc_in = nc.dram_tensor("cc_in", [1, 1], F32, kind="Internal")
    cc_out = nc.dram_tensor("cc_out", [1, 1], F32, kind="Internal")
    ag_in = nc.dram_tensor("ag_in", [1, D], F32, kind="Internal")
    ag_out = nc.dram_tensor("ag_out", [8, D], F32, kind="Internal")
    ident_h = nc.inline_tensor(np.eye(128, dtype=np.float32), name="ident_dat")
    RG = [[0, 1, 2, 3, 4, 5, 6, 7]]

    XR = nc.alloc_sbuf_tensor("XR", [128, NTT * 1024], F32)
    POOL = nc.alloc_sbuf_tensor("POOL", [128, POOL_N], F32)
    CO = nc.alloc_sbuf_tensor("CO", [128, CONSTS_N], F32)
    STT = nc.alloc_sbuf_tensor("STATS", [128, STATS_N], F32)
    SDB = nc.alloc_sbuf_tensor("SDB", [128, 1024], F32)
    SDBF = nc.alloc_sbuf_tensor("SDBF", [128, 1024], F32)
    IDT = nc.alloc_sbuf_tensor("IDT", [128, 128], F32)

    pAB = [nc.alloc_psum_tensor("p%d" % k, [128, 512], F32) for k in range(4)]
    pT = nc.alloc_psum_tensor("pT", [128, 1024], F32)
    pT2 = nc.alloc_psum_tensor("pT2", [128, 1024], F32)

    P = Plan(nc)
    for s in ("LD", "SCL", "XA0", "XA1", "XA2", "XA3",
              "GW0", "GW1", "GW2", "GW3", "GW4",
              "WS0", "WS1", "WS2", "WS3", "WDS",
              "HS0", "HS1", "HS2", "HS3", "ACS", "ACS2",
              "HR0", "HR1", "HR2", "HR3", "PS0", "PS1", "CCS", "STO",
              "TEN", "VEC", "SCA", "GPS", "CCD"):
        P.sem(s)

    def co(c):
        return CO[:, c:c + 1]

    def cob(c, n):
        return bass.AP(CO, c, [[CONSTS_N, 128], [0, n]])

    def stc(c):
        return STT[:, c:c + 1]

    def pool(a, n):
        return POOL[:, a:a + n]

    def poolb(off_f32, dims):
        return bass.AP(POOL, off_f32, [[POOL_N, 128]] + dims).bitcast(BF16)

    def pool3(off, dims):
        return bass.AP(POOL, off, [[POOL_N, 128]] + dims)

    def vop(fn, inc=None, drain=True):
        return P.op("vector", fn, inc=inc, drain=drain)

    def sop(fn, inc=None, drain=True):
        return P.op("scalar", fn, inc=inc, drain=drain)

    def gop(fn, inc=None, drain=True):
        return P.op("gpsimd", fn, inc=inc, drain=drain)

    def top(fn, inc=None):
        return P.op("tensor", fn, inc=inc)

    for src, c0 in ((ba_h, C_B2A), (bx_h, C_B2X), (lam_h, C_LAM)):
        P.dma("sync", CO[:, c0:c0 + 8], bass.AP(src, 0, [[1, 128], [128, 8]]),
              "LD")
    P.dma("sync", co(C_CLIPX), bass.AP(clipx_h, 0, [[0, 128], [1, 1]]), "LD")
    P.dma("sync", co(C_CLIPD), bass.AP(clipd_h, 0, [[0, 128], [1, 1]]), "LD")
    for r in range(8):
        P.dma("sync", co(C_SEL + r), bass.AP(sel_h, r, [[0, 128], [1, 1]]), "LD")
    P.dma("sync", IDT[:], ident_h.ap(), "LD")
    ld_in = P.count("LD")

    for c, val in ((C_EPS, EPS), (C_ONE, 1.0), (C_HALF, 0.5), (C_ZERO, 0.0),
                   (C_MONE, -1.0), (C_MHALF, -0.5), (C_MLOG8H, -LOG8 / 2),
                   (C_TMP, 1.0)):
        P.op("vector", lambda e, c=c, v=val: e.memset(co(c), v))
    vop(lambda e: e.tensor_scalar(co(C_TMP2), co(C_TMP), 2.0**-30, None, ALU.add))
    vop(lambda e: e.tensor_scalar(co(C_FLAG), co(C_TMP2), -1.0, 2.0**30,
                                  ALU.add, ALU.mult))
    vop(lambda e: e.tensor_scalar(co(C_A), co(C_FLAG), 1.5 * 2.0**52,
                                  1.5 * 2.0**23, ALU.mult, ALU.add))
    vop(lambda e: e.tensor_scalar(co(C_NEGA), co(C_A), -1.0, None, ALU.mult))
    vop(lambda e: e.memset(co(CSCL + 7), 0.0))
    P.wait("vector", "LD", ld_in)
    vop(lambda e: e.tensor_scalar(CO[:, C_B2A:C_B2A + 8], CO[:, C_B2A:C_B2A + 8],
                                  0.5, None, ALU.mult))
    vop(lambda e: e.tensor_scalar(CO[:, C_B2X:C_B2X + 8], CO[:, C_B2X:C_B2X + 8],
                                  0.5, None, ALU.mult))
    v_c0 = P.drain_inc("vector", "VEC")

    P.wait("scalar", "LD", ld_in)
    P.wait("scalar", "VEC", v_c0)
    sop(lambda e: e.activation(co(C_TMP), co(C_CLIPX), AF.Tanh,
                               bias=0.0, scale=co(C_HALF)))
    sop(lambda e: e.activation(co(C_TMP2), co(C_CLIPD), AF.Tanh,
                               bias=0.0, scale=co(C_HALF)))
    sop(lambda e: e.activation(CO[:, C_G8:C_G8 + 8], CO[:, C_LAM:C_LAM + 8],
                               AF.Tanh, bias=0.0, scale=co(C_HALF)))
    s_sig = P.drain_inc("scalar", "SCA")
    P.wait("vector", "SCA", s_sig)
    vop(lambda e: e.tensor_scalar(co(C_SIGXO), co(C_TMP), 0.5 / 127.0,
                                  0.5 / 127.0, ALU.mult, ALU.add))
    vop(lambda e: e.tensor_scalar(co(C_SIGDO), co(C_TMP2), 0.5 / 127.0,
                                  0.5 / 127.0, ALU.mult, ALU.add))
    vop(lambda e: e.tensor_scalar(CO[:, C_G8:C_G8 + 8], CO[:, C_G8:C_G8 + 8],
                                  0.5, 0.5, ALU.mult, ALU.add))
    v_sig = P.drain_inc("vector", "VEC")
    P.wait("scalar", "VEC", v_sig)
    sop(lambda e: e.activation(CO[:, C_LNA:C_LNA + 8], CO[:, C_G8:C_G8 + 8],
                               AF.Ln, bias=0.0, scale=1.0))
    s_lna = P.drain_inc("scalar", "SCA")
    P.wait("vector", "SCA", s_lna)
    vop(lambda e: e.tensor_scalar(CO[:, C_LNA:C_LNA + 8], CO[:, C_LNA:C_LNA + 8],
                                  -LOG8 / 2, None, ALU.add))
    P.drain_inc("vector", "VEC")

    state = {"tv": [0, 0], "sv_s": [0, 0], "sv_v": [0, 0], "ps": [0, 0],
             "sw": [0, 0, 0, 0], "first_load_gate": [None], "gbi": 0}
    PRM, PRR, PRI = 168, 172, 176

    def prep_quant(w_hh, img, c_scale, ntiles, width, kind, batch):
        nbat = (ntiles + batch - 1) // batch
        for bi in range(nbat):
            it0 = bi * batch
            nb = min(batch, ntiles - it0)
            w = nb * width
            wslot = bi % 4
            slot = bi % 2
            gw = "GW%d" % state["gbi"]
            state["gbi"] += 1
            P.wait("gpsimd", "SCA", state["sw"][wslot])
            if state["first_load_gate"][0] is not None:
                P.wait("gpsimd", "VEC", state["first_load_gate"][0])
                state["first_load_gate"][0] = None
            P.dma("gpsimd", pool3(P_PW[wslot], [[width, nb], [1, width]]),
                  bass.AP(w_hh, it0 * 128 * width,
                          [[width, 128], [128 * width, nb], [1, width]]),
                  gw)
            P.wait("vector", gw, 16)
            win3 = pool3(P_PW[wslot], [[width, nb], [1, width]])
            vop(lambda e, a=win3, nb=nb: e.tensor_reduce(
                CO[:, PRM:PRM + nb], a, AX.X, ALU.max,
                apply_absolute_value=True))
            vop(lambda e, nb=nb: e.reciprocal(CO[:, PRR:PRR + nb],
                                              CO[:, PRM:PRM + nb]))
            vop(lambda e, nb=nb: e.tensor_scalar(
                CO[:, PRI:PRI + nb], CO[:, PRR:PRR + nb], 127.0, None,
                ALU.mult))
            vop(lambda e, c=c_scale + it0, nb=nb: e.tensor_scalar(
                CO[:, c:c + nb], CO[:, PRM:PRM + nb], 1.0 / 127.0, None,
                ALU.mult))
            v_inv = P.drain_inc("vector", "VEC")
            P.wait("scalar", "VEC", v_inv)
            P.wait("scalar", "TEN", state["tv"][slot])
            for t in range(nb):
                state["sw"][wslot] = sop(
                    lambda e, t=t, ws=wslot, s=slot, width=width: e.activation(
                        pool(P_PQ[s] + t * width, width),
                        pool(P_PW[ws] + t * width, width),
                        AF.Identity, bias=co(C_A), scale=co(PRI + t)),
                    inc="SCA")
            P.wait("tensor", "SCA", state["sw"][wslot])
            nblk = w // 128
            nhalf = (nblk + 7) // 8
            tv = 0
            sv_s = sv_v = 0
            sv_prev = {}
            for half in range(nhalf):
                par2 = half % 2
                pp = pT if par2 == 0 else pT2
                nbk = min(8, nblk - half * 8)
                if half >= 2:
                    P.wait("tensor", sv_prev[half - 2][0], sv_prev[half - 2][1])
                elif half == 0:
                    P.wait("tensor", "SCA", state["sv_s"][1 - slot])
                elif state["sv_v"][1 - slot]:
                    P.wait("tensor", "VEC", state["sv_v"][1 - slot])
                for j in range(nbk):
                    b = half * 8 + j
                    tv = top(lambda e, pp=pp, j=j, b=b, s=slot: e.transpose(
                        pp[:, j * 128:(j + 1) * 128],
                        pool(P_PQ[s] + b * 128, 128), IDT[:]), inc="TEN")
                if par2 == 0:
                    P.wait("scalar", "TEN", tv)
                    P.wait("scalar", "PS%d" % slot, state["ps"][slot])
                    sv_s = sop(lambda e, pp=pp, half=half, nbk=nbk, s=slot:
                               e.activation(
                                   poolb(P_PS[s] + half * 512, [[1, nbk * 64]]),
                                   pp[:, 0:nbk * 128], AF.Identity,
                                   bias=co(C_NEGA), scale=1.0), inc="SCA")
                    sv_prev[half] = ("SCA", sv_s)
                else:
                    P.wait("vector", "TEN", tv)
                    P.wait("vector", "PS%d" % slot, state["ps"][slot])
                    sv_v = vop(lambda e, pp=pp, half=half, nbk=nbk, s=slot:
                               e.tensor_scalar(
                                   poolb(P_PS[s] + half * 512, [[1, nbk * 64]]),
                                   pp[:, 0:nbk * 128], co(C_NEGA), None,
                                   ALU.add), inc="VEC")
                    sv_prev[half] = ("VEC", sv_v)
            state["tv"][slot] = tv
            state["sv_s"][slot] = sv_s
            state["sv_v"][slot] = sv_v
            P.wait("scalar", "SCA", sv_s)
            if sv_v:
                P.wait("scalar", "VEC", sv_v)
            if kind == "gu":
                P.dma("scalar",
                      bass.AP(img, it0 * 131072,
                              [[1024, 128], [131072, nb], [1, 1024]]),
                      poolb(P_PS[slot], [[512, nb], [1, 512]]), "PS%d" % slot)
            else:
                P.wait("vector", "PS%d" % slot, state["ps"][slot])
                mz = vop(lambda e, s=slot: e.memset(
                    pool(P_PS[s] + DI // 2, 1536 - DI // 2), 0.0), inc="VEC")
                P.wait("scalar", "VEC", mz)
                P.dma("scalar",
                      bass.AP(img, 0, [[3072, 128], [1, 3072]]),
                      poolb(P_PS[slot], [[1, 1536]]), "PS%d" % slot)
            state["ps"][slot] = P.count("PS%d" % slot)

    V_SDB = [0]
    PREP_END = {"VEC": 0, "SCA": 0}
    CCD_AG1 = [0]
    CCD_AG2 = [0]

    PREP_N = {}

    XV = {}

    def prep_compute():
        for g in (3, 7):
            gw = "GW%d" % state["gbi"]
            state["gbi"] += 1
            XV[g] = P.dma(
                "gpsimd",
                bass.AP(XR, g * 2048,
                        [[NTT * 1024, 128], [1024, 2], [1, 1024]]),
                bass.AP(x_h, g * 262144,
                        [[1024, 128], [131072, 2], [1, 1024]]), gw)
        for g in (0, 1, 2, 4, 5, 6):
            XV[g] = P.dma(
                "sync" if g < 4 else "scalar",
                bass.AP(XR, g * 2048,
                        [[NTT * 1024, 128], [1024, 2], [1, 1024]]),
                bass.AP(x_h, g * 262144,
                        [[1024, 128], [131072, 2], [1, 1024]]),
                "XA%d" % (g // 2))
        state["first_load_gate"][0] = P.count("VEC")
        prep_quant(gus_h, ag1_in, CSCL, 6, 1024, "gu", 4)
        PREP_N["ps0_gu"] = P.count("PS0")
        PREP_N["ps1_gu"] = P.count("PS1")
        prep_quant(wds_h, ag2_in, CSCL + 6, 1, DI, "wd1", 1)
        PREP_N["ps0_wd"] = P.count("PS0")
        PREP_N["ps1_wd"] = P.count("PS1")
        PREP_N["vec"] = P.count("VEC")
        PREP_N["sca"] = P.count("SCA")
        PREP_END["VEC"] = P.count("VEC")
        PREP_END["SCA"] = P.count("SCA")

    def prep_triggers():
        P.wait("sync", "VEC", P.count("VEC"))
        P.wait("sync", "SCA", PREP_N["sca"])
        sv = P.dma("sync", bass.AP(scl_in, 0, [[1, 128], [128, 8]]),
                   CO[:, CSCL:CSCL + 8], "CCS")
        P.wait("gpsimd", "CCS", sv)
        ccd_scl = P.op("gpsimd", lambda e: e.collective_compute(
            "AllGather", ALU.bypass, ins=[scl_in.ap()], outs=[scl_out.ap()],
            replica_groups=RG), inc="CCD")
        P.wait("gpsimd", "PS0", PREP_N["ps0_gu"])
        P.wait("gpsimd", "PS1", PREP_N["ps1_gu"])
        CCD_AG1[0] = P.op("gpsimd", lambda e: e.collective_compute(
            "AllGather", ALU.bypass, ins=[ag1_in.ap()], outs=[ag1_out.ap()],
            replica_groups=RG), inc="CCD")
        P.wait("gpsimd", "PS0", PREP_N["ps0_wd"])
        P.wait("gpsimd", "PS1", PREP_N["ps1_wd"])
        CCD_AG2[0] = P.op("gpsimd", lambda e: e.collective_compute(
            "AllGather", ALU.bypass, ins=[ag2_in.ap()], outs=[ag2_out.ap()],
            replica_groups=RG), inc="CCD")
        P.wait("sync", "CCD", ccd_scl)
        for dst, off, n in ((C_SG + 0, 0, 6), (C_SG + 6, 1024, 6),
                            (C_SG + 12, 2048, 6), (C_SG + 18, 3072, 4),
                            (C_SU + 0, 3584, 2), (C_SU + 2, 4096, 6),
                            (C_SU + 8, 5120, 6), (C_SU + 14, 6144, 6),
                            (C_SU + 20, 7168, 2)):
            P.dma("sync", CO[:, dst:dst + n],
                  bass.AP(scl_out, off, [[1, 128], [128, n]]), "SCL")
        V_SDB[0] = P.dma("sync", SDB[:],
                         bass.AP(scl_out, 768,
                                 [[0, 128], [1024, 8], [1, 128]]), "SCL")
        P.dma("sync", CO[:, CGM8:CGM8 + 8],
              bass.AP(scl_out, 896, [[0, 128], [1024, 8]]), "SCL")
        V_SDB[0] = P.count("SCL")

    PREP_AX_END = {"PS0": 0, "PS1": 0}
    AXW = [20480, 21504]

    AXS = {"vv": [0, 0], "ps": [0, 0]}

    def prep_ax_unit(u):
        w_hh, img = (wa_h, wahl_img) if u < NDG else (wx_h, wxhl_img)
        dt = u % NDG
        slot = u % 2
        if u == 0:
            P.wait("scalar", "TEN", P.count("TEN"))
            AXS["ps"] = [P.count("PS0"), P.count("PS1")]
        P.wait("scalar", "VEC", AXS["vv"][slot])
        P.dma("scalar", pool(AXW[slot], 1024),
              bass.AP(w_hh, dt * 128 * 1024, [[1024, 128], [1, 1024]]),
              "WS%d" % (2 + slot))
        wv = P.count("WS%d" % (2 + slot))
        P.wait("vector", "WS%d" % (2 + slot), wv)
        win = pool(AXW[slot], 1024)
        hic = poolb(P_PS[slot], [[1, 512]])
        loc = poolb(P_PS[slot] + 512, [[1, 512]])
        P.wait("vector", "PS%d" % slot, AXS["ps"][slot])
        vop(lambda e, a=win, h=hic: e.tensor_copy(h, a))
        vop(lambda e, a=win, h=hic, l=loc: e.tensor_tensor(
            l, a, h, ALU.subtract))
        vv = P.drain_inc("vector", "VEC")
        AXS["vv"][slot] = vv
        P.wait("scalar", "VEC", vv)
        for off, src0 in ((0, P_PS[slot]), (1024, P_PS[slot] + 512)):
            P.dma("scalar",
                  bass.AP(img, off + dt * 128,
                          [[2048, 128], [128 * 2048, NDG], [1, 128]]),
                  poolb(src0, [[64, NDG], [1, 64]]), "PS%d" % slot)
        AXS["ps"][slot] = P.count("PS%d" % slot)
        if u == 2 * NDG - 1:
            PREP_AX_END["PS0"] = P.count("PS0")
            PREP_AX_END["PS1"] = P.count("PS1")

    def ln_stats(xwaits=None):
        for tt in range(NTT):
            if xwaits is not None and tt % 2 == 0:
                P.wait("vector", xwaits[tt // 2][0], xwaits[tt // 2][1])
            xs = XR[:, tt * 1024:tt * 1024 + 512]
            xs2 = XR[:, tt * 1024 + 512:tt * 1024 + 1024]
            vop(lambda e, a=xs: e.bn_stats(STT[:, S_BN:S_BN + 6], a))
            P.op("vector", lambda e, a=xs2: e.bn_stats(
                STT[:, S_BN + 6:S_BN + 12], a))
            vop(lambda e, tt=tt: e.bn_aggr(
                STT[:, S_MV + 2 * tt:S_MV + 2 * tt + 2],
                STT[:, S_BN:S_BN + 12]))
        v_st = P.drain_inc("vector", "VEC")
        P.wait("scalar", "VEC", v_st)
        var_ap = bass.AP(STT, S_MV + 1, [[STATS_N, 128], [2, NTT]])
        sop(lambda e: e.activation(STT[:, S_RSTD:S_RSTD + NTT], var_ap, AF.Ln,
                                   bias=co(C_EPS), scale=1.0))
        sop(lambda e: e.activation(STT[:, S_RSTD:S_RSTD + NTT],
                                   STT[:, S_RSTD:S_RSTD + NTT], AF.Exp,
                                   bias=0.0, scale=co(C_MHALF)))
        return P.drain_inc("scalar", "SCA")

    def ln_stats_quarter(q4, absmax=True):
        for tl in range(4):
            tt = q4 * 4 + tl
            xs = XR[:, tt * 1024:tt * 1024 + 512]
            xs2 = XR[:, tt * 1024 + 512:tt * 1024 + 1024]
            vop(lambda e, a=xs: e.bn_stats(STT[:, S_BN:S_BN + 6], a))
            P.op("vector", lambda e, a=xs2: e.bn_stats(
                STT[:, S_BN + 6:S_BN + 12], a))
            vop(lambda e, tt=tt: e.bn_aggr(
                STT[:, S_MV + 2 * tt:S_MV + 2 * tt + 2],
                STT[:, S_BN:S_BN + 12]))
        v_st = P.drain_inc("vector", "VEC")
        P.wait("scalar", "VEC", v_st)
        var_ap = bass.AP(STT, S_MV + 8 * q4 + 1, [[STATS_N, 128], [2, 4]])
        sop(lambda e, q4=q4, va=var_ap: e.activation(
            STT[:, S_RSTD + 4 * q4:S_RSTD + 4 * q4 + 4], va, AF.Ln,
            bias=co(C_EPS), scale=1.0))
        s_r = sop(lambda e, q4=q4: e.activation(
            STT[:, S_RSTD + 4 * q4:S_RSTD + 4 * q4 + 4],
            STT[:, S_RSTD + 4 * q4:S_RSTD + 4 * q4 + 4], AF.Exp,
            bias=0.0, scale=co(C_MHALF)), inc="SCA")
        if not absmax:
            return s_r
        P.wait("vector", "SCA", s_r)
        for tl in range(4):
            tt = q4 * 4 + tl
            vop(lambda e, tt=tt, s=tl % 2: e.tensor_scalar(
                pool(P_UT2[s], 1024), XR[:, tt * 1024:(tt + 1) * 1024],
                stc(S_MV + 2 * tt), stc(S_RSTD + tt),
                ALU.subtract, ALU.mult))
            vop(lambda e, tt=tt, s=tl % 2: e.tensor_reduce(
                stc(S_GM + tt), pool(P_UT2[s], 1024), AX.X, ALU.max,
                apply_absolute_value=True))
        return s_r

    def allreduce_max(src_col, dst_col):
        tv = top(lambda e, s=src_col: e.transpose(pT[0:1, 0:128], s, IDT[:]),
                 inc="TEN")
        P.wait("vector", "TEN", tv)
        vop(lambda e: e.tensor_reduce(CO[0:1, C_TMP:C_TMP + 1],
                                      pT[0:1, 0:128], AX.X, ALU.max))
        v = P.drain_inc("vector", "VEC")
        P.wait("sync", "VEC", v)
        P.wait("sync", "CCD", P.count("CCD"))
        cv = P.dma("sync", cc_in.ap(), CO[0:1, C_TMP:C_TMP + 1], "CCS")
        P.wait("gpsimd", "CCS", cv)
        cc = P.op("gpsimd", lambda e: e.collective_compute(
            "AllReduce", ALU.max, ins=[cc_in.ap()], outs=[cc_out.ap()],
            replica_groups=RG), inc="CCD")
        P.wait("sync", "CCD", cc)
        gv = P.dma("sync", co(dst_col), bass.AP(cc_out, 0, [[0, 128], [1, 1]]),
                   "LD")
        return gv, tv

    def ffn_pass(fi, overlap_fn=None, pre_fn=None, pre=None):
        import os as _os
        _stop = _os.environ.get("KSTOP", "")
        if pre is not None:
            s_rstd = pre["s_rstd"]
            gv = pre["gv"]
            gv_dist = None
            P.wait("vector", "LD", gv)
            P.wait("vector", "SCA", s_rstd)
            vop(lambda e: e.tensor_scalar(co(C_SX), co(C_GMAX), co(C_SIGXO),
                                          None, ALU.mult))
            return ffn_tail(fi, _stop, s_rstd, gv_dist, overlap_fn)
        xw = None
        if pre_fn is not None:
            pre_fn()
            xw = [("XA0", XV[1]), ("XA0", XV[1]), ("XA1", XV[2]),
                  ("GW0", 16), ("XA2", XV[5]), ("XA2", XV[5]),
                  ("XA3", XV[6]), ("GW1", 16)]
        s_rstd = ln_stats(xw)
        P.wait("vector", "SCA", s_rstd)
        for tt in range(NTT):
            slot = tt % 2
            vop(lambda e, tt=tt, s=slot: e.tensor_scalar(
                pool(P_UT[s], 1024), XR[:, tt * 1024:(tt + 1) * 1024],
                stc(S_MV + 2 * tt), stc(S_RSTD + tt),
                ALU.subtract, ALU.mult))
            vop(lambda e, tt=tt, s=slot: e.tensor_reduce(
                stc(S_GM + tt), pool(P_UT[s], 1024), AX.X, ALU.max,
                apply_absolute_value=True))
        vop(lambda e: e.tensor_reduce(co(C_GMAX), STT[:, S_GM:S_GM + NTT],
                                      AX.X, ALU.max))
        v_gm = P.drain_inc("vector", "VEC")
        P.wait("tensor", "VEC", v_gm)
        gv_dist = None
        if overlap_fn is not None:
            tv = top(lambda e: e.transpose(pT[0:1, 0:128], co(C_GMAX),
                                           IDT[:]), inc="TEN")
            P.wait("vector", "TEN", tv)
            vop(lambda e: e.tensor_reduce(CO[0:1, CSCL + 7:CSCL + 8],
                                          pT[0:1, 0:128], AX.X, ALU.max))
            P.drain_inc("vector", "VEC")
            overlap_fn()
            gv_dist = P.count("SCL")
            P.wait("vector", "SCL", gv_dist)
            vop(lambda e: e.tensor_reduce(co(C_GMAX), CO[:, CGM8:CGM8 + 8],
                                          AX.X, ALU.max))
        else:
            gv, _ = allreduce_max(co(C_GMAX), C_GMAX)
            P.wait("vector", "LD", gv)
        vop(lambda e: e.tensor_scalar(co(C_SX), co(C_GMAX), co(C_SIGXO), None,
                                      ALU.mult))
        return ffn_tail(fi, _stop, s_rstd, gv_dist, overlap_fn)

    def ffn_tail(fi, _stop, s_rstd, gv_dist, overlap_fn):
        vop(lambda e: e.reciprocal(co(C_INVS), co(C_SX)))
        vop(lambda e: e.tensor_scalar(STT[:, S_RSTDI:S_RSTDI + NTT],
                                      STT[:, S_RSTD:S_RSTD + NTT],
                                      co(C_INVS), None, ALU.mult))
        P.drain_inc("vector", "VEC")

        if _stop == "ffnAmax":
            for it in range(NIT):
                P.op("vector", lambda e, it=it: e.tensor_scalar(
                    co(C_SGX + it), co(C_SG + it), co(C_SX), None, ALU.mult))
                P.op("vector", lambda e, it=it: e.tensor_scalar(
                    co(C_SUX + it), co(C_SU + it), co(C_SX), None, ALU.mult))
            P.drain_inc("vector", "VEC")
            return
        t_xp = [0] * NTT
        s_ev = [0] * NTT
        g_cl = [0] * NTT
        P.wait("vector", "TEN", P.count("TEN"))
        P.wait("scalar", "WS0", P.count("WS0"))
        P.wait("scalar", "WS1", P.count("WS1"))
        for tt in range(NTT):
            slot = tt % 2
            vop(lambda e, tt=tt, s=slot: e.tensor_scalar(
                pool(P_UT[s], 1024), XR[:, tt * 1024:(tt + 1) * 1024],
                stc(S_MV + 2 * tt), stc(S_RSTDI + tt),
                ALU.subtract, ALU.mult))
            vop(lambda e, s=slot: e.tensor_scalar(
                pool(P_QTS[s], 1024), pool(P_UT[s], 1024),
                co(C_A), co(C_A), ALU.add, ALU.subtract))
            if tt >= 2:
                P.wait("vector", "TEN", t_xp[tt - 2])
            vop(lambda e, s=slot: e.tensor_scalar(
                pool(P_QCS[s], 1024), pool(P_QTS[s], 1024),
                -128.0, 127.0, ALU.max, ALU.min))
            g_cl[tt] = P.drain_inc("vector", "VEC")
            pp = pT if slot == 0 else pT2
            P.wait("tensor", "VEC", g_cl[tt])
            if tt >= 2:
                P.wait("tensor", "SCA", s_ev[tt - 2])
            for j in range(8):
                t_xp[tt] = top(lambda e, pp=pp, j=j, s=slot: e.transpose(
                    pp[:, j * 128:(j + 1) * 128],
                    pool(P_QCS[s] + j * 128, 128), IDT[:]),
                    inc="TEN" if j == 7 else None)
            P.wait("scalar", "TEN", t_xp[tt])
            s_ev[tt] = sop(lambda e, pp=pp, tt=tt: e.activation(
                poolb(P_XQT + tt * 64, [[1024, 8], [1, 64]]),
                bass.AP(pp, 0, [[1024, 128], [128, 8], [1, 128]]),
                AF.Identity, bias=0.0, scale=1.0), inc="SCA")

        if gv_dist is not None:
            P.wait("vector", "SCL", gv_dist)
        for it in range(NIT):
            P.op("vector", lambda e, it=it: e.tensor_scalar(
                co(C_SGX + it), co(C_SG + it), co(C_SX), None, ALU.mult))
            P.op("vector", lambda e, it=it: e.tensor_scalar(
                co(C_SUX + it), co(C_SU + it), co(C_SX), None, ALU.mult))
        v_scales = P.drain_inc("vector", "VEC")
        if _stop == "ffnB":
            return
        vop(lambda e: e.memset(stc(S_HM), 0.0))
        P.wait("sync", "VEC", PREP_END["VEC"])
        P.wait("sync", "SCA", PREP_END["SCA"])
        P.wait("sync", "CCD", CCD_AG1[0])
        t_xp_last = t_xp[NTT - 1]
        v_bq_done = P.count("VEC")
        w_v = [0] * NIT
        for it in (0, 1):
            slot = it % 2
            P.dma("sync", poolb(P_WRG[slot], [[1, 512]]),
                  bass.AP(ag1_out, it * 131072, [[1024, 128], [1, 1024]]),
                  "WS%d" % slot)
            w_v[it] = P.dma("sync", poolb(P_WRU[slot], [[1, 512]]),
                            bass.AP(ag1_out, (NIT + it) * 131072,
                                    [[1024, 128], [1, 1024]]), "WS%d" % slot)
        NK = NIT * 4
        t_g = [0] * NK
        t_u = [0] * NK
        s_si = [0] * NK
        v_h = [0] * NK
        g_h = [0] * NK
        hsp = [0] * NK
        t_last_it = [0] * NIT
        P.wait("scalar", "VEC", v_scales)
        for it in range(NIT):
            slot = it % 2
            if overlap_fn is not None and 2 <= it < 2 + 2 * NDG:
                prep_ax_unit(it - 2)
            if it >= 2:
                P.wait("sync", "TEN", t_last_it[it - 2])
                P.dma("sync", poolb(P_WRG[slot], [[1, 512]]),
                      bass.AP(ag1_out, it * 131072, [[1024, 128], [1, 1024]]),
                      "WS%d" % slot)
                w_v[it] = P.dma("sync", poolb(P_WRU[slot], [[1, 512]]),
                                bass.AP(ag1_out, (NIT + it) * 131072,
                                        [[1024, 128], [1, 1024]]),
                                "WS%d" % slot)
            for tc in range(4):
                k = it * 4 + tc
                pg = pAB[2 * (k % 2)]
                pu = pAB[2 * (k % 2) + 1]
                P.wait("tensor", "WS%d" % slot, w_v[it])
                if it == 0:
                    P.wait("tensor", "SCA", s_ev[4 * tc + 3])
                if k >= 2:
                    P.wait("tensor", "SCA", s_si[k - 2])
                    P.wait("tensor", "VEC", v_h[k - 2])
                for g in range(NDG):
                    t_g[k] = top(lambda e, g=g, s=slot, tc=tc, pg=pg: e.matmul(
                        pg[:], poolb(P_WRG[s] + g * 64, [[1, 64]]),
                        poolb(P_XQT + g * 1024 + tc * 256, [[1, 256]]),
                        start=(g == 0), stop=(g == NDG - 1)),
                        inc="TEN" if g == NDG - 1 else None)
                for g in range(NDG):
                    t_u[k] = top(lambda e, g=g, s=slot, tc=tc, pu=pu: e.matmul(
                        pu[:], poolb(P_WRU[s] + g * 64, [[1, 64]]),
                        poolb(P_XQT + g * 1024 + tc * 256, [[1, 256]]),
                        start=(g == 0), stop=(g == NDG - 1)),
                        inc="TEN" if g == NDG - 1 else None)
                if tc == 3:
                    t_last_it[it] = t_u[k]
                P.wait("scalar", "TEN", t_g[k])
                if k >= 4:
                    P.wait("scalar", "VEC", v_h[k - 4])
                s_si[k] = sop(lambda e, k=k, it=it, pg=pg: e.activation(
                    pool(P_SIL + (k % 4) * 512, 512), pg[:], AF.Silu,
                    bias=0.0, scale=co(C_SGX + it)), inc="SCA")
                P.wait("vector", "TEN", t_u[k])
                P.wait("vector", "SCA", s_si[k])
                if k >= 4:
                    P.wait("vector", "HS%d" % (k % 4), hsp[k - 4])
                v_h[k] = vop(lambda e, k=k, it=it, pu=pu: e.scalar_tensor_tensor(
                    pool(P_HT + (k % 4) * 512, 512), pu[:], co(C_SUX + it),
                    pool(P_SIL + (k % 4) * 512, 512), ALU.mult, ALU.mult),
                    inc="VEC")
                vop(lambda e, k=k: e.tensor_reduce(
                    stc(S_T1), pool(P_HT + (k % 4) * 512, 512), AX.X, ALU.max,
                    apply_absolute_value=True))
                vop(lambda e: e.tensor_tensor(stc(S_HM), stc(S_HM), stc(S_T1),
                                              ALU.max))
                g_h[k] = P.drain_inc("vector", "VEC")
                P.wait("sync", "VEC", v_h[k])
                hsp[k] = P.dma("sync",
                               bass.AP(h_spill, (tc * NIT + it) * 65536,
                                       [[512, 128], [1, 512]]),
                               pool(P_HT + (k % 4) * 512, 512), "HS%d" % (k % 4))

        if _stop == "ffnC":
            return
        for q in ("sync", "scalar"):
            P.wait(q, "TEN", t_xp_last)
            P.wait(q, "VEC", v_bq_done)
            P.wait(q, "PS0", P.count("PS0"))
            P.wait(q, "PS1", P.count("PS1"))
            P.wait(q, "CCD", CCD_AG2[0])
        wdt_v = 0
        for t8 in range(8):
            wdt_v = P.dma("sync" if t8 % 2 == 0 else "scalar",
                          poolb(P_WDT + t8 * 1408, [[1, 1408]]),
                          bass.AP(ag2_out, 3 * t8 * 131072,
                                  [[3072, 128], [1, 2816]]), "WDS")
        P.wait("tensor", "VEC", g_h[NK - 1])
        gv2, _ = allreduce_max(stc(S_HM), C_GMAX)
        P.wait("vector", "LD", gv2)
        P.wait("vector", "SCL", V_SDB[0])
        vop(lambda e: e.tensor_scalar(co(C_SX), co(C_GMAX), co(C_SIGDO), None,
                                      ALU.mult))
        vop(lambda e: e.reciprocal(co(C_INVS), co(C_SX)))
        vop(lambda e: e.tensor_scalar(co(C_HALFSH), co(C_SX), 0.5, None,
                                      ALU.mult))
        vop(lambda e: e.tensor_scalar(SDBF[:], SDB[:], co(C_HALFSH), None,
                                      ALU.mult))
        v_sdbf = P.drain_inc("vector", "VEC")
        P.wait("sync", "TEN", t_last_it[NIT - 1])
        P.wait("sync", "VEC", g_h[NK - 1])
        P.wait("sync", "SCA", s_si[NK - 1])
        hrv = [[0, 0] for _ in range(4)]
        g_hq = [[0] * NIT for _ in range(4)]
        t_dm = [0] * NTT
        v_ep = [0] * NTT
        first_q = True
        prev_vdq = None
        for s4 in range(4):
            P.wait("sync", "HS%d" % s4, P.count("HS%d" % s4))
        dps = [pT[:, 0:512], pT[:, 512:1024], pT2[:, 0:512], pT2[:, 512:1024],
               pAB[0][:], pAB[1][:], pAB[2][:], pAB[3][:]]
        t_mm = [[0] * NIT for _ in range(4)]
        v_ep2 = [[0] * 4 for _ in range(4)]

        def d_readback(tc):
            for hf, q in ((0, "sync"), (1, "scalar")):
                if tc >= 1:
                    P.wait(q, "SCA", prev_sdq[hf * 11 + 10])
                elif hf == 1:
                    P.wait(q, "TEN", t_last_it[NIT - 1])
                    P.wait(q, "VEC", g_h[NK - 1])
                    for s4 in range(4):
                        P.wait(q, "HS%d" % s4, P.count("HS%d" % s4))
                hrv[tc][hf] = P.dma(
                    q, pool3(P_HRD + hf * 11 * 512, [[512, 11], [1, 512]]),
                    bass.AP(h_spill, (tc * NIT + hf * 11) * 65536,
                            [[512, 128], [65536, 11], [1, 512]]), "HR%d" % hf)

        d_readback(0)
        prev_sdq = None
        for tc in range(4):
            v_dq = [0] * NIT
            s_dq = [0] * NIT
            for ig in range(NIT):
                P.wait("scalar", "HR%d" % (ig // 11), hrv[tc][ig // 11])
                if first_q:
                    P.wait("scalar", "VEC", v_sdbf)
                    for s4 in range(4):
                        P.wait("vector", "HS%d" % s4, P.count("HS%d" % s4))
                    first_q = False
                if ig >= 2:
                    P.wait("scalar", "VEC", v_dq[ig - 2])
                elif prev_sdq is not None:
                    P.wait("scalar", "VEC", prev_vdq[NIT - 2 + ig])
                s_dq[ig] = sop(lambda e, ig=ig: e.activation(
                    pool(P_DQ[ig % 2], 512), pool(P_HRD + ig * 512, 512),
                    AF.Identity, bias=co(C_A), scale=co(C_INVS)), inc="SCA")
                P.wait("vector", "SCA", s_dq[ig])
                if tc >= 1:
                    P.wait("vector", "TEN", t_mm[tc - 1][ig])
                vop(lambda e, ig=ig: e.tensor_scalar(
                    pool(P_DQ[ig % 2], 512), pool(P_DQ[ig % 2], 512),
                    co(C_A), -128.0, ALU.subtract, ALU.max), drain=True)
                vop(lambda e, ig=ig: e.tensor_single_scalar(
                    poolb(P_HQ + ig * 256, [[1, 256]]),
                    pool(P_DQ[ig % 2], 512), 127.0, ALU.min))
                v_dq[ig] = P.drain_inc("vector", "VEC")
                g_hq[tc][ig] = v_dq[ig]
                P.wait("tensor", "VEC", v_dq[ig])
                if ig == 0:
                    if tc == 0:
                        P.wait("tensor", "WDS", wdt_v)
                    else:
                        for tq2 in range(4):
                            P.wait("tensor", "VEC", v_ep2[tc - 1][tq2])
                for tq in range(4):
                    for half in range(2):
                        ps = dps[(2 * tq + half) % 8]
                        t_mm[tc][ig] = top(
                            lambda e, ig=ig, tq=tq, half=half, ps=ps: e.matmul(
                                ps, poolb(P_HQ + ig * 256 + tq * 64, [[1, 64]]),
                                poolb(P_WDT + half * 4 * 1408 + ig * 64,
                                      [[1408, 4], [1, 64]]),
                                start=(ig == 0), stop=(ig == NIT - 1)),
                            inc="TEN" if half == 1 and tq == 3 else None)
            prev_sdq = s_dq
            prev_vdq = v_dq
            if tc < 3:
                d_readback(tc + 1)
            for tq in range(4):
                tt = tc * 4 + tq
                P.wait("vector", "TEN", t_mm[tc][NIT - 1])
                for half in range(2):
                    ps = dps[(2 * tq + half) % 8]
                    vop(lambda e, ps=ps, half=half: e.tensor_tensor(
                        pool(P_TD + half * 512, 512), ps,
                        SDBF[:, half * 512:(half + 1) * 512], ALU.mult))
                vop(lambda e, tt=tt: e.tensor_tensor(
                    XR[:, tt * 1024:(tt + 1) * 1024],
                    XR[:, tt * 1024:(tt + 1) * 1024],
                    pool(P_TD, 1024), ALU.add))
                v_ep2[tc][tq] = P.drain_inc("vector", "VEC")
            if fi == 0:
                ATTN_RSTD[0] = ln_stats_quarter(tc, absmax=False)
        return v_ep2

    ATTN_RSTD = [0]

    def attn_pass():
        s_rstd = ATTN_RSTD[0]
        if not s_rstd:
            s_rstd = ln_stats()
        P.wait("vector", "SCA", s_rstd)
        for g in range(8):
            P.op("vector", lambda e, g=g: e.memset(co(C_YL + g), 0.0))
            P.op("vector", lambda e, g=g: e.memset(co(C_AL + g), 1.0))
        v_init = P.drain_inc("vector", "VEC")
        aw_v = [0] * 8
        t_ri = [0] * 8
        ac_v = [0] * NQ
        st = {
            "v_scan": [0] * 8,
            "t_y": [0] * QTT,
            "v_xr": [0] * QTT,
            "s_at": [0] * 8,
            "s_tb": [0] * 8,
        }

        def emit_A(qq, prev_t_ut):
            par = qq % 2
            s_uhi = [0] * QTT
            v_ulo = [0] * QTT
            t_ut = [0] * QTT
            for tl in range(QTT):
                ti = qq * QTT + tl
                slot = tl % 2
                pp = pT if slot == 0 else pT2
                if tl >= 2:
                    P.wait("vector", "TEN", t_ut[tl - 2])
                elif prev_t_ut is not None:
                    P.wait("vector", "TEN", prev_t_ut[tl + 2])
                vop(lambda e, ti=ti, s=slot: e.tensor_scalar(
                    pool(P_UT2[s], 1024), XR[:, ti * 1024:(ti + 1) * 1024],
                    stc(S_MV + 2 * ti), stc(S_RSTD + ti),
                    ALU.subtract, ALU.mult))
                vu = P.drain_inc("vector", "VEC")
                P.wait("tensor", "VEC", vu)
                if tl >= 2:
                    P.wait("tensor", "SCA", s_uhi[tl - 2])
                    P.wait("tensor", "VEC", v_ulo[tl - 2])
                else:
                    P.wait("tensor", "VEC", st["v_xr"][2 + tl])
                for j in range(8):
                    t_ut[tl] = top(lambda e, pp=pp, j=j, s=slot: e.transpose(
                        pp[:, j * 128:(j + 1) * 128],
                        pool(P_UT2[s] + j * 128, 128), IDT[:]),
                        inc="TEN" if j == 7 else None)
                P.wait("scalar", "TEN", t_ut[tl])
                s_uhi[tl] = sop(lambda e, pp=pp, tl=tl, par=par: e.activation(
                    poolb(P_UHI[par] + tl * 64, [[256, 8], [1, 64]]),
                    bass.AP(pp, 0, [[1024, 128], [128, 8], [1, 128]]),
                    AF.Identity, bias=0.0, scale=1.0), inc="SCA")
                P.wait("vector", "SCA", s_uhi[tl])
                v_ulo[tl] = vop(lambda e, pp=pp, tl=tl, par=par: e.tensor_tensor(
                    poolb(P_ULO[par] + tl * 64, [[256, 8], [1, 64]]),
                    bass.AP(pp, 0, [[1024, 128], [128, 8], [1, 128]]),
                    poolb(P_UHI[par] + tl * 64, [[256, 8], [1, 64]]),
                    ALU.subtract), inc="VEC")
            return s_uhi, v_ulo, t_ut

        def aw_load(ht, q="sync", sem=None):
            slot = ht % 2
            sem = sem or "WS%d" % slot
            P.dma(q, poolb(P_AWR[slot], [[1, 1024]]),
                  bass.AP(wahl_img, ht * 262144, [[2048, 128], [1, 2048]]),
                  sem)
            v = P.dma(q, poolb(P_AWR[slot] + 1024, [[1, 1024]]),
                      bass.AP(wxhl_img, ht * 262144,
                              [[2048, 128], [1, 2048]]),
                      sem)
            aw_v[ht] = (sem, v)

        def emit_MM(qq, v_ulo, ht0, ht1, s_at, s_tb, binter=None):
            par = qq % 2
            if qq == 0 and ht0 == 0:
                P.wait("sync", "PS0", PREP_AX_END["PS0"])
                P.wait("sync", "PS1", PREP_AX_END["PS1"])
                aw_load(0)
                aw_load(1)
            for ht in range(ht0, ht1):
                slot = ht % 2
                if ht >= 2:
                    P.wait("sync", "TEN", t_ri[ht - 2])
                    aw_load(ht)
                pr = pAB[2 * (ht % 2)]
                pi = pAB[2 * (ht % 2) + 1]
                P.wait("tensor", aw_v[ht][0], aw_v[ht][1])
                if ht == 0:
                    P.wait("tensor", "VEC", v_ulo[QTT - 1])
                if ht >= 2:
                    P.wait("tensor", "SCA", s_at[ht - 2])
                    P.wait("tensor", "SCA", s_tb[ht - 2])
                elif qq > 0:
                    P.wait("tensor", "SCA", st["s_at"][ht + 6])
                    P.wait("tensor", "SCA", st["s_tb"][ht + 6])
                mmn = 0
                for woff, uoff in ((0, P_UHI[par]), (0, P_ULO[par]),
                                   (512, P_UHI[par])):
                    for g in range(NDG):
                        mmn += 1
                        t_ri[ht] = top(
                            lambda e, g=g, s=slot, wo=woff, uo=uoff, pr=pr,
                            mm=mmn: e.matmul(
                                pr[:], poolb(P_AWR[s] + wo + g * 64, [[1, 64]]),
                                poolb(uo + g * 256, [[1, 256]]),
                                start=(mm == 1), stop=(mm == 24)),
                            inc="TEN" if mmn == 24 else None)
                P.wait("scalar", "TEN", t_ri[ht])
                if qq > 0:
                    P.wait("scalar", "VEC", st["v_scan"][ht])
                    if ht == 0:
                        P.wait("scalar", "ACS", ac_v[qq - 1][0])
                    elif ht == 4:
                        P.wait("scalar", "ACS2", ac_v[qq - 1][1])
                sop(lambda e, pr=pr, ht=ht: e.activation(
                    pool(P_SC[0], 512), pr[:], AF.Tanh,
                    bias=co(C_B2A + ht), scale=co(C_HALF)))
                s_at[ht] = sop(lambda e, ht=ht: e.activation(
                    pool(P_AT + ht * 512, 512), pool(P_SC[0], 512), AF.Exp,
                    bias=co(C_LNA + ht), scale=co(C_MLOG8H)), inc="SCA")
                mmn = 0
                for woff, uoff in ((1024, P_UHI[par]), (1024, P_ULO[par]),
                                   (1536, P_UHI[par])):
                    for g in range(NDG):
                        mmn += 1
                        t_ri[ht] = top(
                            lambda e, g=g, s=slot, wo=woff, uo=uoff, pi=pi,
                            mm=mmn: e.matmul(
                                pi[:], poolb(P_AWR[s] + wo + g * 64, [[1, 64]]),
                                poolb(uo + g * 256, [[1, 256]]),
                                start=(mm == 1), stop=(mm == 24)),
                            inc="TEN" if mmn == 24 else None)
                P.wait("scalar", "TEN", t_ri[ht])
                s_tb[ht] = sop(lambda e, pi=pi, ht=ht: e.activation(
                    pool(P_BT + ht * 512, 512), pi[:], AF.Tanh,
                    bias=co(C_B2X + ht), scale=co(C_HALF)), inc="SCA")
                if binter is not None and ht >= 4:
                    emit_B_ht(qq, ht - 4, s_at, s_tb, binter[0], binter[1])
            if ht1 == 8 and binter is not None:
                for bh in range(ht1 - 4, 8):
                    emit_B_ht(qq, bh, s_at, s_tb, binter[0], binter[1])
            if ht1 == 8 and qq + 1 < NQ:
                gw = "GW%d" % qq
                P.wait("gpsimd", gw, P.count(gw))
                P.wait("gpsimd", "TEN", t_ri[6])
                aw_load(0, q="gpsimd", sem=gw)
                P.wait("gpsimd", gw, P.count(gw))
                P.wait("gpsimd", "TEN", t_ri[7])
                aw_load(1, q="gpsimd", sem=gw)
            return s_at, s_tb

        def emit_B_ht(qq, ht, s_at, s_tb, v_scan, v_b1):
            par = qq % 2
            if True:
                sc = P_SCP[ht % 2]
                P.wait("scalar", "VEC",
                       v_b1[ht - 2] if ht >= 2 else st["v_scan"][6 + (ht % 2)])
                P.wait("scalar", "SCA", s_tb[ht])
                sop(lambda e, ht=ht, sc=sc: e.activation(
                    pool(sc[0], 512), pool(P_AT + ht * 512, 512), AF.Square,
                    bias=0.0, scale=1.0))
                sop(lambda e, sc=sc: e.activation(
                    pool(sc[1], 512), pool(sc[0], 512), AF.Ln,
                    bias=co(C_ONE), scale=co(C_MONE)))
                s_s = sop(lambda e, sc=sc: e.activation(
                    pool(sc[2], 512), pool(sc[1], 512), AF.Exp,
                    bias=0.0, scale=co(C_HALF)), inc="SCA")
                P.wait("vector", "SCA", s_s)
                vop(lambda e, ht=ht, par=par: e.tensor_tensor(
                    pool(P_SC[3], 512),
                    poolb(P_UHI[par] + ht * 256, [[1, 256]]),
                    poolb(P_ULO[par] + ht * 256, [[1, 256]]), ALU.add))
                vop(lambda e, ht=ht: e.tensor_scalar(
                    pool(P_SC[4], 512), pool(P_BT + ht * 512, 512),
                    0.5, 0.5, ALU.mult, ALU.add))
                vop(lambda e, sc=sc: e.tensor_tensor(
                    pool(P_SC[4], 512), pool(P_SC[4], 512), pool(sc[2], 512),
                    ALU.mult))
                v_b1[ht] = P.drain_inc("vector", "VEC")
                vop(lambda e, ht=ht: e.tensor_tensor(
                    pool(P_BT + ht * 512, 512), pool(P_SC[4], 512),
                    pool(P_SC[3], 512), ALU.mult))
                if ht == 0:
                    P.wait("vector", "TEN", st["t_y"][QTT - 1])
                vop(lambda e, ht=ht: e.tensor_tensor_scan(
                    pool(P_Y + ht * 512, 512), pool(P_AT + ht * 512, 512),
                    pool(P_BT + ht * 512, 512), co(C_YL + ht),
                    ALU.mult, ALU.add))
                vop(lambda e, ht=ht: e.tensor_copy(
                    co(C_YL + ht), pool(P_Y + ht * 512 + 511, 1)))
                vop(lambda e, ht=ht: e.tensor_tensor_scan(
                    pool(P_BT + ht * 512, 512), pool(P_AT + ht * 512, 512),
                    cob(C_ZERO, 512), co(C_AL + ht), ALU.mult, ALU.add))
                vop(lambda e, ht=ht: e.tensor_copy(
                    co(C_AL + ht), pool(P_BT + ht * 512 + 511, 1)))
                v_scan[ht] = P.drain_inc("vector", "VEC")
            if ht == 3:
                P.wait("sync", "VEC", v_scan[3])
                a0 = P.dma("sync",
                           bass.AP(ac_spill, qq * 524288,
                                   [[4096, 128], [1, 2048]]),
                           pool(P_BT, 2048), "ACS")
                ac_v[qq] = (a0, 0)
            if ht == 7:
                P.wait("sync", "VEC", v_scan[7])
                a1 = P.dma("sync",
                           bass.AP(ac_spill, qq * 524288 + 2048,
                                   [[4096, 128], [1, 2048]]),
                           pool(P_BT + 2048, 2048), "ACS2")
                ac_v[qq] = (ac_v[qq][0], a1)

        def emit_EPI(qq, uhi_gate, v_scan_q, src_off=P_Y, v_gate=None):
            t_y = [0] * QTT
            v_xr = [0] * QTT
            for tl in range(QTT):
                ti = qq * QTT + tl
                pp = pT if tl % 2 == 0 else pT2
                if tl == 0:
                    if v_gate is not None:
                        P.wait("tensor", "VEC", v_gate)
                    else:
                        P.wait("tensor", "VEC", v_scan_q[7])
                    if uhi_gate is not None:
                        P.wait("tensor", "SCA", uhi_gate[0][QTT - 1])
                        P.wait("tensor", "VEC", uhi_gate[1][QTT - 1])
                if tl >= 2:
                    P.wait("tensor", "VEC", v_xr[tl - 2])
                for j in range(8):
                    t_y[tl] = top(lambda e, pp=pp, j=j, tl=tl, so=src_off:
                                  e.transpose(
                                      pp[:, j * 128:(j + 1) * 128],
                                      pool(so + j * 512 + tl * 128, 128),
                                      IDT[:]),
                                  inc="TEN" if j == 7 else None)
                P.wait("vector", "TEN", t_y[tl])
                v_xr[tl] = vop(lambda e, ti=ti, pp=pp: e.tensor_tensor(
                    XR[:, ti * 1024:(ti + 1) * 1024],
                    XR[:, ti * 1024:(ti + 1) * 1024], pp[:], ALU.add),
                    inc="VEC")
            return t_y, v_xr

        cur = emit_A(0, None)
        pend = None
        for qq in range(NQ):
            s_at, s_tb = emit_MM(qq, cur[1], 0, 2, [0] * 8, [0] * 8)
            if pend is not None:
                st["t_y"], st["v_xr"] = emit_EPI(
                    pend[0], (cur[0], cur[1]), pend[1])
            if qq + 1 < NQ:
                nxt = emit_A(qq + 1, cur[2])
            else:
                nxt = cur
            v_scan = [0] * 8
            v_b1 = [0] * 8
            s_at, s_tb = emit_MM(qq, cur[1], 2, 8, s_at, s_tb,
                                 binter=(v_scan, v_b1))
            st["v_scan"] = v_scan
            st["s_at"] = s_at
            st["s_tb"] = s_tb
            pend = (qq, v_scan)
            cur = nxt
        st["t_y"], st["v_xr"] = emit_EPI(pend[0], None, pend[1])
        P.wait("sync", "VEC", st["v_scan"][7])
        cv = P.dma("sync", bass.AP(ag_in, 0, [[1, 128], [128, 8]]),
                   CO[:, C_YL:C_YL + 8], "CCS")
        P.wait("gpsimd", "CCS", cv)
        agv = P.op("gpsimd", lambda e: e.collective_compute(
            "AllGather", ALU.bypass, ins=[ag_in.ap()], outs=[ag_out.ap()],
            replica_groups=RG), inc="CCD")
        rbuf = [P_BT, 0, P_AT, P_SC[0]]
        ybuf = [P_Y, P_AWR[0]]
        rb = [0] * NQ
        v_fq = [0] * NQ
        tys = [None] * NQ
        P.wait("sync", "ACS", ac_v[NQ - 1][0])
        P.wait("sync", "ACS2", ac_v[NQ - 1][1])
        P.wait("scalar", "ACS", ac_v[NQ - 1][0])
        P.wait("scalar", "ACS2", ac_v[NQ - 1][1])
        for qq in range(NQ):
            rb[qq] = P.dma("sync" if qq % 2 == 0 else "scalar",
                           pool(rbuf[qq], 4096),
                           bass.AP(ac_spill, qq * 524288,
                                   [[4096, 128], [1, 4096]]),
                           "HR%d" % qq)
        P.wait("sync", "CCD", agv)
        for r in range(8):
            g8v = P.dma("sync", CO[:, C_G8 + r * 8:C_G8 + r * 8 + 8],
                        bass.AP(ag_out, r * 1024, [[1, 128], [128, 8]]), "LD")
        P.wait("vector", "LD", g8v)
        vop(lambda e: e.memset(CO[:, C_CC:C_CC + 8], 0.0))
        for r in range(8):
            vop(lambda e, r=r: e.scalar_tensor_tensor(
                CO[:, C_CC:C_CC + 8],
                CO[:, C_G8 + r * 8:C_G8 + r * 8 + 8],
                co(C_SEL + r), CO[:, C_CC:C_CC + 8], ALU.mult, ALU.add))
        v_cc = P.drain_inc("vector", "VEC")
        for qq in range(NQ):
            par = qq % 2
            P.wait("vector", "HR%d" % qq, rb[qq])
            P.wait("vector", "VEC", v_cc)
            if qq == 0:
                P.wait("vector", "TEN", st["t_y"][QTT - 1])
            elif qq >= 2:
                P.wait("vector", "TEN", tys[qq - 2][QTT - 1])
            for ht in range(8):
                P.op("vector", lambda e, ht=ht, qq=qq, par=par:
                     e.tensor_scalar(
                         pool(ybuf[par] + ht * 512, 512),
                         pool(rbuf[qq] + ht * 512, 512),
                         co(C_CC + ht), None, ALU.mult))
            v_fq[qq] = P.drain_inc("vector", "VEC")
            tys[qq], st["v_xr"] = emit_EPI(qq, None, None,
                                           src_off=ybuf[par],
                                           v_gate=v_fq[qq])
            s_r2 = ln_stats_quarter(qq)
        st["t_y"] = tys[NQ - 1]
        vop(lambda e: e.tensor_reduce(co(C_GMAX), STT[:, S_GM:S_GM + NTT],
                                      AX.X, ALU.max))
        v_gm2 = P.drain_inc("vector", "VEC")
        P.wait("tensor", "VEC", v_gm2)
        gv2p, _ = allreduce_max(co(C_GMAX), C_GMAX)
        return {"s_rstd": s_r2, "gv": gv2p}

    import os
    stop = os.environ.get("KSTOP", "")
    P.barrier()
    if stop != "ln":
        if stop == "ffnA":
            ln_stats()
        else:
            ffn_pass(0, overlap_fn=prep_triggers, pre_fn=prep_compute)
            P.barrier()
            if stop not in ("ffn0", "ffnAmax", "ffnB", "ffnC", "ffnD"):
                attn_pre = attn_pass()
                P.barrier()
                if stop != "attn":
                    v_ep2f = ffn_pass(1, pre=attn_pre)
                    for tc in range(4):
                        q = "sync" if tc % 2 == 0 else "scalar"
                        P.wait(q, "VEC", v_ep2f[tc][3])
                        P.dma(q,
                              bass.AP(y_h, tc * 524288,
                                      [[1024, 128], [131072, 4], [1, 1024]]),
                              bass.AP(XR, tc * 4096,
                                      [[NTT * 1024, 128], [1024, 4],
                                       [1, 1024]]), "STO")
                    P.barrier()
    if stop in ("ln", "ffnA", "ffn0", "ffnAmax", "ffnB", "ffnC", "ffnD",
                "attn"):
        P.dma("sync",
              bass.AP(y_h, 0, [[1024, 128], [131072, NTT], [1, 1024]]),
              bass.AP(XR, 0, [[NTT * 1024, 128], [1024, NTT], [1, 1024]]),
              "STO")
        P.wait("sync", "STO", 16)
    else:
        P.wait("sync", "STO", 64)

    with nc.allow_non_contiguous_dma("small vector loads"), nc.Block() as block:
        @block.sync
        def _(e):
            for f in P.ops["sync"]:
                f(e)

        @block.vector
        def _(e):
            for f in P.ops["vector"]:
                f(e)

        @block.scalar
        def _(e):
            for f in P.ops["scalar"]:
                f(e)

        @block.tensor
        def _(e):
            for f in P.ops["tensor"]:
                f(e)

        @block.gpsimd
        def _(e):
            for f in P.ops["gpsimd"]:
                f(e)

    return nc


_CACHED = {}


def _get_nc():
    if "nc" not in _CACHED:
        _CACHED["nc"] = build_program()
    return _CACHED["nc"]


def kernel(L0_attn_x, L0_attn_Wa, L0_attn_Wx, L0_attn_ba, L0_attn_bx,
           L0_attn_Lam, L0_ffn_clip_x, L0_ffn_clip_down, L0_ffn_w_gate,
           L0_ffn_w_up, L0_ffn_w_down):
    x = np.ascontiguousarray(np.asarray(L0_attn_x, dtype=np.float32))
    B, T, _ = x.shape
    wg = np.ascontiguousarray(L0_ffn_w_gate, dtype=np.float32)
    wu = np.ascontiguousarray(L0_ffn_w_up, dtype=np.float32)
    wd = np.ascontiguousarray(L0_ffn_w_down, dtype=np.float32)
    wa = np.ascontiguousarray(L0_attn_Wa, dtype=np.float32)
    wx = np.ascontiguousarray(L0_attn_Wx, dtype=np.float32)
    gu_all = np.concatenate(
        [wg.reshape(NIT, 128, D), wu.reshape(NIT, 128, D),
         np.ones((4, 128, D), dtype=np.float32)], axis=0)
    shared = {
        "wa": wa,
        "wx": wx,
        "ba": np.ascontiguousarray(L0_attn_ba, dtype=np.float32),
        "bx": np.ascontiguousarray(L0_attn_bx, dtype=np.float32),
        "lam": np.ascontiguousarray(L0_attn_Lam, dtype=np.float32),
        "clipx": np.ascontiguousarray(L0_ffn_clip_x, dtype=np.float32),
        "clipd": np.ascontiguousarray(L0_ffn_clip_down, dtype=np.float32),
    }
    in_maps = []
    for c in range(NCORE):
        b, half = c // 2, c % 2
        sel = np.zeros(8, dtype=np.float32)
        if half == 1:
            sel[c - 1] = 1.0
        m = dict(shared)
        m["x"] = np.ascontiguousarray(x[b, half * NT:(half + 1) * NT, :])
        m["sel8"] = sel
        m["gus"] = np.ascontiguousarray(gu_all[6 * c:6 * c + 6])
        m["wds"] = np.ascontiguousarray(wd[128 * c:128 * c + 128, :])
        in_maps.append(m)
    nc = _get_nc()
    res = run_bass_kernel_spmd(nc, in_maps, core_ids=list(range(NCORE)))
    out = np.empty((B, T, D), dtype=np.float32)
    for c in range(NCORE):
        b, half = c // 2, c % 2
        out[b, half * NT:(half + 1) * NT, :] = np.asarray(res.results[c]["y"])
    return out


# version 31
# speedup vs baseline: 1.0708x; 1.0369x over previous
import numpy as np
import ml_dtypes
import concourse.bass as bass
import concourse.mybir as mybir
from concourse.bass_utils import run_bass_kernel_spmd

F32 = mybir.dt.float32
BF16 = mybir.dt.bfloat16
AF = mybir.ActivationFunctionType
ALU = mybir.AluOpType
AX = mybir.AxisListType

NCORE = 8
NT = 2048
D = 1024
DI = 2816
NIT = DI // 128
NDG = D // 128
NTT = NT // 128
NQ = 4
QT = NT // NQ
QTT = QT // 128
LOG8 = float(np.log(8.0))
EPS = 1e-5

POOL_N = 30720
P_PW = [0, 4096, 8192, 12288]
P_PQ = [16384, 20480]
P_PS = [24576, 26624]
P_XQT = 0
P_WRG = [8192, 8704]
P_WRU = [9216, 9728]
P_SIL = 10240
P_HT = 12288
P_UT = [14336, 15360]
P_QTS = [16384, 17408]
P_QCS = [18432, 19456]
P_HRD = 0
P_HQ = 11264
P_WDT = 16896
P_DQ = [28160, 28672]
P_TD = 29184
P_UHI = [0, 20480]
P_ULO = [2048, 22528]
P_AWR = [4096, 6144]
P_AT = 8192
P_BT = 12288
P_Y = 16384
P_UT2 = [24576, 25600]
P_SC = [26624 + 512 * k for k in range(6)]
P_SCP = [[26624, 27136, 27648], [29184, 29696, 30208]]

C_TMP = 0
C_FLAG = 1
C_A = 2
C_EPS = 3
C_ONE = 4
C_HALF = 5
C_ZERO = 6
C_MLOG8H = 7
C_SIGXO = 8
C_SIGDO = 9
C_INVS = 10
C_SX = 12
C_HALFSH = 14
C_GMAX = 15
C_B2A = 16
C_B2X = 24
C_LNA = 32
C_LAM = 40
C_SG = 48
C_SU = 70
C_SGX = 92
C_SUX = 114
C_YL = 136
C_AL = 144
C_CC = 152
C_G8 = 160
C_SEL = 224
C_CLIPX = 232
C_CLIPD = 233
C_MONE = 234
C_TMP2 = 235
C_TMP3 = 236
C_MHALF = 237
C_NEGA = 238
CSCL = 184
CGM8 = 192
C_SD = 200
CONSTS_N = 240

S_MV = 0
S_RSTD = 32
S_RSTDI = 48
S_GM = 64
S_HM = 80
S_T1 = 81
S_BN = 82
STATS_N = 96


class Plan:
    ENGS = ("sync", "vector", "scalar", "tensor", "gpsimd")

    def __init__(self, nc):
        self.nc = nc
        self.ops = {e: [] for e in self.ENGS}
        self.sems = {}

    def sem(self, name):
        self.sems[name] = [self.nc.alloc_semaphore(name=name), 0]

    def count(self, name):
        return self.sems[name][1]

    def wait(self, eng, name, value=None):
        if value is None:
            value = self.sems[name][1]
        if value is None or value <= 0:
            return
        h = self.sems[name][0]
        self.ops[eng].append(lambda e, h=h, v=value: e.wait_ge(h, v))

    def op(self, eng, fn, inc=None, drain=False):
        if drain:
            self.ops[eng].append(lambda e: e.drain())
        if inc is None:
            self.ops[eng].append(lambda e, f=fn: f(e))
            return None
        self.sems[inc][1] += 1
        v = self.sems[inc][1]
        hh = self.sems[inc][0]

        def g(e, f=fn, hh=hh):
            f(e).then_inc(hh, 1)
        self.ops[eng].append(g)
        return v

    def drain_inc(self, eng, inc):
        self.sems[inc][1] += 1
        v = self.sems[inc][1]
        hh = self.sems[inc][0]
        self.ops[eng].append(lambda e, hh=hh: e.drain().then_inc(hh, 1))
        return v

    def dma(self, eng, out_ap, in_ap, inc):
        self.sems[inc][1] += 16
        v = self.sems[inc][1]
        hh = self.sems[inc][0]

        def g(e, o=out_ap, i=in_ap, hh=hh):
            e.dma_start(out=o, in_=i).then_inc(hh, 16)
        self.ops[eng].append(g)
        return v

    def barrier(self):
        self.drain_inc("vector", "VEC")
        self.drain_inc("scalar", "SCA")
        self.drain_inc("gpsimd", "GPS")
        snap = {n: s[1] for n, s in self.sems.items()}
        for e in self.ENGS:
            for n, v in snap.items():
                if v > 0:
                    self.wait(e, n, v)


def build_program():
    nc = bass.Bass("TRN2", target_bir_lowering=False, debug=False)

    x_h = nc.dram_tensor("x", [NT, D], F32, kind="ExternalInput")
    gus_h = nc.dram_tensor("gus", [6, 128, D], F32, kind="ExternalInput")
    wds_h = nc.dram_tensor("wds", [128, DI], F32, kind="ExternalInput")
    wa_h = nc.dram_tensor("wa", [D, D], F32, kind="ExternalInput")
    wx_h = nc.dram_tensor("wx", [D, D], F32, kind="ExternalInput")
    ba_h = nc.dram_tensor("ba", [D], F32, kind="ExternalInput")
    bx_h = nc.dram_tensor("bx", [D], F32, kind="ExternalInput")
    lam_h = nc.dram_tensor("lam", [D], F32, kind="ExternalInput")
    clipx_h = nc.dram_tensor("clipx", [1], F32, kind="ExternalInput")
    clipd_h = nc.dram_tensor("clipd", [1], F32, kind="ExternalInput")
    sel_h = nc.dram_tensor("sel8", [8], F32, kind="ExternalInput")
    y_h = nc.dram_tensor("y", [NT, D], F32, kind="ExternalOutput")

    ag1_in = nc.dram_tensor("ag1_in", [1, 6 * 131072], BF16, kind="Internal")
    ag1_out = nc.dram_tensor("ag1_out", [8, 6 * 131072], BF16, kind="Internal")
    ag2_in = nc.dram_tensor("ag2_in", [1, 3 * 131072], BF16, kind="Internal")
    ag2_out = nc.dram_tensor("ag2_out", [8, 3 * 131072], BF16, kind="Internal")
    wahl_img = nc.dram_tensor("wahl_img", [NDG, 128, 2048], BF16, kind="Internal")
    wxhl_img = nc.dram_tensor("wxhl_img", [NDG, 128, 2048], BF16, kind="Internal")
    scl_in = nc.dram_tensor("scl_in", [1, 1024], F32, kind="Internal")
    scl_out = nc.dram_tensor("scl_out", [8, 1024], F32, kind="Internal")
    h_spill = nc.dram_tensor("h_spill", [4, NIT, 128, 512], F32, kind="Internal")
    ac_spill = nc.dram_tensor("ac_spill", [NQ, 128, 4096], F32, kind="Internal")
    cc_in = nc.dram_tensor("cc_in", [1, 1], F32, kind="Internal")
    cc_out = nc.dram_tensor("cc_out", [1, 1], F32, kind="Internal")
    ag_in = nc.dram_tensor("ag_in", [1, D], F32, kind="Internal")
    ag_out = nc.dram_tensor("ag_out", [8, D], F32, kind="Internal")
    ident_h = nc.inline_tensor(np.eye(128, dtype=np.float32), name="ident_dat")
    RG = [[0, 1, 2, 3, 4, 5, 6, 7]]

    XR = nc.alloc_sbuf_tensor("XR", [128, NTT * 1024], F32)
    POOL = nc.alloc_sbuf_tensor("POOL", [128, POOL_N], F32)
    CO = nc.alloc_sbuf_tensor("CO", [128, CONSTS_N], F32)
    STT = nc.alloc_sbuf_tensor("STATS", [128, STATS_N], F32)
    SDB = nc.alloc_sbuf_tensor("SDB", [128, 1024], F32)
    SDBF = nc.alloc_sbuf_tensor("SDBF", [128, 1024], F32)
    IDT = nc.alloc_sbuf_tensor("IDT", [128, 128], F32)

    pAB = [nc.alloc_psum_tensor("p%d" % k, [128, 512], F32) for k in range(4)]
    pT = nc.alloc_psum_tensor("pT", [128, 1024], F32)
    pT2 = nc.alloc_psum_tensor("pT2", [128, 1024], F32)

    P = Plan(nc)
    for s in ("LD", "SCL", "XA0", "XA1", "XA2", "XA3",
              "GW0", "GW1", "GW2",
              "WS0", "WS1", "WS2", "WS3", "WDS",
              "HS0", "HS1", "HS2", "HS3", "ACS", "ACS2",
              "HR0", "HR1", "HR2", "HR3", "PS0", "PS1", "CCS", "STO",
              "TEN", "VEC", "SCA", "GPS", "CCD"):
        P.sem(s)

    def co(c):
        return CO[:, c:c + 1]

    def cob(c, n):
        return bass.AP(CO, c, [[CONSTS_N, 128], [0, n]])

    def stc(c):
        return STT[:, c:c + 1]

    def pool(a, n):
        return POOL[:, a:a + n]

    def poolb(off_f32, dims):
        return bass.AP(POOL, off_f32, [[POOL_N, 128]] + dims).bitcast(BF16)

    def pool3(off, dims):
        return bass.AP(POOL, off, [[POOL_N, 128]] + dims)

    def vop(fn, inc=None, drain=True):
        return P.op("vector", fn, inc=inc, drain=drain)

    def sop(fn, inc=None, drain=True):
        return P.op("scalar", fn, inc=inc, drain=drain)

    def gop(fn, inc=None, drain=True):
        return P.op("gpsimd", fn, inc=inc, drain=drain)

    def top(fn, inc=None):
        return P.op("tensor", fn, inc=inc)

    for src, c0 in ((ba_h, C_B2A), (bx_h, C_B2X), (lam_h, C_LAM)):
        P.dma("sync", CO[:, c0:c0 + 8], bass.AP(src, 0, [[1, 128], [128, 8]]),
              "LD")
    P.dma("sync", co(C_CLIPX), bass.AP(clipx_h, 0, [[0, 128], [1, 1]]), "LD")
    P.dma("sync", co(C_CLIPD), bass.AP(clipd_h, 0, [[0, 128], [1, 1]]), "LD")
    for r in range(8):
        P.dma("sync", co(C_SEL + r), bass.AP(sel_h, r, [[0, 128], [1, 1]]), "LD")
    P.dma("sync", IDT[:], ident_h.ap(), "LD")
    ld_in = P.count("LD")

    for c, val in ((C_EPS, EPS), (C_ONE, 1.0), (C_HALF, 0.5), (C_ZERO, 0.0),
                   (C_MONE, -1.0), (C_MHALF, -0.5), (C_MLOG8H, -LOG8 / 2),
                   (C_TMP, 1.0)):
        P.op("vector", lambda e, c=c, v=val: e.memset(co(c), v))
    vop(lambda e: e.tensor_scalar(co(C_TMP2), co(C_TMP), 2.0**-30, None, ALU.add))
    vop(lambda e: e.tensor_scalar(co(C_FLAG), co(C_TMP2), -1.0, 2.0**30,
                                  ALU.add, ALU.mult))
    vop(lambda e: e.tensor_scalar(co(C_A), co(C_FLAG), 1.5 * 2.0**52,
                                  1.5 * 2.0**23, ALU.mult, ALU.add))
    vop(lambda e: e.tensor_scalar(co(C_NEGA), co(C_A), -1.0, None, ALU.mult))
    vop(lambda e: e.memset(co(CSCL + 7), 0.0))
    P.wait("vector", "LD", ld_in)
    vop(lambda e: e.tensor_scalar(CO[:, C_B2A:C_B2A + 8], CO[:, C_B2A:C_B2A + 8],
                                  0.5, None, ALU.mult))
    vop(lambda e: e.tensor_scalar(CO[:, C_B2X:C_B2X + 8], CO[:, C_B2X:C_B2X + 8],
                                  0.5, None, ALU.mult))
    v_c0 = P.drain_inc("vector", "VEC")

    P.wait("scalar", "LD", ld_in)
    P.wait("scalar", "VEC", v_c0)
    sop(lambda e: e.activation(co(C_TMP), co(C_CLIPX), AF.Tanh,
                               bias=0.0, scale=co(C_HALF)))
    sop(lambda e: e.activation(co(C_TMP2), co(C_CLIPD), AF.Tanh,
                               bias=0.0, scale=co(C_HALF)))
    sop(lambda e: e.activation(CO[:, C_G8:C_G8 + 8], CO[:, C_LAM:C_LAM + 8],
                               AF.Tanh, bias=0.0, scale=co(C_HALF)))
    s_sig = P.drain_inc("scalar", "SCA")
    P.wait("vector", "SCA", s_sig)
    vop(lambda e: e.tensor_scalar(co(C_SIGXO), co(C_TMP), 0.5 / 127.0,
                                  0.5 / 127.0, ALU.mult, ALU.add))
    vop(lambda e: e.tensor_scalar(co(C_SIGDO), co(C_TMP2), 0.5 / 127.0,
                                  0.5 / 127.0, ALU.mult, ALU.add))
    vop(lambda e: e.tensor_scalar(CO[:, C_G8:C_G8 + 8], CO[:, C_G8:C_G8 + 8],
                                  0.5, 0.5, ALU.mult, ALU.add))
    v_sig = P.drain_inc("vector", "VEC")
    P.wait("scalar", "VEC", v_sig)
    sop(lambda e: e.activation(CO[:, C_LNA:C_LNA + 8], CO[:, C_G8:C_G8 + 8],
                               AF.Ln, bias=0.0, scale=1.0))
    s_lna = P.drain_inc("scalar", "SCA")
    P.wait("vector", "SCA", s_lna)
    vop(lambda e: e.tensor_scalar(CO[:, C_LNA:C_LNA + 8], CO[:, C_LNA:C_LNA + 8],
                                  -LOG8 / 2, None, ALU.add))
    P.drain_inc("vector", "VEC")

    state = {"tv": [0, 0], "sv_s": [0, 0], "sv_v": [0, 0], "ps": [0, 0],
             "sw": [0, 0, 0, 0], "first_load_gate": [None], "gbi": 0}
    PRM, PRR, PRI = 168, 172, 176

    def prep_quant(w_hh, img, c_scale, ntiles, width, kind, batch):
        nbat = (ntiles + batch - 1) // batch
        for bi in range(nbat):
            it0 = bi * batch
            nb = min(batch, ntiles - it0)
            w = nb * width
            wslot = bi % 4
            slot = bi % 2
            gw = "GW%d" % state["gbi"]
            state["gbi"] += 1
            P.wait("gpsimd", "SCA", state["sw"][wslot])
            if state["first_load_gate"][0] is not None:
                P.wait("gpsimd", "VEC", state["first_load_gate"][0])
                state["first_load_gate"][0] = None
            P.dma("gpsimd", pool3(P_PW[wslot], [[width, nb], [1, width]]),
                  bass.AP(w_hh, it0 * 128 * width,
                          [[width, 128], [128 * width, nb], [1, width]]),
                  gw)
            P.wait("vector", gw, 16)
            win3 = pool3(P_PW[wslot], [[width, nb], [1, width]])
            vop(lambda e, a=win3, nb=nb: e.tensor_reduce(
                CO[:, PRM:PRM + nb], a, AX.X, ALU.max,
                apply_absolute_value=True))
            vop(lambda e, nb=nb: e.reciprocal(CO[:, PRR:PRR + nb],
                                              CO[:, PRM:PRM + nb]))
            vop(lambda e, nb=nb: e.tensor_scalar(
                CO[:, PRI:PRI + nb], CO[:, PRR:PRR + nb], 127.0, None,
                ALU.mult))
            vop(lambda e, c=c_scale + it0, nb=nb: e.tensor_scalar(
                CO[:, c:c + nb], CO[:, PRM:PRM + nb], 1.0 / 127.0, None,
                ALU.mult))
            v_inv = P.drain_inc("vector", "VEC")
            P.wait("scalar", "VEC", v_inv)
            P.wait("scalar", "TEN", state["tv"][slot])
            for t in range(nb):
                state["sw"][wslot] = sop(
                    lambda e, t=t, ws=wslot, s=slot, width=width: e.activation(
                        pool(P_PQ[s] + t * width, width),
                        pool(P_PW[ws] + t * width, width),
                        AF.Identity, bias=co(C_A), scale=co(PRI + t)),
                    inc="SCA")
            P.wait("tensor", "SCA", state["sw"][wslot])
            nblk = w // 128
            nhalf = (nblk + 7) // 8
            tv = 0
            sv_s = sv_v = 0
            sv_prev = {}
            for half in range(nhalf):
                par2 = half % 2
                pp = pT if par2 == 0 else pT2
                nbk = min(8, nblk - half * 8)
                if half >= 2:
                    P.wait("tensor", sv_prev[half - 2][0], sv_prev[half - 2][1])
                elif half == 0:
                    P.wait("tensor", "SCA", state["sv_s"][1 - slot])
                elif state["sv_v"][1 - slot]:
                    P.wait("tensor", "VEC", state["sv_v"][1 - slot])
                for j in range(nbk):
                    b = half * 8 + j
                    tv = top(lambda e, pp=pp, j=j, b=b, s=slot: e.transpose(
                        pp[:, j * 128:(j + 1) * 128],
                        pool(P_PQ[s] + b * 128, 128), IDT[:]), inc="TEN")
                if par2 == 0:
                    P.wait("scalar", "TEN", tv)
                    P.wait("scalar", "PS%d" % slot, state["ps"][slot])
                    sv_s = sop(lambda e, pp=pp, half=half, nbk=nbk, s=slot:
                               e.activation(
                                   poolb(P_PS[s] + half * 512, [[1, nbk * 64]]),
                                   pp[:, 0:nbk * 128], AF.Identity,
                                   bias=co(C_NEGA), scale=1.0), inc="SCA")
                    sv_prev[half] = ("SCA", sv_s)
                else:
                    P.wait("vector", "TEN", tv)
                    P.wait("vector", "PS%d" % slot, state["ps"][slot])
                    sv_v = vop(lambda e, pp=pp, half=half, nbk=nbk, s=slot:
                               e.tensor_scalar(
                                   poolb(P_PS[s] + half * 512, [[1, nbk * 64]]),
                                   pp[:, 0:nbk * 128], co(C_NEGA), None,
                                   ALU.add), inc="VEC")
                    sv_prev[half] = ("VEC", sv_v)
            state["tv"][slot] = tv
            state["sv_s"][slot] = sv_s
            state["sv_v"][slot] = sv_v
            P.wait("scalar", "SCA", sv_s)
            if sv_v:
                P.wait("scalar", "VEC", sv_v)
            if kind == "gu":
                P.dma("scalar",
                      bass.AP(img, it0 * 131072,
                              [[1024, 128], [131072, nb], [1, 1024]]),
                      poolb(P_PS[slot], [[512, nb], [1, 512]]), "PS%d" % slot)
            else:
                P.wait("vector", "PS%d" % slot, state["ps"][slot])
                mz = vop(lambda e, s=slot: e.memset(
                    pool(P_PS[s] + DI // 2, 1536 - DI // 2), 0.0), inc="VEC")
                P.wait("scalar", "VEC", mz)
                P.dma("scalar",
                      bass.AP(img, 0, [[3072, 128], [1, 3072]]),
                      poolb(P_PS[slot], [[1, 1536]]), "PS%d" % slot)
            state["ps"][slot] = P.count("PS%d" % slot)

    V_SDB = [0]
    PREP_END = {"VEC": 0, "SCA": 0}
    CCD_AG1 = [0]
    CCD_AG2 = [0]

    PREP_N = {}

    XV = {}

    def prep_compute():
        for g in range(8):
            XV[g] = P.dma(
                "sync" if g < 4 else "scalar",
                bass.AP(XR, g * 2048,
                        [[NTT * 1024, 128], [1024, 2], [1, 1024]]),
                bass.AP(x_h, g * 262144,
                        [[1024, 128], [131072, 2], [1, 1024]]),
                "XA%d" % (g // 2))
        state["first_load_gate"][0] = P.count("VEC")
        prep_quant(gus_h, ag1_in, CSCL, 6, 1024, "gu", 4)
        PREP_N["ps0_gu"] = P.count("PS0")
        PREP_N["ps1_gu"] = P.count("PS1")
        prep_quant(wds_h, ag2_in, CSCL + 6, 1, DI, "wd1", 1)
        PREP_N["ps0_wd"] = P.count("PS0")
        PREP_N["ps1_wd"] = P.count("PS1")
        PREP_N["vec"] = P.count("VEC")
        PREP_N["sca"] = P.count("SCA")
        PREP_END["VEC"] = P.count("VEC")
        PREP_END["SCA"] = P.count("SCA")

    def prep_triggers():
        P.wait("sync", "VEC", P.count("VEC"))
        P.wait("sync", "SCA", PREP_N["sca"])
        sv = P.dma("sync", bass.AP(scl_in, 0, [[1, 128], [128, 8]]),
                   CO[:, CSCL:CSCL + 8], "CCS")
        P.wait("gpsimd", "CCS", sv)
        ccd_scl = P.op("gpsimd", lambda e: e.collective_compute(
            "AllGather", ALU.bypass, ins=[scl_in.ap()], outs=[scl_out.ap()],
            replica_groups=RG), inc="CCD")
        P.wait("gpsimd", "PS0", PREP_N["ps0_gu"])
        P.wait("gpsimd", "PS1", PREP_N["ps1_gu"])
        CCD_AG1[0] = P.op("gpsimd", lambda e: e.collective_compute(
            "AllGather", ALU.bypass, ins=[ag1_in.ap()], outs=[ag1_out.ap()],
            replica_groups=RG), inc="CCD")
        P.wait("gpsimd", "PS0", PREP_N["ps0_wd"])
        P.wait("gpsimd", "PS1", PREP_N["ps1_wd"])
        CCD_AG2[0] = P.op("gpsimd", lambda e: e.collective_compute(
            "AllGather", ALU.bypass, ins=[ag2_in.ap()], outs=[ag2_out.ap()],
            replica_groups=RG), inc="CCD")
        P.wait("sync", "CCD", ccd_scl)
        for dst, off, n in ((C_SG + 0, 0, 6), (C_SG + 6, 1024, 6),
                            (C_SG + 12, 2048, 6), (C_SG + 18, 3072, 4),
                            (C_SU + 0, 3584, 2), (C_SU + 2, 4096, 6),
                            (C_SU + 8, 5120, 6), (C_SU + 14, 6144, 6),
                            (C_SU + 20, 7168, 2)):
            P.dma("sync", CO[:, dst:dst + n],
                  bass.AP(scl_out, off, [[1, 128], [128, n]]), "SCL")
        V_SDB[0] = P.dma("sync", SDB[:],
                         bass.AP(scl_out, 768,
                                 [[0, 128], [1024, 8], [1, 128]]), "SCL")
        P.dma("sync", CO[:, CGM8:CGM8 + 8],
              bass.AP(scl_out, 896, [[0, 128], [1024, 8]]), "SCL")
        V_SDB[0] = P.count("SCL")

    PREP_AX_END = {"PS0": 0, "PS1": 0}
    AXW = [20480, 21504]

    AXS = {"vv": [0, 0], "ps": [0, 0]}

    def prep_ax_unit(u):
        w_hh, img = (wa_h, wahl_img) if u < NDG else (wx_h, wxhl_img)
        dt = u % NDG
        slot = u % 2
        if u == 0:
            P.wait("scalar", "TEN", P.count("TEN"))
            AXS["ps"] = [P.count("PS0"), P.count("PS1")]
        P.wait("scalar", "VEC", AXS["vv"][slot])
        P.dma("scalar", pool(AXW[slot], 1024),
              bass.AP(w_hh, dt * 128 * 1024, [[1024, 128], [1, 1024]]),
              "WS%d" % (2 + slot))
        wv = P.count("WS%d" % (2 + slot))
        P.wait("vector", "WS%d" % (2 + slot), wv)
        win = pool(AXW[slot], 1024)
        hic = poolb(P_PS[slot], [[1, 512]])
        loc = poolb(P_PS[slot] + 512, [[1, 512]])
        P.wait("vector", "PS%d" % slot, AXS["ps"][slot])
        vop(lambda e, a=win, h=hic: e.tensor_copy(h, a))
        vop(lambda e, a=win, h=hic, l=loc: e.tensor_tensor(
            l, a, h, ALU.subtract))
        vv = P.drain_inc("vector", "VEC")
        AXS["vv"][slot] = vv
        P.wait("scalar", "VEC", vv)
        for off, src0 in ((0, P_PS[slot]), (1024, P_PS[slot] + 512)):
            P.dma("scalar",
                  bass.AP(img, off + dt * 128,
                          [[2048, 128], [128 * 2048, NDG], [1, 128]]),
                  poolb(src0, [[64, NDG], [1, 64]]), "PS%d" % slot)
        AXS["ps"][slot] = P.count("PS%d" % slot)
        if u == 2 * NDG - 1:
            PREP_AX_END["PS0"] = P.count("PS0")
            PREP_AX_END["PS1"] = P.count("PS1")

    def ln_stats(xwaits=None):
        for tt in range(NTT):
            if xwaits is not None and tt % 2 == 0:
                P.wait("vector", xwaits[tt // 2][0], xwaits[tt // 2][1])
            xs = XR[:, tt * 1024:tt * 1024 + 512]
            xs2 = XR[:, tt * 1024 + 512:tt * 1024 + 1024]
            vop(lambda e, a=xs: e.bn_stats(STT[:, S_BN:S_BN + 6], a))
            P.op("vector", lambda e, a=xs2: e.bn_stats(
                STT[:, S_BN + 6:S_BN + 12], a))
            vop(lambda e, tt=tt: e.bn_aggr(
                STT[:, S_MV + 2 * tt:S_MV + 2 * tt + 2],
                STT[:, S_BN:S_BN + 12]))
        v_st = P.drain_inc("vector", "VEC")
        P.wait("scalar", "VEC", v_st)
        var_ap = bass.AP(STT, S_MV + 1, [[STATS_N, 128], [2, NTT]])
        sop(lambda e: e.activation(STT[:, S_RSTD:S_RSTD + NTT], var_ap, AF.Ln,
                                   bias=co(C_EPS), scale=1.0))
        sop(lambda e: e.activation(STT[:, S_RSTD:S_RSTD + NTT],
                                   STT[:, S_RSTD:S_RSTD + NTT], AF.Exp,
                                   bias=0.0, scale=co(C_MHALF)))
        return P.drain_inc("scalar", "SCA")

    def ln_stats_quarter(q4, absmax=True):
        for tl in range(4):
            tt = q4 * 4 + tl
            xs = XR[:, tt * 1024:tt * 1024 + 512]
            xs2 = XR[:, tt * 1024 + 512:tt * 1024 + 1024]
            vop(lambda e, a=xs: e.bn_stats(STT[:, S_BN:S_BN + 6], a))
            P.op("vector", lambda e, a=xs2: e.bn_stats(
                STT[:, S_BN + 6:S_BN + 12], a))
            vop(lambda e, tt=tt: e.bn_aggr(
                STT[:, S_MV + 2 * tt:S_MV + 2 * tt + 2],
                STT[:, S_BN:S_BN + 12]))
        v_st = P.drain_inc("vector", "VEC")
        P.wait("scalar", "VEC", v_st)
        var_ap = bass.AP(STT, S_MV + 8 * q4 + 1, [[STATS_N, 128], [2, 4]])
        sop(lambda e, q4=q4, va=var_ap: e.activation(
            STT[:, S_RSTD + 4 * q4:S_RSTD + 4 * q4 + 4], va, AF.Ln,
            bias=co(C_EPS), scale=1.0))
        s_r = sop(lambda e, q4=q4: e.activation(
            STT[:, S_RSTD + 4 * q4:S_RSTD + 4 * q4 + 4],
            STT[:, S_RSTD + 4 * q4:S_RSTD + 4 * q4 + 4], AF.Exp,
            bias=0.0, scale=co(C_MHALF)), inc="SCA")
        if not absmax:
            return s_r
        P.wait("vector", "SCA", s_r)
        for tl in range(4):
            tt = q4 * 4 + tl
            vop(lambda e, tt=tt, s=tl % 2: e.tensor_scalar(
                pool(P_UT2[s], 1024), XR[:, tt * 1024:(tt + 1) * 1024],
                stc(S_MV + 2 * tt), stc(S_RSTD + tt),
                ALU.subtract, ALU.mult))
            vop(lambda e, tt=tt, s=tl % 2: e.tensor_reduce(
                stc(S_GM + tt), pool(P_UT2[s], 1024), AX.X, ALU.max,
                apply_absolute_value=True))
        return s_r

    def allreduce_max(src_col, dst_col):
        tv = top(lambda e, s=src_col: e.transpose(pT[0:1, 0:128], s, IDT[:]),
                 inc="TEN")
        P.wait("vector", "TEN", tv)
        vop(lambda e: e.tensor_reduce(CO[0:1, C_TMP:C_TMP + 1],
                                      pT[0:1, 0:128], AX.X, ALU.max))
        v = P.drain_inc("vector", "VEC")
        P.wait("sync", "VEC", v)
        P.wait("sync", "CCD", P.count("CCD"))
        cv = P.dma("sync", cc_in.ap(), CO[0:1, C_TMP:C_TMP + 1], "CCS")
        P.wait("gpsimd", "CCS", cv)
        cc = P.op("gpsimd", lambda e: e.collective_compute(
            "AllReduce", ALU.max, ins=[cc_in.ap()], outs=[cc_out.ap()],
            replica_groups=RG), inc="CCD")
        P.wait("sync", "CCD", cc)
        gv = P.dma("sync", co(dst_col), bass.AP(cc_out, 0, [[0, 128], [1, 1]]),
                   "LD")
        return gv, tv

    def ffn_pass(fi, overlap_fn=None, pre_fn=None, pre=None):
        import os as _os
        _stop = _os.environ.get("KSTOP", "")
        if pre is not None:
            s_rstd = pre["s_rstd"]
            gv = pre["gv"]
            gv_dist = None
            P.wait("vector", "LD", gv)
            P.wait("vector", "SCA", s_rstd)
            vop(lambda e: e.tensor_scalar(co(C_SX), co(C_GMAX), co(C_SIGXO),
                                          None, ALU.mult))
            return ffn_tail(fi, _stop, s_rstd, gv_dist, overlap_fn)
        xw = None
        if pre_fn is not None:
            pre_fn()
            xw = [("XA%d" % (g // 2), XV[g | 1]) for g in range(8)]
        s_rstd = ln_stats(xw)
        P.wait("vector", "SCA", s_rstd)
        for tt in range(NTT):
            slot = tt % 2
            vop(lambda e, tt=tt, s=slot: e.tensor_scalar(
                pool(P_UT[s], 1024), XR[:, tt * 1024:(tt + 1) * 1024],
                stc(S_MV + 2 * tt), stc(S_RSTD + tt),
                ALU.subtract, ALU.mult))
            vop(lambda e, tt=tt, s=slot: e.tensor_reduce(
                stc(S_GM + tt), pool(P_UT[s], 1024), AX.X, ALU.max,
                apply_absolute_value=True))
        vop(lambda e: e.tensor_reduce(co(C_GMAX), STT[:, S_GM:S_GM + NTT],
                                      AX.X, ALU.max))
        v_gm = P.drain_inc("vector", "VEC")
        P.wait("tensor", "VEC", v_gm)
        gv_dist = None
        if overlap_fn is not None:
            tv = top(lambda e: e.transpose(pT[0:1, 0:128], co(C_GMAX),
                                           IDT[:]), inc="TEN")
            P.wait("vector", "TEN", tv)
            vop(lambda e: e.tensor_reduce(CO[0:1, CSCL + 7:CSCL + 8],
                                          pT[0:1, 0:128], AX.X, ALU.max))
            P.drain_inc("vector", "VEC")
            overlap_fn()
            gv_dist = P.count("SCL")
            P.wait("vector", "SCL", gv_dist)
            vop(lambda e: e.tensor_reduce(co(C_GMAX), CO[:, CGM8:CGM8 + 8],
                                          AX.X, ALU.max))
        else:
            gv, _ = allreduce_max(co(C_GMAX), C_GMAX)
            P.wait("vector", "LD", gv)
        vop(lambda e: e.tensor_scalar(co(C_SX), co(C_GMAX), co(C_SIGXO), None,
                                      ALU.mult))
        return ffn_tail(fi, _stop, s_rstd, gv_dist, overlap_fn)

    def ffn_tail(fi, _stop, s_rstd, gv_dist, overlap_fn):
        vop(lambda e: e.reciprocal(co(C_INVS), co(C_SX)))
        vop(lambda e: e.tensor_scalar(STT[:, S_RSTDI:S_RSTDI + NTT],
                                      STT[:, S_RSTD:S_RSTD + NTT],
                                      co(C_INVS), None, ALU.mult))
        P.drain_inc("vector", "VEC")

        if _stop == "ffnAmax":
            for it in range(NIT):
                P.op("vector", lambda e, it=it: e.tensor_scalar(
                    co(C_SGX + it), co(C_SG + it), co(C_SX), None, ALU.mult))
                P.op("vector", lambda e, it=it: e.tensor_scalar(
                    co(C_SUX + it), co(C_SU + it), co(C_SX), None, ALU.mult))
            P.drain_inc("vector", "VEC")
            return
        t_xp = [0] * NTT
        s_ev = [0] * NTT
        g_cl = [0] * NTT
        P.wait("vector", "TEN", P.count("TEN"))
        P.wait("scalar", "WS0", P.count("WS0"))
        P.wait("scalar", "WS1", P.count("WS1"))
        for tt in range(NTT):
            slot = tt % 2
            vop(lambda e, tt=tt, s=slot: e.tensor_scalar(
                pool(P_UT[s], 1024), XR[:, tt * 1024:(tt + 1) * 1024],
                stc(S_MV + 2 * tt), stc(S_RSTDI + tt),
                ALU.subtract, ALU.mult))
            vop(lambda e, s=slot: e.tensor_scalar(
                pool(P_QTS[s], 1024), pool(P_UT[s], 1024),
                co(C_A), co(C_A), ALU.add, ALU.subtract))
            if tt >= 2:
                P.wait("vector", "TEN", t_xp[tt - 2])
            vop(lambda e, s=slot: e.tensor_scalar(
                pool(P_QCS[s], 1024), pool(P_QTS[s], 1024),
                -128.0, 127.0, ALU.max, ALU.min))
            g_cl[tt] = P.drain_inc("vector", "VEC")
            pp = pT if slot == 0 else pT2
            P.wait("tensor", "VEC", g_cl[tt])
            if tt >= 2:
                P.wait("tensor", "SCA", s_ev[tt - 2])
            for j in range(8):
                t_xp[tt] = top(lambda e, pp=pp, j=j, s=slot: e.transpose(
                    pp[:, j * 128:(j + 1) * 128],
                    pool(P_QCS[s] + j * 128, 128), IDT[:]),
                    inc="TEN" if j == 7 else None)
            P.wait("scalar", "TEN", t_xp[tt])
            s_ev[tt] = sop(lambda e, pp=pp, tt=tt: e.activation(
                poolb(P_XQT + tt * 64, [[1024, 8], [1, 64]]),
                bass.AP(pp, 0, [[1024, 128], [128, 8], [1, 128]]),
                AF.Identity, bias=0.0, scale=1.0), inc="SCA")

        if gv_dist is not None:
            P.wait("vector", "SCL", gv_dist)
        for it in range(NIT):
            P.op("vector", lambda e, it=it: e.tensor_scalar(
                co(C_SGX + it), co(C_SG + it), co(C_SX), None, ALU.mult))
            P.op("vector", lambda e, it=it: e.tensor_scalar(
                co(C_SUX + it), co(C_SU + it), co(C_SX), None, ALU.mult))
        v_scales = P.drain_inc("vector", "VEC")
        if _stop == "ffnB":
            return
        vop(lambda e: e.memset(stc(S_HM), 0.0))
        P.wait("sync", "VEC", PREP_END["VEC"])
        P.wait("sync", "SCA", PREP_END["SCA"])
        P.wait("sync", "CCD", CCD_AG1[0])
        t_xp_last = t_xp[NTT - 1]
        v_bq_done = P.count("VEC")
        w_v = [0] * NIT
        for it in (0, 1):
            slot = it % 2
            P.dma("sync", poolb(P_WRG[slot], [[1, 512]]),
                  bass.AP(ag1_out, it * 131072, [[1024, 128], [1, 1024]]),
                  "WS%d" % slot)
            w_v[it] = P.dma("sync", poolb(P_WRU[slot], [[1, 512]]),
                            bass.AP(ag1_out, (NIT + it) * 131072,
                                    [[1024, 128], [1, 1024]]), "WS%d" % slot)
        NK = NIT * 4
        t_g = [0] * NK
        t_u = [0] * NK
        s_si = [0] * NK
        v_h = [0] * NK
        g_h = [0] * NK
        hsp = [0] * NK
        t_last_it = [0] * NIT
        P.wait("scalar", "VEC", v_scales)
        for it in range(NIT):
            slot = it % 2
            if overlap_fn is not None and 2 <= it < 2 + 2 * NDG:
                prep_ax_unit(it - 2)
            if it >= 2:
                P.wait("sync", "TEN", t_last_it[it - 2])
                P.dma("sync", poolb(P_WRG[slot], [[1, 512]]),
                      bass.AP(ag1_out, it * 131072, [[1024, 128], [1, 1024]]),
                      "WS%d" % slot)
                w_v[it] = P.dma("sync", poolb(P_WRU[slot], [[1, 512]]),
                                bass.AP(ag1_out, (NIT + it) * 131072,
                                        [[1024, 128], [1, 1024]]),
                                "WS%d" % slot)
            for tc in range(4):
                k = it * 4 + tc
                pg = pAB[2 * (k % 2)]
                pu = pAB[2 * (k % 2) + 1]
                P.wait("tensor", "WS%d" % slot, w_v[it])
                if it == 0:
                    P.wait("tensor", "SCA", s_ev[4 * tc + 3])
                if k >= 2:
                    P.wait("tensor", "SCA", s_si[k - 2])
                    P.wait("tensor", "VEC", v_h[k - 2])
                for g in range(NDG):
                    t_g[k] = top(lambda e, g=g, s=slot, tc=tc, pg=pg: e.matmul(
                        pg[:], poolb(P_WRG[s] + g * 64, [[1, 64]]),
                        poolb(P_XQT + g * 1024 + tc * 256, [[1, 256]]),
                        start=(g == 0), stop=(g == NDG - 1)),
                        inc="TEN" if g == NDG - 1 else None)
                for g in range(NDG):
                    t_u[k] = top(lambda e, g=g, s=slot, tc=tc, pu=pu: e.matmul(
                        pu[:], poolb(P_WRU[s] + g * 64, [[1, 64]]),
                        poolb(P_XQT + g * 1024 + tc * 256, [[1, 256]]),
                        start=(g == 0), stop=(g == NDG - 1)),
                        inc="TEN" if g == NDG - 1 else None)
                if tc == 3:
                    t_last_it[it] = t_u[k]
                P.wait("scalar", "TEN", t_g[k])
                if k >= 4:
                    P.wait("scalar", "VEC", v_h[k - 4])
                s_si[k] = sop(lambda e, k=k, it=it, pg=pg: e.activation(
                    pool(P_SIL + (k % 4) * 512, 512), pg[:], AF.Silu,
                    bias=0.0, scale=co(C_SGX + it)), inc="SCA")
                P.wait("vector", "TEN", t_u[k])
                P.wait("vector", "SCA", s_si[k])
                if k >= 4:
                    P.wait("vector", "HS%d" % (k % 4), hsp[k - 4])
                v_h[k] = vop(lambda e, k=k, it=it, pu=pu: e.scalar_tensor_tensor(
                    pool(P_HT + (k % 4) * 512, 512), pu[:], co(C_SUX + it),
                    pool(P_SIL + (k % 4) * 512, 512), ALU.mult, ALU.mult),
                    inc="VEC")
                vop(lambda e, k=k: e.tensor_reduce(
                    stc(S_T1), pool(P_HT + (k % 4) * 512, 512), AX.X, ALU.max,
                    apply_absolute_value=True))
                vop(lambda e: e.tensor_tensor(stc(S_HM), stc(S_HM), stc(S_T1),
                                              ALU.max))
                g_h[k] = P.drain_inc("vector", "VEC")
                P.wait("sync", "VEC", v_h[k])
                hsp[k] = P.dma("sync",
                               bass.AP(h_spill, (tc * NIT + it) * 65536,
                                       [[512, 128], [1, 512]]),
                               pool(P_HT + (k % 4) * 512, 512), "HS%d" % (k % 4))

        if _stop == "ffnC":
            return
        for q in ("sync", "scalar"):
            P.wait(q, "TEN", t_xp_last)
            P.wait(q, "VEC", v_bq_done)
            P.wait(q, "PS0", P.count("PS0"))
            P.wait(q, "PS1", P.count("PS1"))
            P.wait(q, "CCD", CCD_AG2[0])
        wdt_v = 0
        for t8 in range(8):
            wdt_v = P.dma("sync" if t8 % 2 == 0 else "scalar",
                          poolb(P_WDT + t8 * 1408, [[1, 1408]]),
                          bass.AP(ag2_out, 3 * t8 * 131072,
                                  [[3072, 128], [1, 2816]]), "WDS")
        P.wait("tensor", "VEC", g_h[NK - 1])
        gv2, _ = allreduce_max(stc(S_HM), C_GMAX)
        P.wait("vector", "LD", gv2)
        P.wait("vector", "SCL", V_SDB[0])
        vop(lambda e: e.tensor_scalar(co(C_SX), co(C_GMAX), co(C_SIGDO), None,
                                      ALU.mult))
        vop(lambda e: e.reciprocal(co(C_INVS), co(C_SX)))
        vop(lambda e: e.tensor_scalar(co(C_HALFSH), co(C_SX), 0.5, None,
                                      ALU.mult))
        vop(lambda e: e.tensor_scalar(SDBF[:], SDB[:], co(C_HALFSH), None,
                                      ALU.mult))
        v_sdbf = P.drain_inc("vector", "VEC")
        P.wait("sync", "TEN", t_last_it[NIT - 1])
        P.wait("sync", "VEC", g_h[NK - 1])
        P.wait("sync", "SCA", s_si[NK - 1])
        hrv = [[0, 0] for _ in range(4)]
        g_hq = [[0] * NIT for _ in range(4)]
        t_dm = [0] * NTT
        v_ep = [0] * NTT
        first_q = True
        prev_vdq = None
        for s4 in range(4):
            P.wait("sync", "HS%d" % s4, P.count("HS%d" % s4))
        dps = [pT[:, 0:512], pT[:, 512:1024], pT2[:, 0:512], pT2[:, 512:1024],
               pAB[0][:], pAB[1][:], pAB[2][:], pAB[3][:]]
        t_mm = [[0] * NIT for _ in range(4)]
        v_ep2 = [[0] * 4 for _ in range(4)]

        def d_readback(tc):
            for hf, q in ((0, "sync"), (1, "scalar")):
                if tc >= 1:
                    P.wait(q, "SCA", prev_sdq[hf * 11 + 10])
                elif hf == 1:
                    P.wait(q, "TEN", t_last_it[NIT - 1])
                    P.wait(q, "VEC", g_h[NK - 1])
                    for s4 in range(4):
                        P.wait(q, "HS%d" % s4, P.count("HS%d" % s4))
                hrv[tc][hf] = P.dma(
                    q, pool3(P_HRD + hf * 11 * 512, [[512, 11], [1, 512]]),
                    bass.AP(h_spill, (tc * NIT + hf * 11) * 65536,
                            [[512, 128], [65536, 11], [1, 512]]), "HR%d" % hf)

        d_readback(0)
        prev_sdq = None
        for tc in range(4):
            v_dq = [0] * NIT
            s_dq = [0] * NIT
            for ig in range(NIT):
                P.wait("scalar", "HR%d" % (ig // 11), hrv[tc][ig // 11])
                if first_q:
                    P.wait("scalar", "VEC", v_sdbf)
                    for s4 in range(4):
                        P.wait("vector", "HS%d" % s4, P.count("HS%d" % s4))
                    first_q = False
                if ig >= 2:
                    P.wait("scalar", "VEC", v_dq[ig - 2])
                elif prev_sdq is not None:
                    P.wait("scalar", "VEC", prev_vdq[NIT - 2 + ig])
                s_dq[ig] = sop(lambda e, ig=ig: e.activation(
                    pool(P_DQ[ig % 2], 512), pool(P_HRD + ig * 512, 512),
                    AF.Identity, bias=co(C_A), scale=co(C_INVS)), inc="SCA")
                P.wait("vector", "SCA", s_dq[ig])
                if tc >= 1:
                    P.wait("vector", "TEN", t_mm[tc - 1][ig])
                vop(lambda e, ig=ig: e.tensor_scalar(
                    pool(P_DQ[ig % 2], 512), pool(P_DQ[ig % 2], 512),
                    co(C_A), -128.0, ALU.subtract, ALU.max), drain=True)
                vop(lambda e, ig=ig: e.tensor_single_scalar(
                    poolb(P_HQ + ig * 256, [[1, 256]]),
                    pool(P_DQ[ig % 2], 512), 127.0, ALU.min))
                v_dq[ig] = P.drain_inc("vector", "VEC")
                g_hq[tc][ig] = v_dq[ig]
                P.wait("tensor", "VEC", v_dq[ig])
                if ig == 0:
                    if tc == 0:
                        P.wait("tensor", "WDS", wdt_v)
                    else:
                        for tq2 in range(4):
                            P.wait("tensor", "VEC", v_ep2[tc - 1][tq2])
                for tq in range(4):
                    for half in range(2):
                        ps = dps[(2 * tq + half) % 8]
                        t_mm[tc][ig] = top(
                            lambda e, ig=ig, tq=tq, half=half, ps=ps: e.matmul(
                                ps, poolb(P_HQ + ig * 256 + tq * 64, [[1, 64]]),
                                poolb(P_WDT + half * 4 * 1408 + ig * 64,
                                      [[1408, 4], [1, 64]]),
                                start=(ig == 0), stop=(ig == NIT - 1)),
                            inc="TEN" if half == 1 and tq == 3 else None)
            prev_sdq = s_dq
            prev_vdq = v_dq
            if tc < 3:
                d_readback(tc + 1)
            for tq in range(4):
                tt = tc * 4 + tq
                P.wait("vector", "TEN", t_mm[tc][NIT - 1])
                for half in range(2):
                    ps = dps[(2 * tq + half) % 8]
                    vop(lambda e, ps=ps, half=half: e.tensor_tensor(
                        pool(P_TD + half * 512, 512), ps,
                        SDBF[:, half * 512:(half + 1) * 512], ALU.mult))
                vop(lambda e, tt=tt: e.tensor_tensor(
                    XR[:, tt * 1024:(tt + 1) * 1024],
                    XR[:, tt * 1024:(tt + 1) * 1024],
                    pool(P_TD, 1024), ALU.add))
                v_ep2[tc][tq] = P.drain_inc("vector", "VEC")
            if fi == 0:
                ATTN_RSTD[0] = ln_stats_quarter(tc, absmax=False)
        return v_ep2

    ATTN_RSTD = [0]

    def attn_pass():
        s_rstd = ATTN_RSTD[0]
        if not s_rstd:
            s_rstd = ln_stats()
        P.wait("vector", "SCA", s_rstd)
        for g in range(8):
            P.op("vector", lambda e, g=g: e.memset(co(C_YL + g), 0.0))
            P.op("vector", lambda e, g=g: e.memset(co(C_AL + g), 1.0))
        v_init = P.drain_inc("vector", "VEC")
        aw_v = [0] * 8
        t_ri = [0] * 8
        ac_v = [0] * NQ
        st = {
            "v_scan": [0] * 8,
            "t_y": [0] * QTT,
            "v_xr": [0] * QTT,
            "s_at": [0] * 8,
            "s_tb": [0] * 8,
        }

        def emit_A(qq, prev_t_ut):
            par = qq % 2
            s_uhi = [0] * QTT
            v_ulo = [0] * QTT
            t_ut = [0] * QTT
            for tl in range(QTT):
                ti = qq * QTT + tl
                slot = tl % 2
                pp = pT if slot == 0 else pT2
                if tl >= 2:
                    P.wait("vector", "TEN", t_ut[tl - 2])
                elif prev_t_ut is not None:
                    P.wait("vector", "TEN", prev_t_ut[tl + 2])
                vop(lambda e, ti=ti, s=slot: e.tensor_scalar(
                    pool(P_UT2[s], 1024), XR[:, ti * 1024:(ti + 1) * 1024],
                    stc(S_MV + 2 * ti), stc(S_RSTD + ti),
                    ALU.subtract, ALU.mult))
                vu = P.drain_inc("vector", "VEC")
                P.wait("tensor", "VEC", vu)
                if tl >= 2:
                    P.wait("tensor", "SCA", s_uhi[tl - 2])
                    P.wait("tensor", "VEC", v_ulo[tl - 2])
                else:
                    P.wait("tensor", "VEC", st["v_xr"][2 + tl])
                for j in range(8):
                    t_ut[tl] = top(lambda e, pp=pp, j=j, s=slot: e.transpose(
                        pp[:, j * 128:(j + 1) * 128],
                        pool(P_UT2[s] + j * 128, 128), IDT[:]),
                        inc="TEN" if j == 7 else None)
                P.wait("scalar", "TEN", t_ut[tl])
                s_uhi[tl] = sop(lambda e, pp=pp, tl=tl, par=par: e.activation(
                    poolb(P_UHI[par] + tl * 64, [[256, 8], [1, 64]]),
                    bass.AP(pp, 0, [[1024, 128], [128, 8], [1, 128]]),
                    AF.Identity, bias=0.0, scale=1.0), inc="SCA")
                P.wait("vector", "SCA", s_uhi[tl])
                v_ulo[tl] = vop(lambda e, pp=pp, tl=tl, par=par: e.tensor_tensor(
                    poolb(P_ULO[par] + tl * 64, [[256, 8], [1, 64]]),
                    bass.AP(pp, 0, [[1024, 128], [128, 8], [1, 128]]),
                    poolb(P_UHI[par] + tl * 64, [[256, 8], [1, 64]]),
                    ALU.subtract), inc="VEC")
            return s_uhi, v_ulo, t_ut

        def aw_load(ht, q="sync", sem=None):
            slot = ht % 2
            sem = sem or "WS%d" % slot
            P.dma(q, poolb(P_AWR[slot], [[1, 1024]]),
                  bass.AP(wahl_img, ht * 262144, [[2048, 128], [1, 2048]]),
                  sem)
            v = P.dma(q, poolb(P_AWR[slot] + 1024, [[1, 1024]]),
                      bass.AP(wxhl_img, ht * 262144,
                              [[2048, 128], [1, 2048]]),
                      sem)
            aw_v[ht] = (sem, v)

        def emit_MM(qq, v_ulo, ht0, ht1, s_at, s_tb, binter=None):
            par = qq % 2
            if qq == 0 and ht0 == 0:
                P.wait("sync", "PS0", PREP_AX_END["PS0"])
                P.wait("sync", "PS1", PREP_AX_END["PS1"])
                aw_load(0)
                aw_load(1)
            for ht in range(ht0, ht1):
                slot = ht % 2
                if ht >= 2:
                    P.wait("sync", "TEN", t_ri[ht - 2])
                    aw_load(ht)
                pr = pAB[2 * (ht % 2)]
                pi = pAB[2 * (ht % 2) + 1]
                P.wait("tensor", aw_v[ht][0], aw_v[ht][1])
                if ht == 0:
                    P.wait("tensor", "VEC", v_ulo[QTT - 1])
                if ht >= 2:
                    P.wait("tensor", "SCA", s_at[ht - 2])
                    P.wait("tensor", "SCA", s_tb[ht - 2])
                elif qq > 0:
                    P.wait("tensor", "SCA", st["s_at"][ht + 6])
                    P.wait("tensor", "SCA", st["s_tb"][ht + 6])
                mmn = 0
                for woff, uoff in ((0, P_UHI[par]), (0, P_ULO[par]),
                                   (512, P_UHI[par])):
                    for g in range(NDG):
                        mmn += 1
                        t_ri[ht] = top(
                            lambda e, g=g, s=slot, wo=woff, uo=uoff, pr=pr,
                            mm=mmn: e.matmul(
                                pr[:], poolb(P_AWR[s] + wo + g * 64, [[1, 64]]),
                                poolb(uo + g * 256, [[1, 256]]),
                                start=(mm == 1), stop=(mm == 24)),
                            inc="TEN" if mmn == 24 else None)
                P.wait("scalar", "TEN", t_ri[ht])
                if qq > 0:
                    P.wait("scalar", "VEC", st["v_scan"][ht])
                    if ht == 0:
                        P.wait("scalar", "ACS", ac_v[qq - 1][0])
                    elif ht == 4:
                        P.wait("scalar", "ACS2", ac_v[qq - 1][1])
                sop(lambda e, pr=pr, ht=ht: e.activation(
                    pool(P_SC[0], 512), pr[:], AF.Tanh,
                    bias=co(C_B2A + ht), scale=co(C_HALF)))
                s_at[ht] = sop(lambda e, ht=ht: e.activation(
                    pool(P_AT + ht * 512, 512), pool(P_SC[0], 512), AF.Exp,
                    bias=co(C_LNA + ht), scale=co(C_MLOG8H)), inc="SCA")
                mmn = 0
                for woff, uoff in ((1024, P_UHI[par]), (1024, P_ULO[par]),
                                   (1536, P_UHI[par])):
                    for g in range(NDG):
                        mmn += 1
                        t_ri[ht] = top(
                            lambda e, g=g, s=slot, wo=woff, uo=uoff, pi=pi,
                            mm=mmn: e.matmul(
                                pi[:], poolb(P_AWR[s] + wo + g * 64, [[1, 64]]),
                                poolb(uo + g * 256, [[1, 256]]),
                                start=(mm == 1), stop=(mm == 24)),
                            inc="TEN" if mmn == 24 else None)
                P.wait("scalar", "TEN", t_ri[ht])
                s_tb[ht] = sop(lambda e, pi=pi, ht=ht: e.activation(
                    pool(P_BT + ht * 512, 512), pi[:], AF.Tanh,
                    bias=co(C_B2X + ht), scale=co(C_HALF)), inc="SCA")
                if binter is not None and ht >= 4:
                    emit_B_ht(qq, ht - 4, s_at, s_tb, binter[0], binter[1])
            if ht1 == 8 and binter is not None:
                for bh in range(ht1 - 4, 8):
                    emit_B_ht(qq, bh, s_at, s_tb, binter[0], binter[1])
            if ht1 == 8 and qq + 1 < NQ:
                gw = "GW%d" % qq
                P.wait("gpsimd", gw, P.count(gw))
                P.wait("gpsimd", "TEN", t_ri[6])
                aw_load(0, q="gpsimd", sem=gw)
                P.wait("gpsimd", gw, P.count(gw))
                P.wait("gpsimd", "TEN", t_ri[7])
                aw_load(1, q="gpsimd", sem=gw)
            return s_at, s_tb

        def emit_B_ht(qq, ht, s_at, s_tb, v_scan, v_b1):
            par = qq % 2
            if True:
                sc = P_SCP[ht % 2]
                P.wait("scalar", "VEC",
                       v_b1[ht - 2] if ht >= 2 else st["v_scan"][6 + (ht % 2)])
                P.wait("scalar", "SCA", s_tb[ht])
                sop(lambda e, ht=ht, sc=sc: e.activation(
                    pool(sc[0], 512), pool(P_AT + ht * 512, 512), AF.Square,
                    bias=0.0, scale=1.0))
                sop(lambda e, sc=sc: e.activation(
                    pool(sc[1], 512), pool(sc[0], 512), AF.Ln,
                    bias=co(C_ONE), scale=co(C_MONE)))
                s_s = sop(lambda e, sc=sc: e.activation(
                    pool(sc[2], 512), pool(sc[1], 512), AF.Exp,
                    bias=0.0, scale=co(C_HALF)), inc="SCA")
                P.wait("vector", "SCA", s_s)
                vop(lambda e, ht=ht, par=par: e.tensor_tensor(
                    pool(P_SC[3], 512),
                    poolb(P_UHI[par] + ht * 256, [[1, 256]]),
                    poolb(P_ULO[par] + ht * 256, [[1, 256]]), ALU.add))
                vop(lambda e, ht=ht, sc=sc: e.scalar_tensor_tensor(
                    pool(P_SC[4], 512), pool(P_BT + ht * 512, 512),
                    co(C_ONE), pool(sc[2], 512), ALU.add, ALU.mult))
                v_b1[ht] = P.drain_inc("vector", "VEC")
                vop(lambda e, ht=ht: e.scalar_tensor_tensor(
                    pool(P_BT + ht * 512, 512), pool(P_SC[4], 512),
                    co(C_HALF), pool(P_SC[3], 512), ALU.mult, ALU.mult))
                if ht == 0:
                    P.wait("vector", "TEN", st["t_y"][QTT - 1])
                vop(lambda e, ht=ht: e.tensor_tensor_scan(
                    pool(P_Y + ht * 512, 512), pool(P_AT + ht * 512, 512),
                    pool(P_BT + ht * 512, 512), co(C_YL + ht),
                    ALU.mult, ALU.add))
                vop(lambda e, ht=ht: e.tensor_copy(
                    co(C_YL + ht), pool(P_Y + ht * 512 + 511, 1)))
                vop(lambda e, ht=ht: e.tensor_tensor_scan(
                    pool(P_BT + ht * 512, 512), pool(P_AT + ht * 512, 512),
                    cob(C_ZERO, 512), co(C_AL + ht), ALU.mult, ALU.add))
                vop(lambda e, ht=ht: e.tensor_copy(
                    co(C_AL + ht), pool(P_BT + ht * 512 + 511, 1)))
                v_scan[ht] = P.drain_inc("vector", "VEC")
            if ht == 3:
                P.wait("sync", "VEC", v_scan[3])
                a0 = P.dma("sync",
                           bass.AP(ac_spill, qq * 524288,
                                   [[4096, 128], [1, 2048]]),
                           pool(P_BT, 2048), "ACS")
                ac_v[qq] = (a0, 0)
            if ht == 7:
                P.wait("sync", "VEC", v_scan[7])
                a1 = P.dma("sync",
                           bass.AP(ac_spill, qq * 524288 + 2048,
                                   [[4096, 128], [1, 2048]]),
                           pool(P_BT + 2048, 2048), "ACS2")
                ac_v[qq] = (ac_v[qq][0], a1)

        def emit_EPI(qq, uhi_gate, v_scan_q, src_off=P_Y, v_gate=None):
            t_y = [0] * QTT
            v_xr = [0] * QTT
            for tl in range(QTT):
                ti = qq * QTT + tl
                pp = pT if tl % 2 == 0 else pT2
                if tl == 0:
                    if v_gate is not None:
                        P.wait("tensor", "VEC", v_gate)
                    else:
                        P.wait("tensor", "VEC", v_scan_q[7])
                    if uhi_gate is not None:
                        P.wait("tensor", "SCA", uhi_gate[0][QTT - 1])
                        P.wait("tensor", "VEC", uhi_gate[1][QTT - 1])
                if tl >= 2:
                    P.wait("tensor", "VEC", v_xr[tl - 2])
                for j in range(8):
                    t_y[tl] = top(lambda e, pp=pp, j=j, tl=tl, so=src_off:
                                  e.transpose(
                                      pp[:, j * 128:(j + 1) * 128],
                                      pool(so + j * 512 + tl * 128, 128),
                                      IDT[:]),
                                  inc="TEN" if j == 7 else None)
                P.wait("vector", "TEN", t_y[tl])
                v_xr[tl] = vop(lambda e, ti=ti, pp=pp: e.tensor_tensor(
                    XR[:, ti * 1024:(ti + 1) * 1024],
                    XR[:, ti * 1024:(ti + 1) * 1024], pp[:], ALU.add),
                    inc="VEC")
            return t_y, v_xr

        cur = emit_A(0, None)
        pend = None
        for qq in range(NQ):
            s_at, s_tb = emit_MM(qq, cur[1], 0, 2, [0] * 8, [0] * 8)
            if pend is not None:
                st["t_y"], st["v_xr"] = emit_EPI(
                    pend[0], (cur[0], cur[1]), pend[1])
            if qq + 1 < NQ:
                nxt = emit_A(qq + 1, cur[2])
            else:
                nxt = cur
            v_scan = [0] * 8
            v_b1 = [0] * 8
            s_at, s_tb = emit_MM(qq, cur[1], 2, 8, s_at, s_tb,
                                 binter=(v_scan, v_b1))
            st["v_scan"] = v_scan
            st["s_at"] = s_at
            st["s_tb"] = s_tb
            pend = (qq, v_scan)
            cur = nxt
        st["t_y"], st["v_xr"] = emit_EPI(pend[0], None, pend[1])
        P.wait("sync", "VEC", st["v_scan"][7])
        cv = P.dma("sync", bass.AP(ag_in, 0, [[1, 128], [128, 8]]),
                   CO[:, C_YL:C_YL + 8], "CCS")
        P.wait("gpsimd", "CCS", cv)
        agv = P.op("gpsimd", lambda e: e.collective_compute(
            "AllGather", ALU.bypass, ins=[ag_in.ap()], outs=[ag_out.ap()],
            replica_groups=RG), inc="CCD")
        rbuf = [P_BT, 0, P_AT, P_SC[0]]
        ybuf = [P_Y, P_AWR[0]]
        rb = [0] * NQ
        v_fq = [0] * NQ
        tys = [None] * NQ
        P.wait("sync", "ACS", ac_v[NQ - 1][0])
        P.wait("sync", "ACS2", ac_v[NQ - 1][1])
        P.wait("scalar", "ACS", ac_v[NQ - 1][0])
        P.wait("scalar", "ACS2", ac_v[NQ - 1][1])
        for qq in range(NQ):
            rb[qq] = P.dma("sync" if qq % 2 == 0 else "scalar",
                           pool(rbuf[qq], 4096),
                           bass.AP(ac_spill, qq * 524288,
                                   [[4096, 128], [1, 4096]]),
                           "HR%d" % qq)
        P.wait("sync", "CCD", agv)
        for r in range(8):
            g8v = P.dma("sync", CO[:, C_G8 + r * 8:C_G8 + r * 8 + 8],
                        bass.AP(ag_out, r * 1024, [[1, 128], [128, 8]]), "LD")
        P.wait("vector", "LD", g8v)
        vop(lambda e: e.memset(CO[:, C_CC:C_CC + 8], 0.0))
        for r in range(8):
            vop(lambda e, r=r: e.scalar_tensor_tensor(
                CO[:, C_CC:C_CC + 8],
                CO[:, C_G8 + r * 8:C_G8 + r * 8 + 8],
                co(C_SEL + r), CO[:, C_CC:C_CC + 8], ALU.mult, ALU.add))
        v_cc = P.drain_inc("vector", "VEC")
        for qq in range(NQ):
            par = qq % 2
            P.wait("vector", "HR%d" % qq, rb[qq])
            P.wait("vector", "VEC", v_cc)
            if qq == 0:
                P.wait("vector", "TEN", st["t_y"][QTT - 1])
            elif qq >= 2:
                P.wait("vector", "TEN", tys[qq - 2][QTT - 1])
            for ht in range(8):
                P.op("vector", lambda e, ht=ht, qq=qq, par=par:
                     e.tensor_scalar(
                         pool(ybuf[par] + ht * 512, 512),
                         pool(rbuf[qq] + ht * 512, 512),
                         co(C_CC + ht), None, ALU.mult))
            v_fq[qq] = P.drain_inc("vector", "VEC")
            tys[qq], st["v_xr"] = emit_EPI(qq, None, None,
                                           src_off=ybuf[par],
                                           v_gate=v_fq[qq])
            s_r2 = ln_stats_quarter(qq)
        st["t_y"] = tys[NQ - 1]
        vop(lambda e: e.tensor_reduce(co(C_GMAX), STT[:, S_GM:S_GM + NTT],
                                      AX.X, ALU.max))
        v_gm2 = P.drain_inc("vector", "VEC")
        P.wait("tensor", "VEC", v_gm2)
        gv2p, _ = allreduce_max(co(C_GMAX), C_GMAX)
        return {"s_rstd": s_r2, "gv": gv2p}

    import os
    stop = os.environ.get("KSTOP", "")
    P.barrier()
    if stop != "ln":
        if stop == "ffnA":
            ln_stats()
        else:
            ffn_pass(0, overlap_fn=prep_triggers, pre_fn=prep_compute)
            P.barrier()
            if stop not in ("ffn0", "ffnAmax", "ffnB", "ffnC", "ffnD"):
                attn_pre = attn_pass()
                P.barrier()
                if stop != "attn":
                    v_ep2f = ffn_pass(1, pre=attn_pre)
                    for tc in range(4):
                        q = "sync" if tc % 2 == 0 else "scalar"
                        P.wait(q, "VEC", v_ep2f[tc][3])
                        P.dma(q,
                              bass.AP(y_h, tc * 524288,
                                      [[1024, 128], [131072, 4], [1, 1024]]),
                              bass.AP(XR, tc * 4096,
                                      [[NTT * 1024, 128], [1024, 4],
                                       [1, 1024]]), "STO")
                    P.barrier()
    if stop in ("ln", "ffnA", "ffn0", "ffnAmax", "ffnB", "ffnC", "ffnD",
                "attn"):
        P.dma("sync",
              bass.AP(y_h, 0, [[1024, 128], [131072, NTT], [1, 1024]]),
              bass.AP(XR, 0, [[NTT * 1024, 128], [1024, NTT], [1, 1024]]),
              "STO")
        P.wait("sync", "STO", 16)
    else:
        P.wait("sync", "STO", 64)

    with nc.allow_non_contiguous_dma("small vector loads"), nc.Block() as block:
        @block.sync
        def _(e):
            for f in P.ops["sync"]:
                f(e)

        @block.vector
        def _(e):
            for f in P.ops["vector"]:
                f(e)

        @block.scalar
        def _(e):
            for f in P.ops["scalar"]:
                f(e)

        @block.tensor
        def _(e):
            for f in P.ops["tensor"]:
                f(e)

        @block.gpsimd
        def _(e):
            for f in P.ops["gpsimd"]:
                f(e)

    return nc


_CACHED = {}


def _get_nc():
    if "nc" not in _CACHED:
        _CACHED["nc"] = build_program()
    return _CACHED["nc"]


def kernel(L0_attn_x, L0_attn_Wa, L0_attn_Wx, L0_attn_ba, L0_attn_bx,
           L0_attn_Lam, L0_ffn_clip_x, L0_ffn_clip_down, L0_ffn_w_gate,
           L0_ffn_w_up, L0_ffn_w_down):
    x = np.ascontiguousarray(np.asarray(L0_attn_x, dtype=np.float32))
    B, T, _ = x.shape
    wg = np.ascontiguousarray(L0_ffn_w_gate, dtype=np.float32)
    wu = np.ascontiguousarray(L0_ffn_w_up, dtype=np.float32)
    wd = np.ascontiguousarray(L0_ffn_w_down, dtype=np.float32)
    wa = np.ascontiguousarray(L0_attn_Wa, dtype=np.float32)
    wx = np.ascontiguousarray(L0_attn_Wx, dtype=np.float32)
    gu_all = np.concatenate(
        [wg.reshape(NIT, 128, D), wu.reshape(NIT, 128, D),
         np.ones((4, 128, D), dtype=np.float32)], axis=0)
    shared = {
        "wa": wa,
        "wx": wx,
        "ba": np.ascontiguousarray(L0_attn_ba, dtype=np.float32),
        "bx": np.ascontiguousarray(L0_attn_bx, dtype=np.float32),
        "lam": np.ascontiguousarray(L0_attn_Lam, dtype=np.float32),
        "clipx": np.ascontiguousarray(L0_ffn_clip_x, dtype=np.float32),
        "clipd": np.ascontiguousarray(L0_ffn_clip_down, dtype=np.float32),
    }
    in_maps = []
    for c in range(NCORE):
        b, half = c // 2, c % 2
        sel = np.zeros(8, dtype=np.float32)
        if half == 1:
            sel[c - 1] = 1.0
        m = dict(shared)
        m["x"] = np.ascontiguousarray(x[b, half * NT:(half + 1) * NT, :])
        m["sel8"] = sel
        m["gus"] = np.ascontiguousarray(gu_all[6 * c:6 * c + 6])
        m["wds"] = np.ascontiguousarray(wd[128 * c:128 * c + 128, :])
        in_maps.append(m)
    nc = _get_nc()
    res = run_bass_kernel_spmd(nc, in_maps, core_ids=list(range(NCORE)))
    out = np.empty((B, T, D), dtype=np.float32)
    for c in range(NCORE):
        b, half = c // 2, c % 2
        out[b, half * NT:(half + 1) * NT, :] = np.asarray(res.results[c]["y"])
    return out
